# version 17
# baseline (speedup 1.0000x reference)
import sys
sys.path.insert(0, "/opt/trn_rl_repo")
import numpy as np
import ml_dtypes
import concourse.bass as bass
import concourse.mybir as mybir
from concourse.bass_utils import run_bass_kernel_spmd
from concourse.bass_core_rust import IndirectOffsetOnAxis
from contextlib import ExitStack

F32 = mybir.dt.float32
BF16 = mybir.dt.bfloat16
I32 = mybir.dt.int32
U32 = mybir.dt.uint32
ALU = mybir.AluOpType
AF = mybir.ActivationFunctionType
AX = mybir.AxisListType

NCORES = 8
T = 4096
H = 1024
FT = H // 128
TC = 512
NCH = T // TC

ENGINES = ("pe", "act", "dve", "pool", "sp")


class Prog:
    R = 12

    def __init__(self, nc):
        self.nc = nc
        self.ops = {e: [] for e in ENGINES}
        self.res = {}
        self.out_toks = []
        self.dma_count = {}
        self.fence_toks = set()
        self.last_dma = {}
        self.cc_unfenced = []

    def fence(self):
        toks = set()
        for e in ENGINES:
            for i in range(len(self.ops[e]) - 1, -1, -1):
                if self.ops[e][i]["kind"] == "c":
                    toks.add((e, i))
                    break
        toks |= set(self.last_dma.values())
        toks |= set(self.cc_unfenced)
        self.cc_unfenced = []
        self.fence_toks = toks

    def _rec(self, handle, tag):
        d = self.res.setdefault(id(handle), {})
        return d.setdefault(tag, {"w": None, "r": []})

    def _tags(self, handle, tag):
        d = self.res.setdefault(id(handle), {})
        if tag is None:
            return list(d.keys())
        out = [tag]
        if None in d:
            out.append(None)
        return out

    def op(self, eng, emit, reads=(), writes=(), inc=1, is_out=False):
        def norm(x):
            return x if isinstance(x, tuple) else (x, None)
        waits = set()
        for r in reads:
            h, t = norm(r)
            for tt in self._tags(h, t):
                rec = self.res[id(h)].get(tt)
                if rec and rec["w"] is not None:
                    waits.add(rec["w"])
        for w in writes:
            h, t = norm(w)
            for tt in self._tags(h, t):
                rec = self.res[id(h)].get(tt)
                if rec:
                    if rec["w"] is not None:
                        waits.add(rec["w"])
                    waits.update(rec["r"])
        waits |= self.fence_toks
        waits = {tk for tk in waits if not (eng == "pe" and tk[0] == "pe")}
        idx = len(self.ops[eng])
        tok = (eng, idx)
        kind = "dma" if inc == 16 else ("cc" if inc == -1 else "c")
        d = None
        if kind == "dma":
            d = self.dma_count.get(eng, 0)
            self.dma_count[eng] = d + 1
            self.last_dma[(eng, d % self.R)] = tok
        elif kind == "cc":
            self.cc_unfenced.append(tok)
        self.ops[eng].append({"emit": emit, "waits": waits, "inc": inc,
                              "need": kind != "c", "kind": kind, "d": d})
        for r in reads:
            h, t = norm(r)
            self._rec(h, t)["r"].append(tok)
        for w in writes:
            h, t = norm(w)
            rec = self._rec(h, t)
            rec["w"] = tok
            rec["r"] = []
        if is_out:
            self.out_toks.append(tok)
        return tok

    def build(self, block, sems, dsems, ccsems):
        needed = set(self.out_toks)
        for e in ENGINES:
            for o in self.ops[e]:
                needed |= o["waits"]
        for (e, i) in needed:
            self.ops[e][i]["need"] = True
        cnt = {}
        for e in ENGINES:
            c = 0
            cl = []
            for o in self.ops[e]:
                if o["kind"] == "c" and o["need"]:
                    c += o["inc"]
                cl.append(c)
            cnt[e] = cl
        cc_of = {}
        ci = 0
        for e in ENGINES:
            for i, o in enumerate(self.ops[e]):
                if o["kind"] == "cc":
                    cc_of[(e, i)] = ccsems[ci]
                    ci += 1
        R = self.R

        def wait_target(tok):
            we, wi = tok
            o = self.ops[we][wi]
            if o["kind"] == "dma":
                d = o["d"]
                return (dsems[we][d % R], 16 * (d // R + 1))
            if o["kind"] == "cc":
                return (cc_of[tok], 1)
            return (sems[we], cnt[we][wi])

        def make_body(eng):
            def body(E):
                covered = {}
                for o in self.ops[eng]:
                    for tok in sorted(o["waits"]):
                        sem, val = wait_target(tok)
                        if covered.get(id(sem), -1) < val:
                            E.wait_ge(sem, val)
                            covered[id(sem)] = val
                    if o["kind"] == "dma" and o["d"] >= R:
                        sem = dsems[eng][o["d"] % R]
                        val = 16 * (o["d"] // R)
                        if covered.get(id(sem), -1) < val:
                            E.wait_ge(sem, val)
                            covered[id(sem)] = val
                    inst = o["emit"](E)
                    if o["kind"] == "dma":
                        inst.then_inc(dsems[eng][o["d"] % R], 16)
                    elif o["kind"] == "cc":
                        inst.then_inc(cc_of_by_inst[id(o)], 1)
                    elif o["need"]:
                        inst.then_inc(sems[eng], o["inc"])
                if eng == "pool":
                    for tok in sorted(set(self.out_toks)):
                        sem, val = wait_target(tok)
                        if covered.get(id(sem), -1) < val:
                            E.wait_ge(sem, val)
                            covered[id(sem)] = val
            return body

        cc_of_by_inst = {id(self.ops[e][i]): s for (e, i), s in cc_of.items()}
        block.tensor(make_body("pe"))
        block.scalar(make_body("act"))
        block.vector(make_body("dve"))
        block.gpsimd(make_body("pool"))
        block.sync(make_body("sp"))

    def n_cc(self):
        return sum(1 for e in ENGINES for o in self.ops[e] if o["kind"] == "cc")


class Ctx:
    def __init__(self, nc, es):
        self.nc = nc
        self.es = es
        self.pr = Prog(nc)
        self._n = 0
        self.psum = [es.enter_context(nc.psum_tensor(f"pb{i}", [128, 512], F32))
                     for i in range(6)]
        self.psum_bf2 = [es.enter_context(nc.psum_tensor(f"pbf{i}", [128, 1024], BF16))
                         for i in range(2)]
        self.psum_bf = self.psum_bf2[0]
        self._pool = {}
        self.consts = {}
        self._ccols = {}
        self._scopes = []
        self._bregs = {}

    def breg(self, E, val):
        key = (id(E), val)
        r = self._bregs.get(key)
        if r is None:
            r = E.to_reg(val)
            self._bregs[key] = r
        return r

    def constcol(self, val, dtype=F32):
        key = (float(val), dtype)
        t = self._ccols.get(key)
        if t is None:
            assert not self._scopes, f"constcol({val}) first used inside a phase scope; pre-create it"
            self._n += 1
            t = self.es.enter_context(self.nc.sbuf_tensor(f"t{self._n}", [128, 1], dtype))
            self.pr.op("dve", lambda E: E.memset(t[:], val), [], [t])
            self._ccols[key] = t
        return t

    def push(self):
        self._scopes.append((ExitStack(), self._pool))
        self._pool = {}

    def pop(self):
        es, pool = self._scopes.pop()
        es.close()
        self._pool = pool
        self.pr.fence()

    def sbuf(self, shape, dtype):
        self._n += 1
        es = self._scopes[-1][0] if self._scopes else self.es
        return es.enter_context(self.nc.sbuf_tensor(f"t{self._n}", list(shape), dtype))

    def tile(self, shape, dtype):
        key = (tuple(shape), dtype)
        lst = self._pool.get(key)
        if lst:
            return lst.pop()
        t = self.sbuf(shape, dtype)
        self.pr.op("dve", lambda E: E.memset(t[:], 0.0), [], [t])
        return t

    def free(self, *hs):
        for h in hs:
            self._pool.setdefault((tuple(h.shape), h.dtype), []).append(h)

    def finish(self):
        es2 = ExitStack()
        sems = {e: es2.enter_context(self.nc.semaphore(f"s_{e}")) for e in ENGINES}
        dsems = {e: [es2.enter_context(self.nc.semaphore(f"d_{e}{i}"))
                     for i in range(Prog.R)]
                 for e in ENGINES if self.pr.dma_count.get(e)}
        ccsems = [es2.enter_context(self.nc.semaphore(f"cc{i}"))
                  for i in range(self.pr.n_cc())]
        block = es2.enter_context(self.nc.Block())
        self.pr.build(block, sems, dsems, ccsems)
        es2.close()
        self.es.close()
        return self.nc


def new_ctx():
    nc = bass.Bass("TRN2", target_bir_lowering=False)
    es = ExitStack()
    return Ctx(nc, es)


def dma(ctx, eng, out, in_, reads, writes, is_out=False, **kw):
    return ctx.pr.op(eng, lambda E: E.dma_start(out=out, in_=in_, **kw),
                     reads=reads, writes=writes, inc=16, is_out=is_out)


def op_dve(ctx, emit, reads, writes):
    return ctx.pr.op("dve", emit, reads, writes)


def op_act(ctx, emit, reads, writes):
    return ctx.pr.op("act", emit, reads, writes)


def op_pe(ctx, emit, reads, writes):
    return ctx.pr.op("pe", emit, reads, writes)


def op_pool(ctx, emit, reads, writes, inc=1, is_out=False):
    return ctx.pr.op("pool", emit, reads, writes, inc=inc, is_out=is_out)


def decl_const_inputs(nc):
    return {
        "konst_f32": nc.dram_tensor("konst_f32", [128, 128], F32, kind="ExternalInput"),
        "konst_bf16": nc.dram_tensor("konst_bf16", [128, 2], BF16, kind="ExternalInput"),
        "ident_bf16": nc.dram_tensor("ident_bf16", [128, 128], BF16, kind="ExternalInput"),
        "ident_f32": nc.dram_tensor("ident_f32", [128, 128], F32, kind="ExternalInput"),
    }


def host_consts():
    kf = np.zeros((128, 128), np.float32)
    kf[:, 0] = 1.0
    kf[0, :] = 1.0
    kb = np.zeros((128, 2), ml_dtypes.bfloat16)
    kb[:, 0] = 1.0
    return {"konst_f32": kf, "konst_bf16": kb,
            "ident_bf16": np.eye(128, dtype=ml_dtypes.bfloat16),
            "ident_f32": np.eye(128, dtype=np.float32)}


def setup_consts(ctx, dr):
    c = {}
    c["ones_col_bf"] = ctx.sbuf((128, 1), BF16)
    c["ones_col_f"] = ctx.sbuf((128, 1), F32)
    c["ones_row_f"] = ctx.sbuf((1, 128), F32)
    dma(ctx, "sp", c["ones_col_bf"][:], dr["konst_bf16"].ap()[:, 0:1],
        reads=[dr["konst_bf16"]], writes=[c["ones_col_bf"]],
        allow_slow_non_contiguous=True)
    dma(ctx, "sp", c["ones_col_f"][:], dr["konst_f32"].ap()[:, 0:1],
        reads=[dr["konst_f32"]], writes=[c["ones_col_f"]],
        allow_slow_non_contiguous=True)
    dma(ctx, "sp", c["ones_row_f"][:], dr["konst_f32"].ap()[0:1, 0:128],
        reads=[dr["konst_f32"]], writes=[c["ones_row_f"]])
    c["ident_bf"] = ctx.sbuf((128, 128), BF16)
    c["ident_f"] = ctx.sbuf((128, 128), F32)
    dma(ctx, "sp", c["ident_bf"][:], dr["ident_bf16"].ap(),
        reads=[dr["ident_bf16"]], writes=[c["ident_bf"]])
    dma(ctx, "sp", c["ident_f"][:], dr["ident_f32"].ap(),
        reads=[dr["ident_f32"]], writes=[c["ident_f"]])
    ctx.consts = c
    for v in (-1.0, -0.5, 1e-5, 1.0, 0.0):
        ctx.constcol(v)
    return c


def cast_to_bf16_dram(ctx, nc, name, src_handle, shape):
    dst = nc.dram_tensor(name, list(shape), BF16)
    dma(ctx, "pool", dst.ap(), src_handle.ap(), reads=[src_handle], writes=[dst])
    return dst


def load_wT_panels(ctx, wb, n_in=H, n_out=H):
    panels = []
    for k in range(n_in // 128):
        t = ctx.sbuf((128, n_out), BF16)
        dma(ctx, "sp", t[0:128, 0:n_out], wb.ap()[0:n_out, 128 * k:128 * (k + 1)],
            reads=[wb], writes=[t], transpose=True)
        panels.append(t)
    return panels


def load_w_direct_panels(ctx, wb, n_in, n_out):
    panels = []
    for k in range(n_in // 128):
        t = ctx.sbuf((128, n_out), BF16)
        dma(ctx, "sp", t[0:128, 0:n_out], wb.ap()[128 * k:128 * (k + 1), 0:n_out],
            reads=[wb], writes=[t])
        panels.append(t)
    return panels


def load_percol(ctx, vec_handle, n=H):
    t = ctx.sbuf((128, n // 128), F32)
    src = bass.AP(vec_handle, 0, [[1, 128], [128, n // 128]])
    dma(ctx, "sp", t[:], src, reads=[vec_handle], writes=[t],
        allow_slow_non_contiguous=True)
    return t


def ln_chunk(ctx, x_tiles, Tc, bmu_bank=4, brs_bank=5):
    c = ctx.consts
    p_mu = ctx.psum[bmu_bank]
    p_sq = ctx.psum[brs_bank]
    sq = [ctx.tile((128, Tc), BF16) for _ in range(FT)]
    for k in range(FT):
        op_act(ctx, lambda E, k=k: E.activation(sq[k][:, 0:Tc], x_tiles[k][:, 0:Tc], AF.Square),
               [x_tiles[k]], [sq[k]])
    for k in range(FT):
        op_pe(ctx, lambda E, k=k: E.matmul(p_mu[0:1, 0:Tc], c["ones_col_f"][:],
                                           x_tiles[k][:, 0:Tc], start=(k == 0), stop=(k == FT - 1)),
              [x_tiles[k], c["ones_col_f"]], [p_mu])
    for k in range(FT):
        op_pe(ctx, lambda E, k=k: E.matmul(p_sq[0:1, 0:Tc], c["ones_col_bf"][:],
                                           sq[k][:, 0:Tc], start=(k == 0), stop=(k == FT - 1)),
              [sq[k], c["ones_col_bf"]], [p_sq])
    mu = ctx.tile((1, 512), F32)
    var = ctx.tile((1, 512), F32)
    op_dve(ctx, lambda E: E.tensor_scalar_mul(mu[0:1, 0:Tc], p_mu[0:1, 0:Tc], 1.0 / H),
           [p_mu], [mu])
    op_dve(ctx, lambda E: E.tensor_scalar_mul(var[0:1, 0:Tc], p_sq[0:1, 0:Tc], 1.0 / H),
           [p_sq], [var])
    negmusq = ctx.tile((1, 512), F32)
    op_dve(ctx, lambda E: E.scalar_tensor_tensor(negmusq[0:1, 0:Tc], mu[0:1, 0:Tc], -1.0,
                                                 mu[0:1, 0:Tc], ALU.mult, ALU.mult),
           [mu], [negmusq])
    op_dve(ctx, lambda E: E.tensor_add(var[0:1, 0:Tc], var[0:1, 0:Tc], negmusq[0:1, 0:Tc]),
           [var, negmusq], [var])
    rstd = ctx.tile((1, 512), F32)
    epsc = ctx.constcol(1e-5)
    mh = ctx.constcol(-0.5)
    op_act(ctx, lambda E: E.activation(rstd[0:1, 0:Tc], var[0:1, 0:Tc], AF.Ln,
                                       bias=epsc[0:1, 0:1]),
           [var, epsc], [rstd])
    op_act(ctx, lambda E: E.activation(rstd[0:1, 0:Tc], rstd[0:1, 0:Tc], AF.Exp,
                                       scale=mh[0:1, 0:1]),
           [rstd, mh], [rstd])
    op_pe(ctx, lambda E: E.matmul(p_mu[0:128, 0:Tc], c["ones_row_f"][0:1, 0:128],
                                  mu[0:1, 0:Tc], start=True, stop=True),
          [mu, c["ones_row_f"]], [p_mu])
    op_pe(ctx, lambda E: E.matmul(p_sq[0:128, 0:Tc], c["ones_row_f"][0:1, 0:128],
                                  rstd[0:1, 0:Tc], start=True, stop=True),
          [rstd, c["ones_row_f"]], [p_sq])
    xn = [ctx.tile((128, Tc), F32) for _ in range(FT)]
    for k in range(FT):
        op_dve(ctx, lambda E, k=k: E.tensor_sub(xn[k][:, 0:Tc], x_tiles[k][:, 0:Tc],
                                                p_mu[0:128, 0:Tc]),
               [x_tiles[k], p_mu], [xn[k]])
        op_dve(ctx, lambda E, k=k: E.tensor_mul(xn[k][:, 0:Tc], xn[k][:, 0:Tc],
                                                p_sq[0:128, 0:Tc]),
               [xn[k], p_sq], [xn[k]])
    ctx.free(mu, var, negmusq, rstd, *sq)
    return xn


def emit_cm(ctx, xin, xout, wTr, wTk, wTv, mur_t, muk_t, scale=0.5):
    carry = [ctx.sbuf((128, 1), F32) for _ in range(FT)]
    for k in range(FT):
        op_dve(ctx, lambda E, k=k: E.memset(carry[k][:], 0.0), [], [carry[k]])
    for ch in range(NCH):
        c0 = ch * TC
        xt = [ctx.tile((128, TC), F32) for _ in range(FT)]
        for k in range(FT):
            dma(ctx, "sp", xt[k][:], xin.ap()[128 * k:128 * (k + 1), c0:c0 + TC],
                reads=[(xin, ("c", ch))], writes=[xt[k]])
        xn = ln_chunk(ctx, xt, TC)
        xr = [ctx.tile((128, TC), BF16) for _ in range(FT)]
        xk = [ctx.tile((128, TC), BF16) for _ in range(FT)]
        xp = [ctx.tile((128, TC), F32) for _ in range(FT)]
        df = [ctx.tile((128, TC), F32) for _ in range(FT)]
        for k in range(FT):
            op_act(ctx, lambda E, a=xp[k], b=xn[k]: E.activation(a[:, 1:TC], b[:, 0:TC - 1], AF.Copy),
                   [xn[k]], [xp[k]])
            op_dve(ctx, lambda E, a=xp[k], b=carry[k]: E.tensor_copy(a[:, 0:1], b[:]),
                   [carry[k]], [xp[k]])
            op_dve(ctx, lambda E, a=carry[k], b=xn[k]: E.tensor_copy(a[:], b[:, TC - 1:TC]),
                   [xn[k]], [carry[k]])
            op_dve(ctx, lambda E, a=df[k], b=xn[k], c=xp[k]: E.tensor_sub(a[:], b[:], c[:]),
                   [xn[k], xp[k]], [df[k]])
            op_dve(ctx, lambda E, a=xr[k], b=df[k], m=mur_t, k=k, c=xp[k]: E.scalar_tensor_tensor(
                a[:], b[:], m[:, k:k + 1], c[:], ALU.mult, ALU.add),
                [df[k], mur_t, xp[k]], [xr[k]])
            op_dve(ctx, lambda E, a=xk[k], b=df[k], m=muk_t, k=k, c=xp[k]: E.scalar_tensor_tensor(
                a[:], b[:], m[:, k:k + 1], c[:], ALU.mult, ALU.add),
                [df[k], muk_t, xp[k]], [xk[k]])
        ctx.free(*xn, *xp, *df)
        r = [ctx.tile((128, TC), F32) for _ in range(FT)]
        for m in range(FT):
            pb = ctx.psum[m % 4]
            for k in range(FT):
                op_pe(ctx, lambda E, m=m, k=k, pb=pb, w=wTr[k], a=xr[k]: E.matmul(
                    pb[0:128, 0:TC], w[:, 128 * m:128 * (m + 1)], a[:, 0:TC],
                    start=(k == 0), stop=(k == FT - 1)),
                    [wTr[k], xr[k]], [pb])
            op_act(ctx, lambda E, pb=pb, a=r[m]: E.activation(a[:, 0:TC], pb[0:128, 0:TC], AF.Sigmoid),
                   [pb], [r[m]])
        ctx.free(*xr)
        kact = [ctx.tile((128, TC), BF16) for _ in range(FT)]
        ktmp = ctx.tile((128, TC), F32)
        for m in range(FT):
            pb = ctx.psum[m % 4]
            for k in range(FT):
                op_pe(ctx, lambda E, m=m, k=k, pb=pb, w=wTk[k], a=xk[k]: E.matmul(
                    pb[0:128, 0:TC], w[:, 128 * m:128 * (m + 1)], a[:, 0:TC],
                    start=(k == 0), stop=(k == FT - 1)),
                    [wTk[k], xk[k]], [pb])
            op_act(ctx, lambda E, pb=pb, a=ktmp: E.activation(a[:, 0:TC], pb[0:128, 0:TC], AF.Relu),
                   [pb], [ktmp])
            op_act(ctx, lambda E, a=kact[m], b=ktmp: E.activation(a[:, 0:TC], b[:, 0:TC], AF.Square),
                   [ktmp], [kact[m]])
        ctx.free(ktmp, *xk)
        for m in range(FT):
            pb = ctx.psum[m % 4]
            for k in range(FT):
                op_pe(ctx, lambda E, m=m, k=k, pb=pb, w=wTv[k], a=kact[k]: E.matmul(
                    pb[0:128, 0:TC], w[:, 128 * m:128 * (m + 1)], a[:, 0:TC],
                    start=(k == 0), stop=(k == FT - 1)),
                    [wTv[k], kact[k]], [pb])
            op_dve(ctx, lambda E, pb=pb, a=r[m]: E.tensor_mul(a[:, 0:TC], a[:, 0:TC], pb[0:128, 0:TC]),
                   [pb, r[m]], [r[m]])
            op_dve(ctx, lambda E, a=xt[m], b=r[m]: E.scalar_tensor_tensor(
                a[:, 0:TC], b[:, 0:TC], scale, a[:, 0:TC], ALU.mult, ALU.add),
                [r[m], xt[m]], [xt[m]])
            dma(ctx, "pool", xout.ap()[128 * m:128 * (m + 1), c0:c0 + TC], xt[m][:, 0:TC],
                reads=[xt[m]], writes=[(xout, ("c", ch))], is_out=True)
        ctx.free(*kact, *r, *xt)


def stage_cm_weights(ctx, nc, dr, pfx):
    wrb = cast_to_bf16_dram(ctx, nc, f"{pfx}_wrb", dr[f"{pfx}_W_r"], (H, H))
    wkb = cast_to_bf16_dram(ctx, nc, f"{pfx}_wkb", dr[f"{pfx}_W_k"], (H, H))
    wvb = cast_to_bf16_dram(ctx, nc, f"{pfx}_wvb", dr[f"{pfx}_W_v"], (H, H))
    return {
        "wTr": load_wT_panels(ctx, wrb),
        "wTk": load_wT_panels(ctx, wkb),
        "wTv": load_wT_panels(ctx, wvb),
        "mur": load_percol(ctx, dr[f"{pfx}_mu_r"]),
        "muk": load_percol(ctx, dr[f"{pfx}_mu_k"]),
    }


def emit_rwkv(ctx, xin, xout, w):
    ncar = [ctx.sbuf((128, 1), F32) for _ in range(FT)]
    dcar = [ctx.sbuf((128, 1), F32) for _ in range(FT)]
    carry = [ctx.sbuf((128, 1), F32) for _ in range(FT)]
    for k in range(FT):
        op_dve(ctx, lambda E, a=ncar[k]: E.memset(a[:], 0.0), [], [ncar[k]])
        op_dve(ctx, lambda E, a=dcar[k]: E.memset(a[:], 0.0), [], [dcar[k]])
        op_dve(ctx, lambda E, a=carry[k]: E.memset(a[:], 0.0), [], [carry[k]])
    for ch in range(NCH):
        c0 = ch * TC
        xt = [ctx.tile((128, TC), F32) for _ in range(FT)]
        for k in range(FT):
            dma(ctx, "sp", xt[k][:], xin.ap()[128 * k:128 * (k + 1), c0:c0 + TC],
                reads=[(xin, ("c", ch))], writes=[xt[k]])
        xn = ln_chunk(ctx, xt, TC)
        ki = [ctx.tile((128, TC), BF16) for _ in range(FT)]
        vi = [ctx.tile((128, TC), BF16) for _ in range(FT)]
        ri = [ctx.tile((128, TC), BF16) for _ in range(FT)]
        xp = [ctx.tile((128, TC), F32) for _ in range(FT)]
        df = [ctx.tile((128, TC), F32) for _ in range(FT)]
        for k in range(FT):
            op_act(ctx, lambda E, a=xp[k], b=xn[k]: E.activation(a[:, 1:TC], b[:, 0:TC - 1], AF.Copy),
                   [xn[k]], [xp[k]])
            op_dve(ctx, lambda E, a=xp[k], b=carry[k]: E.tensor_copy(a[:, 0:1], b[:]),
                   [carry[k]], [xp[k]])
            op_dve(ctx, lambda E, a=carry[k], b=xn[k]: E.tensor_copy(a[:], b[:, TC - 1:TC]),
                   [xn[k]], [carry[k]])
            op_dve(ctx, lambda E, a=df[k], b=xn[k], c=xp[k]: E.tensor_sub(a[:], b[:], c[:]),
                   [xn[k], xp[k]], [df[k]])
            for dst, mix in ((ki, w["tmk"]), (vi, w["tmv"]), (ri, w["tmr"])):
                op_dve(ctx, lambda E, a=dst[k], b=df[k], m=mix, k=k, c=xp[k]:
                       E.scalar_tensor_tensor(a[:], b[:], m[:, k:k + 1], c[:],
                                              ALU.mult, ALU.add),
                       [df[k], mix, xp[k]], [dst[k]])
        ctx.free(*xn, *xp, *df)
        kv = [ctx.tile((128, TC), F32) for _ in range(FT)]
        vv = [ctx.tile((128, TC), F32) for _ in range(FT)]
        rv = [ctx.tile((128, TC), F32) for _ in range(FT)]
        for (dst, src, pans, act) in ((kv, ki, w["wTk"], None), (vv, vi, w["wTv"], None),
                                      (rv, ri, w["wTr"], AF.Sigmoid)):
            for m in range(FT):
                pb = ctx.psum[m % 4]
                for k in range(FT):
                    op_pe(ctx, lambda E, m=m, k=k, pb=pb, wp=pans[k], a=src[k]: E.matmul(
                        pb[0:128, 0:TC], wp[:, 128 * m:128 * (m + 1)], a[:, 0:TC],
                        start=(k == 0), stop=(k == FT - 1)),
                        [pans[k], src[k]], [pb])
                if act is None:
                    op_act(ctx, lambda E, pb=pb, a=dst[m]: E.activation(a[:, 0:TC], pb[0:128, 0:TC], AF.Copy),
                           [pb], [dst[m]])
                else:
                    op_act(ctx, lambda E, pb=pb, a=dst[m]: E.activation(a[:, 0:TC], pb[0:128, 0:TC], act),
                           [pb], [dst[m]])
        ctx.free(*ki, *vi, *ri)
        rwkv = [ctx.tile((128, TC), BF16) for _ in range(FT)]
        ek2 = [ctx.tile((128, TC), F32) for _ in range(2)]
        ee2 = [ctx.tile((128, TC), F32)]
        ekv = ctx.tile((128, TC), F32)
        num = [ctx.tile((128, TC + 1), F32) for _ in range(FT)]
        den = [ctx.tile((128, TC + 1), F32) for _ in range(FT)]
        an = ctx.tile((128, TC), F32)
        ad = ctx.tile((128, TC), F32)
        for m in range(FT):
            ek, ee = ek2[m % 2], ee2[0]
            op_act(ctx, lambda E, a=ek, b=kv[m]: E.activation(a[:, 0:TC], b[:, 0:TC], AF.Exp),
                   [kv[m]], [ek])
            op_act(ctx, lambda E, a=ee, b=kv[m], tf=w["tf"], m=m:
                   E.activation(a[:, 0:TC], b[:, 0:TC], AF.Exp, bias=tf[:, m:m + 1]),
                   [kv[m], w["tf"]], [ee])
            op_dve(ctx, lambda E, a=ekv, b=ek, c=vv[m]: E.tensor_mul(a[:, 0:TC], b[:, 0:TC], c[:, 0:TC]),
                   [ek, vv[m]], [ekv])
            op_dve(ctx, lambda E, a=num[m], b=ncar[m]: E.tensor_copy(a[:, 0:1], b[:]),
                   [ncar[m]], [num[m]])
            op_dve(ctx, lambda E, a=den[m], b=dcar[m]: E.tensor_copy(a[:, 0:1], b[:]),
                   [dcar[m]], [den[m]])
            op_dve(ctx, lambda E, a=num[m], td=w["td"], m=m, b=ekv: E.tensor_tensor_scan(
                a[:, 1:TC + 1], td[:, m:m + 1].to_broadcast([128, TC]), b[:, 0:TC],
                a[:, 0:1], ALU.mult, ALU.add),
                [ekv, w["td"], num[m]], [num[m]])
            op_dve(ctx, lambda E, a=den[m], td=w["td"], m=m, b=ek: E.tensor_tensor_scan(
                a[:, 1:TC + 1], td[:, m:m + 1].to_broadcast([128, TC]), b[:, 0:TC],
                a[:, 0:1], ALU.mult, ALU.add),
                [ek, w["td"], den[m]], [den[m]])
            op_dve(ctx, lambda E, a=ncar[m], b=num[m]: E.tensor_copy(a[:], b[:, TC:TC + 1]),
                   [num[m]], [ncar[m]])
            op_dve(ctx, lambda E, a=dcar[m], b=den[m]: E.tensor_copy(a[:], b[:, TC:TC + 1]),
                   [den[m]], [dcar[m]])
            op_dve(ctx, lambda E, a=an, b=ee, c=vv[m]: E.tensor_mul(a[:, 0:TC], b[:, 0:TC], c[:, 0:TC]),
                   [ee, vv[m]], [an])
            op_dve(ctx, lambda E, a=an, b=num[m]: E.tensor_add(a[:, 0:TC], a[:, 0:TC], b[:, 0:TC]),
                   [an, num[m]], [an])
            op_dve(ctx, lambda E, a=ad, b=ee, c=den[m]: E.tensor_add(a[:, 0:TC], b[:, 0:TC], c[:, 0:TC]),
                   [ee, den[m]], [ad])
            op_dve(ctx, lambda E, a=ad: E.reciprocal(a[:, 0:TC], a[:, 0:TC]),
                   [ad], [ad])
            op_dve(ctx, lambda E, a=an, b=ad: E.tensor_mul(a[:, 0:TC], a[:, 0:TC], b[:, 0:TC]),
                   [an, ad], [an])
            op_dve(ctx, lambda E, a=rwkv[m], b=an, c=rv[m]: E.tensor_mul(a[:, 0:TC], b[:, 0:TC], c[:, 0:TC]),
                   [an, rv[m]], [rwkv[m]])
        ctx.free(*ek2, *ee2, ekv, an, ad, *kv, *vv, *rv, *num, *den)
        for m in range(FT):
            pb = ctx.psum[m % 4]
            for k in range(FT):
                op_pe(ctx, lambda E, m=m, k=k, pb=pb, wp=w["wTo"][k], a=rwkv[k]: E.matmul(
                    pb[0:128, 0:TC], wp[:, 128 * m:128 * (m + 1)], a[:, 0:TC],
                    start=(k == 0), stop=(k == FT - 1)),
                    [w["wTo"][k], rwkv[k]], [pb])
            op_dve(ctx, lambda E, pb=pb, a=xt[m], bo=w["bo"], m=m: E.scalar_tensor_tensor(
                a[:, 0:TC], pb[0:128, 0:TC], bo[:, m:m + 1], a[:, 0:TC], ALU.add, ALU.add),
                [pb, w["bo"], xt[m]], [xt[m]])
            dma(ctx, "pool", xout.ap()[128 * m:128 * (m + 1), c0:c0 + TC], xt[m][:, 0:TC],
                reads=[xt[m]], writes=[(xout, ("c", ch))], is_out=True)
        ctx.free(*rwkv, *xt)


def stage_rwkv_weights(ctx, nc, dr):
    wkb = cast_to_bf16_dram(ctx, nc, "rk_wkb", dr["Wk"], (H, H))
    wvb = cast_to_bf16_dram(ctx, nc, "rk_wvb", dr["Wv"], (H, H))
    wrb = cast_to_bf16_dram(ctx, nc, "rk_wrb", dr["Wr"], (H, H))
    wob = cast_to_bf16_dram(ctx, nc, "rk_wob", dr["Wo"], (H, H))
    w = {
        "wTk": load_wT_panels(ctx, wkb),
        "wTv": load_wT_panels(ctx, wvb),
        "wTr": load_wT_panels(ctx, wrb),
        "wTo": load_wT_panels(ctx, wob),
        "tmk": load_percol(ctx, dr["time_mix_key"]),
        "tmv": load_percol(ctx, dr["time_mix_value"]),
        "tmr": load_percol(ctx, dr["time_mix_receptance"]),
        "tf": load_percol(ctx, dr["time_first"]),
        "bo": load_percol(ctx, dr["bo"]),
    }
    tdraw = load_percol(ctx, dr["time_decay"])
    td = ctx.sbuf((128, FT), F32)
    m1 = ctx.constcol(-1.0)
    op_act(ctx, lambda E: E.activation(td[:], tdraw[:], AF.Exp), [tdraw], [td])
    op_act(ctx, lambda E: E.activation(td[:], td[:], AF.Exp, scale=m1[:, 0:1]),
           [td, m1], [td])
    w["td"] = td
    return w


DI = 2048
DT2 = DI // 128
NN = 16
DTR = 64
TCB = 256
NCHB = None


def load_percol2(ctx, vec_handle, d, k):
    nt = d // 128
    t = ctx.sbuf((128, nt * k), F32)
    src = bass.AP(vec_handle, 0, [[k, 128], [128 * k, nt], [1, k]])
    dst = bass.AP(t, 0, [[nt * k, 128], [k, nt], [1, k]])
    dma(ctx, "sp", dst, src, reads=[vec_handle], writes=[t])
    return t


def emit_mamba(ctx, nc, xin, xout, dr, pfx="mb"):
    nchb = T // TCB
    xcpre = nc.dram_tensor(f"{pfx}_xcpre", [DI, T], F32)
    z_scr = nc.dram_tensor(f"{pfx}_z", [DI, T], F32)
    yf_scr = nc.dram_tensor(f"{pfx}_yf", [DI, T], BF16)
    ys_scr = nc.dram_tensor(f"{pfx}_ys", [DI, T], BF16)
    bc_scr = nc.dram_tensor(f"{pfx}_bc", [64, TCB], BF16)
    mon_scr = nc.dram_tensor(f"{pfx}_mon", [H, T], BF16)
    one = ctx.constcol(1.0)
    m1 = ctx.constcol(-1.0)

    cw = load_percol2(ctx, dr["conv_w"], DI, 4)
    cb = load_percol(ctx, dr["conv_b"], DI)
    dpb = load_percol(ctx, dr["dt_proj_b"], DI)
    Dc = load_percol(ctx, dr["D"], DI)
    lng = load_percol(ctx, dr["ln_g"], H)
    lnb = load_percol(ctx, dr["ln_b"], H)
    fb1 = load_percol(ctx, dr["ff_b1"], 4 * H)
    fb2 = load_percol(ctx, dr["ff_b2"], H)
    a_neg = load_percol2(ctx, dr["A_log"], DI, NN)
    op_act(ctx, lambda E: E.activation(a_neg[:], a_neg[:], AF.Exp), [a_neg], [a_neg])
    op_dve(ctx, lambda E: E.tensor_scalar_mul(a_neg[:], a_neg[:], -1.0), [a_neg], [a_neg])
    carr = [ctx.sbuf((128, 2 * NN), F32) for _ in range(DT2)]
    for dt in range(DT2):
        op_dve(ctx, lambda E, a=carr[dt]: E.memset(a[:], 0.0), [], [carr[dt]])

    ctx.push()
    ipb = cast_to_bf16_dram(ctx, nc, f"{pfx}_ipb", dr["in_proj_w"], (H, 2 * DI))
    ipp = load_w_direct_panels(ctx, ipb, H, 2 * DI)
    for ch in range(NCH):
        c0 = ch * TC
        xt = [ctx.tile((128, TC), F32) for _ in range(FT)]
        for k in range(FT):
            dma(ctx, "sp", xt[k][:], xin.ap()[128 * k:128 * (k + 1), c0:c0 + TC],
                reads=[(xin, ("c", ch))], writes=[xt[k]])
        xn = ln_chunk(ctx, xt, TC)
        xb = [ctx.tile((128, TC), BF16) for _ in range(FT)]
        for k in range(FT):
            op_act(ctx, lambda E, a=xb[k], b=xn[k]: E.activation(a[:, 0:TC], b[:, 0:TC], AF.Copy),
                   [xn[k]], [xb[k]])
        ev = [ctx.tile((128, TC), F32) for _ in range(4)]
        for m in range(32):
            pb = ctx.psum[m % 4]
            for k in range(FT):
                op_pe(ctx, lambda E, m=m, k=k, pb=pb, w=ipp[k], a=xb[k]: E.matmul(
                    pb[0:128, 0:TC], w[:, 128 * m:128 * (m + 1)], a[:, 0:TC],
                    start=(k == 0), stop=(k == FT - 1)),
                    [ipp[k], xb[k]], [pb])
            e = ev[m % 4]
            op_act(ctx, lambda E, pb=pb, a=e: E.activation(a[:, 0:TC], pb[0:128, 0:TC], AF.Copy),
                   [pb], [e])
            tgt, row = (xcpre, m) if m < DT2 else (z_scr, m - DT2)
            dma(ctx, "pool", tgt.ap()[128 * row:128 * (row + 1), c0:c0 + TC], e[:, 0:TC],
                reads=[e], writes=[(tgt, ("c", ch, row))])
        ctx.free(*xt, *xn, *xb, *ev)
    ctx.pop()

    ctx.push()
    xpb = cast_to_bf16_dram(ctx, nc, f"{pfx}_xpb", dr["x_proj_w"], (DI, 96))
    xpp = load_w_direct_panels(ctx, xpb, DI, 96)
    dtb_w = cast_to_bf16_dram(ctx, nc, f"{pfx}_dtb", dr["dt_proj_w"], (DTR, DI))
    dtp = ctx.sbuf((DTR, DI), BF16)
    dma(ctx, "sp", dtp[0:DTR, 0:DI], dtb_w.ap()[0:DTR, 0:DI], reads=[dtb_w], writes=[dtp])

    HW = 16
    xcp4 = [ctx.sbuf((128, TCB + HW), F32) for _ in range(4)]
    raw4 = [ctx.sbuf((128, TCB + HW), F32) for _ in range(4)]
    ta8 = [[ctx.sbuf((128, TCB), F32) for _ in range(4)] for _ in range(2)]
    for b in xcp4 + raw4 + ta8[0] + ta8[1]:
        op_dve(ctx, lambda E, a=b: E.memset(a[:, 0:HW], 0.0), [], [a for a in ()] or [b])
    for dirn in range(2):
        for ch in range(nchb):
            ol = (T - (ch + 1) * TCB) if dirn else (ch * TCB)
            xc = [ctx.tile((128, TCB), F32) for _ in range(DT2)]
            xcb = [ctx.tile((128, TCB), BF16) for _ in range(DT2)]
            for dt in range(DT2):
                xcp = xcp4[dt % 4]
                if ch == 0:
                    op_dve(ctx, lambda E, a=xcp: E.memset(a[:, 0:HW], 0.0), [], [(xcp, "h")])
                    if dirn == 0:
                        dma(ctx, "sp", xcp[:, HW:HW + TCB],
                            xcpre.ap()[128 * dt:128 * (dt + 1), 0:TCB],
                            reads=[xcpre], writes=[(xcp, "b")])
                    else:
                        raw = raw4[dt % 4]
                        dma(ctx, "sp", raw[:, 0:TCB],
                            xcpre.ap()[128 * dt:128 * (dt + 1), T - TCB:T],
                            reads=[xcpre], writes=[raw])
                        op_dve(ctx, lambda E, a=xcp, b=raw: E.tensor_copy(
                            a[:, HW:HW + TCB],
                            bass.AP(b, TCB - 1, [[TCB + HW, 128], [-1, TCB]])),
                            [raw], [(xcp, "b")])
                else:
                    if dirn == 0:
                        dma(ctx, "sp", xcp[:, HW - 3:HW + TCB],
                            xcpre.ap()[128 * dt:128 * (dt + 1), ol - 3:ol + TCB],
                            reads=[xcpre], writes=[(xcp, "b")])
                    else:
                        raw = raw4[dt % 4]
                        dma(ctx, "sp", raw[:, 0:TCB + 3],
                            xcpre.ap()[128 * dt:128 * (dt + 1), ol:ol + TCB + 3],
                            reads=[xcpre], writes=[raw])
                        op_dve(ctx, lambda E, a=xcp, b=raw: E.tensor_copy(
                            a[:, HW - 3:HW + TCB],
                            bass.AP(b, TCB + 2, [[TCB + HW, 128], [-1, TCB + 3]])),
                            [raw], [(xcp, "b")])
                ta = ta8[dt % 2]
                for i in range(4):
                    op_act(ctx, lambda E, a=ta[i], b=xcp, c=cw, dt=dt, i=i: E.activation(
                        a[:, 0:TCB], b[:, HW - 3 + i:HW - 3 + i + TCB], AF.Copy,
                        scale=c[:, 4 * dt + i:4 * dt + i + 1]),
                        [(xcp, "h"), (xcp, "b"), cw], [ta[i]])
                op_pool(ctx, lambda E, a=ta[0], b=ta[1]: E.tensor_add(a[:, 0:TCB], a[:, 0:TCB], b[:, 0:TCB]),
                        [ta[0], ta[1]], [ta[0]])
                op_pool(ctx, lambda E, a=ta[2], b=ta[3]: E.tensor_add(a[:, 0:TCB], a[:, 0:TCB], b[:, 0:TCB]),
                        [ta[2], ta[3]], [ta[2]])
                op_pool(ctx, lambda E, a=ta[0], b=ta[2]: E.tensor_add(a[:, 0:TCB], a[:, 0:TCB], b[:, 0:TCB]),
                        [ta[0], ta[2]], [ta[0]])
                if dt >= 1:
                    pcv = dt - 1
                    op_act(ctx, lambda E, a=xc[pcv], b=ta8[pcv % 2][0], c=cb, pcv=pcv: E.activation(
                        a[:, 0:TCB], b[:, 0:TCB], AF.Silu, bias=c[:, pcv:pcv + 1]),
                        [ta8[pcv % 2][0], cb], [xc[pcv]])
                    op_act(ctx, lambda E, a=xcb[pcv], b=ta8[pcv % 2][0], c=cb, pcv=pcv: E.activation(
                        a[:, 0:TCB], b[:, 0:TCB], AF.Silu, bias=c[:, pcv:pcv + 1]),
                        [ta8[pcv % 2][0], cb], [xcb[pcv]])
            pcv = DT2 - 1
            op_act(ctx, lambda E, a=xc[pcv], b=ta8[pcv % 2][0], c=cb, pcv=pcv: E.activation(
                a[:, 0:TCB], b[:, 0:TCB], AF.Silu, bias=c[:, pcv:pcv + 1]),
                [ta8[pcv % 2][0], cb], [xc[pcv]])
            op_act(ctx, lambda E, a=xcb[pcv], b=ta8[pcv % 2][0], c=cb, pcv=pcv: E.activation(
                a[:, 0:TCB], b[:, 0:TCB], AF.Silu, bias=c[:, pcv:pcv + 1]),
                [ta8[pcv % 2][0], cb], [xcb[pcv]])
            par = (dirn * nchb + ch) % 2
            pb = ctx.psum[4]
            for dt in range(DT2):
                op_pe(ctx, lambda E, dt=dt, pb=pb, w=xpp[dt], a=xcb[dt]: E.matmul(
                    pb[0:96, 0:TCB], w[0:128, 0:96], a[:, 0:TCB],
                    start=(dt == 0), stop=(dt == DT2 - 1)),
                    [xpp[dt], xcb[dt]], [pb])
            dblb = ctx.tile((128, TCB), BF16)
            bcb = ctx.tile((128, TCB), BF16)
            op_dve(ctx, lambda E, a=dblb, pb=pb: E.tensor_copy(a[0:DTR, 0:TCB], pb[0:DTR, 0:TCB]),
                   [pb], [dblb])
            op_dve(ctx, lambda E, a=bcb, pb=pb: E.tensor_copy(a[64:96, 0:TCB], pb[64:96, 0:TCB]),
                   [pb], [bcb])
            dma(ctx, "pool", bc_scr.ap()[par * 32:par * 32 + 32, 0:TCB], bcb[64:96, 0:TCB],
                reads=[bcb], writes=[(bc_scr, par)])
            brep = ctx.tile((128, NN * TCB), BF16)
            crep = ctx.tile((128, NN * TCB), BF16)
            for rep, r0 in ((brep, 0), (crep, NN)):
                src = bass.AP(bc_scr, (par * 32 + r0) * TCB, [[0, 1], [TCB, NN], [1, TCB]])
                dst = bass.AP(rep, 0, [[NN * TCB, 128], [TCB, NN], [1, TCB]])
                dma(ctx, "sp", dst, src.partition_broadcast(128),
                    reads=[(bc_scr, par)], writes=[rep])
            delta = [ctx.tile((128, TCB), F32) for _ in range(DT2)]
            ut2 = [ctx.tile((128, TCB), F32) for _ in range(2)]
            a12 = [ctx.tile((128, TCB), F32) for _ in range(2)]
            for dt in range(DT2):
                pb2 = ctx.psum[3 - dt % 2]
                u_t, a1 = ut2[dt % 2], a12[dt % 2]
                op_pe(ctx, lambda E, dt=dt, pb2=pb2, a=dblb, w=dtp: E.matmul(
                    pb2[0:128, 0:TCB], w[0:DTR, 128 * dt:128 * (dt + 1)],
                    a[0:DTR, 0:TCB], start=True, stop=True),
                    [dtp, dblb], [pb2])
                op_dve(ctx, lambda E, a=u_t, pb=pb2, c=dpb, dt=dt: E.tensor_scalar(
                    a[:, 0:TCB], pb[0:128, 0:TCB], c[:, dt:dt + 1], None, ALU.add),
                    [pb2, dpb], [u_t])
                op_act(ctx, lambda E, a=a1, b=u_t: E.activation(a[:, 0:TCB], b[:, 0:TCB], AF.Abs),
                       [u_t], [a1])
                op_act(ctx, lambda E, a=a1, s=m1: E.activation(a[:, 0:TCB], a[:, 0:TCB],
                                                               AF.Exp, scale=s[:, 0:1]),
                       [a1, m1], [a1])
                op_act(ctx, lambda E, a=a1, s=one: E.activation(a[:, 0:TCB], a[:, 0:TCB],
                                                                AF.Ln, bias=s[:, 0:1]),
                       [a1, one], [a1])
                op_dve(ctx, lambda E, a=delta[dt], b=u_t, c=a1: E.scalar_tensor_tensor(
                    a[:, 0:TCB], b[:, 0:TCB], 0.0, c[:, 0:TCB], ALU.max, ALU.add),
                    [u_t, a1], [delta[dt]])
            ctx.free(*ut2, *a12, dblb, bcb)
            aa2 = [ctx.tile((128, NN * TCB), BF16) for _ in range(2)]
            db2 = [ctx.tile((128, NN * TCB), BF16) for _ in range(2)]
            hh2 = [ctx.tile((128, NN * TCB), BF16) for _ in range(2)]
            t2a2 = [ctx.tile((128, 8 * TCB), F32) for _ in range(2)]
            du2 = [ctx.tile((128, TCB), BF16) for _ in range(2)]
            yy = [ctx.tile((128, TCB), F32) for _ in range(DT2)]
            for dt in range(DT2):
                aa, db, hh = aa2[dt % 2], db2[dt % 2], hh2[dt % 2]
                t2a, du = t2a2[dt % 2], du2[dt % 2]
                op_dve(ctx, lambda E, a=du, b=delta[dt], c=xc[dt]: E.tensor_mul(
                    a[:, 0:TCB], b[:, 0:TCB], c[:, 0:TCB]),
                    [delta[dt], xc[dt]], [du])
                for n in range(NN):
                    op_act(ctx, lambda E, a=aa, n=n, b=delta[dt], s=a_neg, dt=dt: E.activation(
                        a[:, n * TCB:(n + 1) * TCB], b[:, 0:TCB], AF.Exp,
                        scale=s[:, NN * dt + n:NN * dt + n + 1]),
                        [delta[dt], a_neg], [(aa, n)])
                duv = bass.AP(du, 0, [[TCB, 128], [0, NN], [1, TCB]])
                op_dve(ctx, lambda E, a=db, b=duv, c=brep: E.tensor_tensor(
                    a[:, 0:NN * TCB], b, c[:, 0:NN * TCB], ALU.mult),
                    [du, brep], [db])
                aaS = bass.AP(aa, 0, [[NN * TCB, 128], [TCB, NN]])
                dbS = bass.AP(db, 0, [[NN * TCB, 128], [TCB, NN]])
                t16 = ctx.tile((128, NN), F32)
                op_dve(ctx, lambda E, a=t16, b=aaS, cr=carr[dt], c0=dirn * NN: E.tensor_tensor(
                    a[:, 0:NN], b, cr[:, c0:c0 + NN], ALU.mult),
                    [(aa, n) for n in range(NN)] + [carr[dt]], [t16])
                op_dve(ctx, lambda E, a=dbS, b=t16: E.tensor_tensor(
                    a, b[:, 0:NN], a, ALU.add),
                    [t16, db], [db])
                op_dve(ctx, lambda E, a=aaS: E.tensor_scalar_mul(a, a, 0.0),
                       [(aa, n) for n in range(NN)], [(aa, "z")])
                zc = ctx.constcol(0.0)
                op_dve(ctx, lambda E, a=hh, b=aa, c=db, z=zc: E.tensor_tensor_scan(
                    a[:, 0:NN * TCB], b[:, 0:NN * TCB], c[:, 0:NN * TCB],
                    z[:, 0:1], ALU.mult, ALU.add),
                    [(aa, "z"), db, zc] + [(aa, n) for n in range(NN)],
                    [hh] + [(hh, n) for n in range(NN)])
                ctx.free(t16)
                op_dve(ctx, lambda E, a=carr[dt], b=hh, col=dirn * NN: E.tensor_copy(
                    a[:, col:col + NN],
                    bass.AP(b, TCB - 1, [[NN * TCB, 128], [TCB, NN]])),
                    [hh], [carr[dt]])
                op_dve(ctx, lambda E, a=hh, b=crep: E.tensor_mul(a[:, 0:NN * TCB], a[:, 0:NN * TCB],
                                                                 b[:, 0:NN * TCB]),
                       [hh, crep], [hh])
                op_pool(ctx, lambda E, a=t2a, b=hh: E.tensor_add(
                    a[:, 0:8 * TCB], b[:, 0:8 * TCB], b[:, 8 * TCB:16 * TCB]),
                    [hh], [t2a])
                op_pool(ctx, lambda E, a=t2a: E.tensor_add(
                    a[:, 0:4 * TCB], a[:, 0:4 * TCB], a[:, 4 * TCB:8 * TCB]),
                    [t2a], [t2a])
                if dt >= 1:
                    pdt = dt - 1
                    t2p = t2a2[pdt % 2]
                    op_dve(ctx, lambda E, a=t2p: E.tensor_add(
                        a[:, 0:2 * TCB], a[:, 0:2 * TCB], a[:, 2 * TCB:4 * TCB]),
                        [t2p], [t2p])
                    op_dve(ctx, lambda E, a=t2p: E.tensor_add(
                        a[:, 0:TCB], a[:, 0:TCB], a[:, TCB:2 * TCB]),
                        [t2p], [t2p])
                    op_dve(ctx, lambda E, a=yy[pdt], b=t2p, c=xc[pdt], d=Dc, pdt=pdt:
                           E.scalar_tensor_tensor(a[:, 0:TCB], c[:, 0:TCB], d[:, pdt:pdt + 1],
                                                  b[:, 0:TCB], ALU.mult, ALU.add),
                           [t2p, xc[pdt], Dc], [yy[pdt]])
            pdt = DT2 - 1
            t2p = t2a2[pdt % 2]
            op_dve(ctx, lambda E, a=t2p: E.tensor_add(
                a[:, 0:2 * TCB], a[:, 0:2 * TCB], a[:, 2 * TCB:4 * TCB]),
                [t2p], [t2p])
            op_dve(ctx, lambda E, a=t2p: E.tensor_add(
                a[:, 0:TCB], a[:, 0:TCB], a[:, TCB:2 * TCB]),
                [t2p], [t2p])
            op_dve(ctx, lambda E, a=yy[pdt], b=t2p, c=xc[pdt], d=Dc, pdt=pdt:
                   E.scalar_tensor_tensor(a[:, 0:TCB], c[:, 0:TCB], d[:, pdt:pdt + 1],
                                          b[:, 0:TCB], ALU.mult, ALU.add),
                   [t2p, xc[pdt], Dc], [yy[pdt]])
            ctx.free(*aa2, *db2, *hh2, *t2a2, *du2, *delta, *xc, *xcb, brep, crep)
            if dirn == 0:
                for dt in range(DT2):
                    dma(ctx, "pool", yf_scr.ap()[128 * dt:128 * (dt + 1), ol:ol + TCB],
                        yy[dt][:, 0:TCB], reads=[yy[dt]], writes=[(yf_scr, ("c", ch, dt))])
            else:
                zt2 = [ctx.tile((128, TCB), F32) for _ in range(2)]
                yfb2 = [ctx.tile((128, TCB), BF16) for _ in range(2)]
                yf2 = [ctx.tile((128, TCB), F32) for _ in range(2)]
                zs2 = [ctx.tile((128, TCB), F32) for _ in range(2)]
                for dt in range(DT2):
                    zt, yfb, yf, zs = zt2[dt % 2], yfb2[dt % 2], yf2[dt % 2], zs2[dt % 2]
                    dma(ctx, "sp", yfb[:, 0:TCB], yf_scr.ap()[128 * dt:128 * (dt + 1), ol:ol + TCB],
                        reads=[yf_scr], writes=[yfb])
                    op_dve(ctx, lambda E, a=yf, c=yfb, b=yy[dt]: E.tensor_tensor(
                        a[:, 0:TCB], c[:, 0:TCB],
                        bass.AP(b, TCB - 1, [[TCB, 128], [-1, TCB]]), ALU.add),
                        [yy[dt], yfb], [yf])
                    dma(ctx, "sp", zt[:, 0:TCB], z_scr.ap()[128 * dt:128 * (dt + 1), ol:ol + TCB],
                        reads=[z_scr], writes=[zt])
                    op_act(ctx, lambda E, a=zs, b=zt: E.activation(a[:, 0:TCB], b[:, 0:TCB], AF.Silu),
                           [zt], [zs])
                    op_dve(ctx, lambda E, a=yf, b=zs: E.tensor_mul(a[:, 0:TCB], a[:, 0:TCB], b[:, 0:TCB]),
                           [zs, yf], [yf])
                    dma(ctx, "pool", ys_scr.ap()[128 * dt:128 * (dt + 1), ol:ol + TCB],
                        yf[:, 0:TCB], reads=[yf], writes=[(ys_scr, ("c", ch, dt))])
                ctx.free(*zt2, *yfb2, *yf2, *zs2)
            ctx.free(*yy)
    ctx.pop()

    ctx.push()
    opb = cast_to_bf16_dram(ctx, nc, f"{pfx}_opb", dr["out_proj_w"], (DI, H))
    opp = load_w_direct_panels(ctx, opb, DI, H)
    for ch in range(NCH):
        c0 = ch * TC
        ysb = [ctx.tile((128, TC), BF16) for _ in range(DT2)]
        for k in range(DT2):
            dma(ctx, "pool", ysb[k][:, 0:TC], ys_scr.ap()[128 * k:128 * (k + 1), c0:c0 + TC],
                reads=[ys_scr], writes=[ysb[k]])
        mo = [ctx.tile((128, TC), F32) for _ in range(FT)]
        for m in range(FT):
            pb = ctx.psum[m % 4]
            for k in range(DT2):
                op_pe(ctx, lambda E, m=m, k=k, pb=pb, w=opp[k], a=ysb[k]: E.matmul(
                    pb[0:128, 0:TC], w[:, 128 * m:128 * (m + 1)], a[:, 0:TC],
                    start=(k == 0), stop=(k == DT2 - 1)),
                    [opp[k], ysb[k]], [pb])
            op_act(ctx, lambda E, pb=pb, a=mo[m]: E.activation(a[:, 0:TC], pb[0:128, 0:TC], AF.Copy),
                   [pb], [mo[m]])
        xn = ln_chunk(ctx, mo, TC)
        for m in range(FT):
            ob = ctx.tile((128, TC), BF16)
            op_dve(ctx, lambda E, a=ob, b=xn[m], g=lng, bb=lnb, m=m: E.tensor_scalar(
                a[:, 0:TC], b[:, 0:TC], g[:, m:m + 1], bb[:, m:m + 1], ALU.mult, ALU.add),
                [xn[m], lng, lnb], [ob])
            dma(ctx, "pool", mon_scr.ap()[128 * m:128 * (m + 1), c0:c0 + TC], ob[:, 0:TC],
                reads=[ob], writes=[(mon_scr, ("c", ch, m))])
            ctx.free(ob)
        ctx.free(*ysb, *mo, *xn)
    ctx.pop()

    ctx.push()
    f1b = cast_to_bf16_dram(ctx, nc, f"{pfx}_f1b", dr["ff_w1"], (H, 4 * H))
    f2b = cast_to_bf16_dram(ctx, nc, f"{pfx}_f2b", dr["ff_w2"], (4 * H, H))
    f1p = load_w_direct_panels(ctx, f1b, H, 4 * H)
    f2p = load_w_direct_panels(ctx, f2b, 4 * H, H)
    TM = 256
    for ch in range(T // TM):
        c0 = ch * TM
        h0 = [ctx.tile((128, TM), BF16) for _ in range(FT)]
        for k in range(FT):
            dma(ctx, "sp", h0[k][:, 0:TM], mon_scr.ap()[128 * k:128 * (k + 1), c0:c0 + TM],
                reads=[mon_scr], writes=[h0[k]])
        h1 = [ctx.tile((128, TM), BF16) for _ in range(32)]
        for m in range(32):
            pb = ctx.psum[m % 4]
            for k in range(FT):
                op_pe(ctx, lambda E, m=m, k=k, pb=pb, w=f1p[k], a=h0[k]: E.matmul(
                    pb[0:128, 0:TM], w[:, 128 * m:128 * (m + 1)], a[:, 0:TM],
                    start=(k == 0), stop=(k == FT - 1)),
                    [f1p[k], h0[k]], [pb])
            op_act(ctx, lambda E, pb=pb, a=h1[m], b=fb1, m=m: E.activation(
                a[:, 0:TM], pb[0:128, 0:TM], AF.Gelu, bias=b[:, m:m + 1]),
                [pb, fb1], [h1[m]])
        xt = [ctx.tile((128, TM), F32) for _ in range(FT)]
        for m in range(FT):
            dma(ctx, "sp", xt[m][:, 0:TM], xin.ap()[128 * m:128 * (m + 1), c0:c0 + TM],
                reads=[(xin, ("d", ch))], writes=[xt[m]])
        for m in range(FT):
            pb = ctx.psum[4 + m % 2]
            for k in range(32):
                op_pe(ctx, lambda E, m=m, k=k, pb=pb, w=f2p[k], a=h1[k]: E.matmul(
                    pb[0:128, 0:TM], w[:, 128 * m:128 * (m + 1)], a[:, 0:TM],
                    start=(k == 0), stop=(k == 31)),
                    [f2p[k], h1[k]], [pb])
            op_dve(ctx, lambda E, pb=pb, a=xt[m], b=fb2, m=m: E.scalar_tensor_tensor(
                a[:, 0:TM], pb[0:128, 0:TM], b[:, m:m + 1], a[:, 0:TM], ALU.add, ALU.add),
                [pb, fb2, xt[m]], [xt[m]])
            dma(ctx, "pool", xout.ap()[128 * m:128 * (m + 1), c0:c0 + TM], xt[m][:, 0:TM],
                reads=[xt[m]], writes=[(xout, ("d", ch))], is_out=True)
        ctx.free(*h0, *h1, *xt)
    ctx.pop()


EXP = 64
KTOP = 8
CAP = 512
NLOC = 8 * T


def moe_shared_buffers(nc, pfx):
    nloc = KTOP * T
    return {
        "xtm_loc": nc.dram_tensor(f"{pfx}_xtm", [T, H], BF16),
        "xrep": nc.dram_tensor(f"{pfx}_xrep", [NCORES * T, H], BF16),
        "idx_scr": nc.dram_tensor(f"{pfx}_idxs", [nloc], F32),
        "rank_scr": nc.dram_tensor(f"{pfx}_rank", [nloc], F32),
        "cnt_loc": nc.dram_tensor(f"{pfx}_cntl", [EXP, 1], F32),
        "cnt_all": nc.dram_tensor(f"{pfx}_cnta", [NCORES * EXP, 1], F32),
        "canvas_loc": nc.dram_tensor(f"{pfx}_cvl", [EXP * CAP, 2], F32),
        "canvas": nc.dram_tensor(f"{pfx}_cv", [EXP * CAP, 2], F32),
        "pub_loc": nc.dram_tensor(f"{pfx}_pub", [8 * CAP, H], BF16),
        "outrep": nc.dram_tensor(f"{pfx}_orep", [EXP * CAP, H], BF16),
        "sh_scr": nc.dram_tensor(f"{pfx}_sh", [H, T], BF16),
    }


def emit_moe(ctx, nc, xin, xout, dr, wb, sb, meta, scale=0.5):
    NTT = T // 128
    idxf = ctx.sbuf((128, NTT * KTOP), F32)
    w_all = ctx.sbuf((128, NTT * KTOP), F32)
    si = ctx.sbuf((128, NTT * KTOP), I32)
    tokf = ctx.sbuf((128, NTT), F32)
    tok0 = ctx.sbuf((128, 1), F32)
    dma(ctx, "sp", tok0[:], meta["tokid0"].ap(), reads=[meta["tokid0"]], writes=[tok0])
    for tt in range(NTT):
        op_dve(ctx, lambda E, tt=tt: E.tensor_scalar(
            tokf[:, tt:tt + 1], tok0[:].to_broadcast([128, 1]), float(128 * tt), None, ALU.add),
            [tok0], [(tokf, tt)])

    ctx.push()
    gwp = []
    for k in range(FT):
        t = ctx.sbuf((128, EXP), F32)
        dma(ctx, "sp", t[0:128, 0:EXP], wb["gate_f32"].ap()[128 * k:128 * (k + 1), 0:EXP],
            reads=[wb["gate_f32"]], writes=[t])
        gwp.append(t)
    sgup = load_w_direct_panels(ctx, wb["sgu"], H, 2 * IS)
    sdwp = load_w_direct_panels(ctx, wb["sdw"], IS, H)
    sgwp = load_w_direct_panels(ctx, wb["sg"], H, 1)
    for ch in range(NCH):
        c0 = ch * TC
        xt = [ctx.tile((128, TC), F32) for _ in range(FT)]
        for k in range(FT):
            dma(ctx, "sp", xt[k][:], xin.ap()[128 * k:128 * (k + 1), c0:c0 + TC],
                reads=[(xin, ("c", ch))], writes=[xt[k]])
        xn = ln_chunk(ctx, xt, TC)
        xnb = [ctx.tile((128, TC), BF16) for _ in range(FT)]
        for k in range(FT):
            op_act(ctx, lambda E, a=xnb[k], b=xn[k]: E.activation(a[:, 0:TC], b[:, 0:TC], AF.Copy),
                   [xn[k]], [xnb[k]])
        ctx.free(*xt)
        xtm = ctx.tile((128, H), BF16)
        for sub in range(TC // 128):
            tt = ch * (TC // 128) + sub
            pbf = ctx.psum_bf2[sub % 2]
            for k in range(FT):
                op_pe(ctx, lambda E, k=k, sub=sub, a=xnb[k], pbf=pbf: E.transpose(
                    pbf[0:128, 128 * k:128 * (k + 1)], a[:, 128 * sub:128 * (sub + 1)],
                    ctx.consts["ident_bf"][0:128, 0:128]),
                    [xnb[k], ctx.consts["ident_bf"]], [(pbf, k), (pbf, "bank")])
            for k in range(FT):
                op_act(ctx, lambda E, k=k, a=xtm, pbf=pbf: E.activation(
                    a[:, 128 * k:128 * (k + 1)], pbf[0:128, 128 * k:128 * (k + 1)], AF.Copy),
                    [(pbf, k), (pbf, "bank")], [(xtm, k)])
            dma(ctx, "pool", sb["xtm_loc"].ap()[128 * tt:128 * (tt + 1), 0:H], xtm[:, 0:H],
                reads=[(xtm, k) for k in range(FT)], writes=[(sb["xtm_loc"], tt)])
            pl = ctx.psum[sub % 2]
            for k in range(FT):
                op_pe(ctx, lambda E, k=k, sub=sub, pl=pl, a=xn[k], g=gwp[k]: E.matmul(
                    pl[0:128, 0:EXP], a[:, 128 * sub:128 * (sub + 1)], g[:, 0:EXP],
                    start=(k == 0), stop=(k == FT - 1)),
                    [xn[k], gwp[k]], [pl])
            lg = ctx.tile((128, EXP), F32)
            mx = ctx.tile((128, KTOP), F32)
            mi = ctx.tile((128, KTOP), U32)
            op_dve(ctx, lambda E, pl=pl, a=lg: E.tensor_copy(a[:, 0:EXP], pl[0:128, 0:EXP]),
                   [pl], [lg])
            op_dve(ctx, lambda E, a=mx, b=lg: E.max(out=a[:], in_=b[:, 0:EXP]), [lg], [mx])
            op_dve(ctx, lambda E, a=mi, b=mx, c=lg: E.max_index(a[:], b[:], c[:, 0:EXP]),
                   [mx, lg], [mi])
            we = ctx.tile((128, KTOP), F32)
            op_dve(ctx, lambda E, a=we, b=mx: E.tensor_scalar(
                a[:], b[:], b[:, 0:1], None, ALU.subtract), [mx], [we])
            op_act(ctx, lambda E, a=we: E.activation(a[:], a[:], AF.Exp), [we], [we])
            sm = ctx.tile((128, 1), F32)
            op_dve(ctx, lambda E, a=sm, b=we: E.tensor_reduce(a[:], b[:], AX.X, ALU.add),
                   [we], [sm])
            op_dve(ctx, lambda E, a=sm: E.reciprocal(a[:], a[:]), [sm], [sm])
            op_dve(ctx, lambda E, a=w_all, b=we, c=sm, tt=tt: E.tensor_scalar(
                a[:, KTOP * tt:KTOP * (tt + 1)], b[:], c[:, 0:1], None, ALU.mult),
                [we, sm], [(w_all, tt)])
            op_dve(ctx, lambda E, a=idxf, b=mi, tt=tt: E.tensor_copy(
                a[:, KTOP * tt:KTOP * (tt + 1)], b[:]),
                [mi], [(idxf, tt)])
            ctx.free(lg, mx, mi, we, sm)
        pgt = ctx.psum[4]
        for k in range(FT):
            op_pe(ctx, lambda E, k=k, pgt=pgt, a=xnb[k], w=sgwp[k]: E.matmul(
                pgt[0:1, 0:TC], w[:, 0:1], a[:, 0:TC], start=(k == 0), stop=(k == FT - 1)),
                [xnb[k], sgwp[k]], [pgt])
        grow = ctx.tile((1, 512), F32)
        op_act(ctx, lambda E, a=grow, pgt=pgt: E.activation(a[0:1, 0:TC], pgt[0:1, 0:TC], AF.Sigmoid),
               [pgt], [grow])
        op_pe(ctx, lambda E, pgt=pgt, a=grow: E.matmul(
            pgt[0:128, 0:TC], ctx.consts["ones_row_f"][0:1, 0:128], a[0:1, 0:TC],
            start=True, stop=True),
            [grow, ctx.consts["ones_row_f"], pgt], [(pgt, "bc")])
        gbc = ctx.tile((128, TC), F32)
        op_dve(ctx, lambda E, a=gbc, pgt=pgt: E.tensor_copy(a[:, 0:TC], pgt[0:128, 0:TC]),
               [(pgt, "bc")], [gbc])
        actb = [ctx.tile((128, TC), BF16) for _ in range(IS // 128)]
        sg_t = ctx.tile((128, TC), F32)
        for mi2 in range(IS // 128):
            pg = ctx.psum[2]
            pu = ctx.psum[3]
            for k in range(FT):
                op_pe(ctx, lambda E, k=k, m=mi2, pg=pg, a=xnb[k], w=sgup[k]: E.matmul(
                    pg[0:128, 0:TC], w[:, 128 * m:128 * (m + 1)], a[:, 0:TC],
                    start=(k == 0), stop=(k == FT - 1)),
                    [sgup[k], xnb[k]], [pg])
            for k in range(FT):
                op_pe(ctx, lambda E, k=k, m=mi2, pu=pu, a=xnb[k], w=sgup[k]: E.matmul(
                    pu[0:128, 0:TC], w[:, IS + 128 * m:IS + 128 * (m + 1)], a[:, 0:TC],
                    start=(k == 0), stop=(k == FT - 1)),
                    [sgup[k], xnb[k]], [pu])
            op_act(ctx, lambda E, pg=pg, a=sg_t: E.activation(a[:, 0:TC], pg[0:128, 0:TC], AF.Silu),
                   [pg], [sg_t])
            op_dve(ctx, lambda E, pu=pu, a=actb[mi2], b=sg_t: E.tensor_tensor(
                a[:, 0:TC], b[:, 0:TC], pu[0:128, 0:TC], ALU.mult),
                [pu, sg_t], [actb[mi2]])
        for m in range(FT):
            pd = ctx.psum[4 + (m % 2)]
            for k in range(IS // 128):
                op_pe(ctx, lambda E, m=m, k=k, pd=pd, a=actb[k], w=sdwp[k]: E.matmul(
                    pd[0:128, 0:TC], w[:, 128 * m:128 * (m + 1)], a[:, 0:TC],
                    start=(k == 0), stop=(k == IS // 128 - 1)),
                    [actb[k], sdwp[k]], [pd])
            shb = ctx.tile((128, TC), BF16)
            op_dve(ctx, lambda E, pd=pd, a=shb, g=gbc: E.tensor_tensor(
                a[:, 0:TC], pd[0:128, 0:TC], g[:, 0:TC], ALU.mult),
                [pd, gbc], [shb])
            dma(ctx, "pool", sb["sh_scr"].ap()[128 * m:128 * (m + 1), c0:c0 + TC], shb[:, 0:TC],
                reads=[shb], writes=[(sb["sh_scr"], (ch, m))])
            ctx.free(shb)
        ctx.free(*xnb, *xn, xtm, grow, gbc, sg_t, *actb)
        ntc = TC // 128
        dst = bass.AP(sb["idx_scr"], ntc * ch * 128 * KTOP,
                      [[KTOP, 128], [128 * KTOP, ntc], [1, KTOP]])
        dma(ctx, "pool", dst, idxf[:, KTOP * ntc * ch:KTOP * ntc * (ch + 1)],
            reads=[(idxf, tt) for tt in range(ntc * ch, ntc * (ch + 1))],
            writes=[(sb["idx_scr"], ch)])
    op_pool(ctx, lambda E: E.collective_compute(
        "AllGather", ALU.bypass, replica_groups=[list(range(NCORES))],
        ins=[sb["xtm_loc"].ap().opt()], outs=[sb["xrep"].ap().opt()]),
        reads=[sb["xtm_loc"]], writes=[sb["xrep"]], inc=-1)
    ctx.pop()

    ctx.push()
    nloc = KTOP * T
    ACH = min(8192, nloc)
    nach = nloc // ACH
    ecol = ctx.sbuf((EXP, 1), F32)
    op_pool(ctx, lambda E: E.iota(ecol[:], [[1, 1]], base=0, channel_multiplier=1,
                                  allow_small_or_imprecise_dtypes=True), [], [ecol])
    cnt = ctx.sbuf((EXP, 1), F32)
    op_dve(ctx, lambda E: E.memset(cnt[:], 0.0), [], [cnt])
    zcol = ctx.constcol(0.0)
    irep = ctx.tile((EXP, ACH), F32)
    oneh = ctx.tile((EXP, ACH), F32)
    csum = ctx.tile((EXP, ACH), F32)
    for a0 in range(nach):
        src = bass.AP(sb["idx_scr"], a0 * ACH, [[0, 1], [1, ACH]])
        dma(ctx, "sp", irep[:], src.partition_broadcast(EXP), reads=[sb["idx_scr"]],
            writes=[irep])
        op_dve(ctx, lambda E: E.tensor_scalar(oneh[:], irep[:], ecol[:, 0:1], None, ALU.is_equal),
               [irep, ecol], [oneh])
        op_dve(ctx, lambda E: E.tensor_reduce(csum[:, 0:1], oneh[:], AX.X, ALU.add),
               [oneh], [(csum, "r")])
        op_dve(ctx, lambda E: E.tensor_add(cnt[:], cnt[:], csum[:, 0:1]),
               [(csum, "r"), cnt], [cnt])
    dma(ctx, "pool", sb["cnt_loc"].ap(), cnt[:], reads=[cnt], writes=[sb["cnt_loc"]])
    op_pool(ctx, lambda E: E.collective_compute(
        "AllGather", ALU.bypass, replica_groups=[list(range(NCORES))],
        ins=[sb["cnt_loc"].ap().opt()], outs=[sb["cnt_all"].ap().opt()]),
        reads=[sb["cnt_loc"]], writes=[sb["cnt_all"]], inc=-1)
    cnt8 = ctx.tile((EXP, NCORES), F32)
    src = bass.AP(sb["cnt_all"], 0, [[1, EXP], [EXP, NCORES]])
    dma(ctx, "sp", cnt8[:], src, reads=[sb["cnt_all"]], writes=[cnt8],
        allow_slow_non_contiguous=True)
    mlt = ctx.tile((EXP, NCORES), F32)
    msrc = bass.AP(meta["mask_lt"], 0, [[0, 1], [1, NCORES]])
    dma(ctx, "sp", mlt[:], msrc.partition_broadcast(EXP), reads=[meta["mask_lt"]],
        writes=[mlt])
    base = ctx.tile((EXP, 1), F32)
    op_dve(ctx, lambda E: E.tensor_mul(cnt8[:], cnt8[:], mlt[:]), [cnt8, mlt], [cnt8])
    op_dve(ctx, lambda E: E.tensor_reduce(base[:], cnt8[:], AX.X, ALU.add),
           [cnt8], [base])
    carry = ctx.tile((EXP, 1), F32)
    op_dve(ctx, lambda E: E.memset(carry[:], 0.0), [], [carry])
    rk = ctx.tile((EXP, ACH), F32)
    ev1 = ctx.tile((1, 512), F32)
    for a0 in range(nach):
        src = bass.AP(sb["idx_scr"], a0 * ACH, [[0, 1], [1, ACH]])
        dma(ctx, "sp", irep[:], src.partition_broadcast(EXP), reads=[sb["idx_scr"]],
            writes=[irep])
        op_dve(ctx, lambda E: E.tensor_scalar(oneh[:], irep[:], ecol[:, 0:1], None, ALU.is_equal),
               [irep, ecol], [oneh])
        op_dve(ctx, lambda E: E.tensor_tensor_scan(
            csum[:], oneh[:], zcol[0:EXP, 0:1].to_broadcast([EXP, ACH]), carry[:, 0:1],
            ALU.add, ALU.add),
            [oneh, zcol, carry], [csum])
        op_dve(ctx, lambda E: E.tensor_copy(carry[:], csum[:, ACH - 1:ACH]),
               [csum], [carry])
        op_dve(ctx, lambda E: E.tensor_scalar(csum[:], csum[:], base[:, 0:1], None, ALU.add),
               [csum, base], [csum])
        op_dve(ctx, lambda E: E.tensor_mul(csum[:], csum[:], oneh[:]), [oneh, csum], [csum])
        for q in range(ACH // 512):
            pq = ctx.psum[q % 2]
            op_pe(ctx, lambda E, q=q, pq=pq: E.matmul(
                pq[0:1, 0:512], ctx.consts["ones_col_f"][0:EXP, 0:1],
                csum[:, 512 * q:512 * (q + 1)], start=True, stop=True),
                [csum, ctx.consts["ones_col_f"]], [pq])
            op_dve(ctx, lambda E, pq=pq: E.tensor_copy(ev1[0:1, 0:512], pq[0:1, 0:512]),
                   [pq], [ev1])
            dma(ctx, "pool", bass.AP(sb["rank_scr"], a0 * ACH + 512 * q, [[0, 1], [1, 512]]),
                ev1[0:1, 0:512], reads=[ev1], writes=[(sb["rank_scr"], (a0, q))])
    ctx.free(irep, oneh, csum, rk, ev1, cnt8, mlt)
    grk = ctx.tile((128, NTT * KTOP), F32)
    rsrc = bass.AP(sb["rank_scr"], 0, [[KTOP, 128], [128 * KTOP, NTT], [1, KTOP]])
    dma(ctx, "sp", grk[:], rsrc, reads=[(sb["rank_scr"], (a, q)) for a in range(nach)
                                        for q in range(ACH // 512)], writes=[grk])
    op_dve(ctx, lambda E: E.tensor_scalar(grk[:], grk[:], -1.0, None, ALU.add),
           [grk], [grk])
    srow = ctx.tile((128, NTT * KTOP), F32)
    op_dve(ctx, lambda E: E.scalar_tensor_tensor(srow[:], idxf[:], float(CAP), grk[:],
                                                 ALU.mult, ALU.add),
           [(idxf, tt) for tt in range(NTT)] + [grk], [srow])
    surv = ctx.tile((128, NTT * KTOP), F32)
    op_dve(ctx, lambda E: E.tensor_scalar(surv[:], grk[:], float(CAP), None, ALU.is_lt),
           [grk], [surv])
    op_dve(ctx, lambda E: E.tensor_mul(srow[:], srow[:], surv[:]), [surv, srow], [srow])
    op_dve(ctx, lambda E: E.tensor_scalar(surv[:], surv[:], -1e7, 1e7, ALU.mult, ALU.add),
           [surv], [surv])
    op_dve(ctx, lambda E: E.tensor_add(srow[:], srow[:], surv[:]), [surv, srow], [srow])
    op_dve(ctx, lambda E: E.tensor_copy(si[:], srow[:]), [srow], [si])
    ctx.free(srow, surv, grk)
    zt = ctx.tile((128, 512), F32)
    op_dve(ctx, lambda E: E.memset(zt[:], 0.0), [], [zt])
    dma(ctx, "pool", bass.AP(sb["canvas_loc"], 0, [[512, 128], [1, 512]]), zt[:],
        reads=[zt], writes=[sb["canvas_loc"]])
    val = ctx.tile((128, 2 * KTOP), F32)
    for tt in range(NTT):
        for j in range(KTOP):
            col = tt * KTOP + j
            op_dve(ctx, lambda E, tt=tt, j=j, col=col: E.tensor_copy(
                bass.AP(val, 2 * j, [[2 * KTOP, 128], [1, 1]]), tokf[:, tt:tt + 1]),
                [(tokf, tt)], [(val, j)]) if False else None
        op_dve(ctx, lambda E, tt=tt: E.tensor_copy(
            bass.AP(val, 0, [[2 * KTOP, 128], [2, KTOP]]),
            tokf[:, tt:tt + 1].to_broadcast([128, KTOP])),
            [(tokf, tt)], [(val, "t")])
        op_dve(ctx, lambda E, tt=tt: E.tensor_copy(
            bass.AP(val, 1, [[2 * KTOP, 128], [2, KTOP]]),
            w_all[:, KTOP * tt:KTOP * (tt + 1)]),
            [(w_all, tt)], [(val, "w")])
        for j in range(KTOP):
            ctx.pr.op("pool", lambda E, tt=tt, j=j: E.indirect_dma_start(
                out=sb["canvas_loc"].ap(),
                out_offset=IndirectOffsetOnAxis(ap=si[:, tt * KTOP + j:tt * KTOP + j + 1], axis=0),
                in_=val[:, 2 * j:2 * j + 2], in_offset=None,
                bounds_check=ctx.breg(E, EXP * CAP - 1), oob_is_err=False),
                reads=[si, (val, "t"), (val, "w")],
                writes=[(sb["canvas_loc"], (tt, j))], inc=16)
    op_pool(ctx, lambda E: E.collective_compute(
        "AllReduce", ALU.add, replica_groups=[list(range(NCORES))],
        ins=[sb["canvas_loc"].ap().opt()], outs=[sb["canvas"].ap().opt()]),
        reads=[sb["canvas_loc"]], writes=[sb["canvas"]], inc=-1)
    ctx.pop()

    ctx.push()
    slot0 = ctx.sbuf((128, 1), I32)
    dma(ctx, "sp", slot0[:], meta["myslot0"].ap(), reads=[meta["myslot0"]], writes=[slot0])
    pbf = ctx.psum_bf
    panq = [(load_w_direct_panels_off(ctx, wb["egu"], 0, H, 2 * I_EXP),
             load_w_direct_panels_off(ctx, wb["edw"], 0, I_EXP, H))]
    xg2 = [ctx.sbuf((128, H), BF16) for _ in range(2)]
    ti2 = [ctx.sbuf((128, 1), I32) for _ in range(2)]
    for el in range(8):
        egup, edwp = panq.pop(0)
        if el < 7:
            panq.append((load_w_direct_panels_off(ctx, wb["egu"], (el + 1) * H, H, 2 * I_EXP),
                         load_w_direct_panels_off(ctx, wb["edw"], (el + 1) * I_EXP, I_EXP, H)))
        tw = [ctx.tile((128, 2), F32) for _ in range(4)]
        wcol = [ctx.tile((128, 1), F32) for _ in range(4)]
        xfm = [ctx.tile((128, 512), BF16) for _ in range(FT)]
        for sg in range(4):
            ti = ti2[sg % 2]
            sidx = ctx.tile((128, 1), I32)
            op_dve(ctx, lambda E, a=sidx, b=slot0, el=el, sg=sg: E.tensor_scalar(
                a[:], b[:], el * CAP + sg * 128, None, ALU.add), [slot0], [sidx])
            ctx.pr.op("pool", lambda E, a=tw[sg], b=sidx: E.indirect_dma_start(
                out=a[:], out_offset=None, in_=sb["canvas"].ap(),
                in_offset=IndirectOffsetOnAxis(ap=b[:], axis=0)),
                reads=[sb["canvas"], sidx], writes=[tw[sg]], inc=16)
            op_dve(ctx, lambda E, a=wcol[sg], b=tw[sg]: E.tensor_copy(a[:], b[:, 1:2]),
                   [tw[sg]], [wcol[sg]])
            op_dve(ctx, lambda E, a=ti, b=tw[sg]: E.tensor_copy(a[:], b[:, 0:1]),
                   [tw[sg]], [ti])
            xg = xg2[sg % 2]
            ctx.pr.op("pool", lambda E, a=xg, b=ti: E.indirect_dma_start(
                out=a[:, 0:H], out_offset=None, in_=sb["xrep"].ap(),
                in_offset=IndirectOffsetOnAxis(ap=b[:], axis=0)),
                reads=[sb["xrep"], ti], writes=[xg], inc=16)
            pbf = ctx.psum_bf2[sg % 2]
            for k in range(FT):
                op_pe(ctx, lambda E, k=k, a=xg, pbf=pbf: E.transpose(
                    pbf[0:128, 128 * k:128 * (k + 1)], a[:, 128 * k:128 * (k + 1)],
                    ctx.consts["ident_bf"][0:128, 0:128]),
                    [xg, ctx.consts["ident_bf"]], [(pbf, k), (pbf, "bank")])
            for k in range(FT):
                op_act(ctx, lambda E, k=k, sg=sg, a=xfm[k], pbf=pbf: E.activation(
                    a[:, 128 * sg:128 * (sg + 1)], pbf[0:128, 128 * k:128 * (k + 1)], AF.Copy),
                    [(pbf, k), (pbf, "bank")], [(xfm[k], sg)])
            ctx.free(sidx)
        actb = [ctx.tile((128, 512), BF16) for _ in range(4)]
        sg_t = ctx.tile((128, 512), F32)
        for mi2 in range(4):
            pg = ctx.psum[mi2 % 2]
            pu = ctx.psum[2 + mi2 % 2]
            for k in range(FT):
                op_pe(ctx, lambda E, k=k, m=mi2, pg=pg, w=egup[k], a=xfm[k]: E.matmul(
                    pg[0:128, 0:512], w[:, 128 * m:128 * (m + 1)], a[:, 0:512],
                    start=(k == 0), stop=(k == FT - 1)),
                    [egup[k], xfm[k]], [pg])
            for k in range(FT):
                op_pe(ctx, lambda E, k=k, m=mi2, pu=pu, w=egup[k], a=xfm[k]: E.matmul(
                    pu[0:128, 0:512], w[:, 512 + 128 * m:512 + 128 * (m + 1)], a[:, 0:512],
                    start=(k == 0), stop=(k == FT - 1)),
                    [egup[k], xfm[k]], [pu])
            op_act(ctx, lambda E, pg=pg, a=sg_t: E.activation(a[:, 0:512], pg[0:128, 0:512], AF.Silu),
                   [pg], [sg_t])
            op_dve(ctx, lambda E, pu=pu, a=actb[mi2], b=sg_t: E.tensor_tensor(
                a[:, 0:512], b[:, 0:512], pu[0:128, 0:512], ALU.mult),
                [pu, sg_t], [actb[mi2]])
        ctx.free(sg_t, *xfm)
        pub = ctx.tile((128, 512), BF16)
        for sl in range(4):
            for nh in range(2):
                pd = ctx.psum[4 + (2 * sl + nh) % 2]
                for k in range(4):
                    op_pe(ctx, lambda E, sl=sl, nh=nh, k=k, pd=pd, a=actb[k], w=edwp[k]: E.matmul(
                        pd[0:128, 0:512], a[:, 128 * sl:128 * (sl + 1)],
                        w[:, 512 * nh:512 * (nh + 1)], start=(k == 0), stop=(k == 3)),
                        [actb[k], edwp[k]], [pd])
                op_dve(ctx, lambda E, pd=pd, a=pub, wc=wcol[sl]: E.tensor_scalar(
                    a[:, 0:512], pd[0:128, 0:512], wc[:, 0:1], None, ALU.mult),
                    [pd, wcol[sl]], [pub])
                dma(ctx, "pool",
                    sb["pub_loc"].ap()[el * CAP + 128 * sl:el * CAP + 128 * (sl + 1),
                                       512 * nh:512 * (nh + 1)],
                    pub[:, 0:512], reads=[pub], writes=[(sb["pub_loc"], (el, sl, nh))])
        ctx.free(pub, *actb, *tw, *wcol, *egup, *edwp)
    op_pool(ctx, lambda E: E.collective_compute(
        "AllGather", ALU.bypass, replica_groups=[list(range(NCORES))],
        ins=[sb["pub_loc"].ap().opt()], outs=[sb["outrep"].ap().opt()]),
        reads=[sb["pub_loc"]], writes=[sb["outrep"]], inc=-1)
    ctx.pop()

    ctx.push()
    TJ = 256
    yj2 = [[ctx.sbuf((128, H), BF16) for _ in range(KTOP)] for _ in range(2)]
    ytk2 = [ctx.sbuf((128, H), F32) for _ in range(2)]
    tb2 = [ctx.sbuf((128, H), F32) for _ in range(2)]
    for ch in range(T // TJ):
        c0 = ch * TJ
        xt = [ctx.tile((128, TJ), F32) for _ in range(FT)]
        for k in range(FT):
            dma(ctx, "sp", xt[k][:, 0:TJ], xin.ap()[128 * k:128 * (k + 1), c0:c0 + TJ],
                reads=[(xin, ("j", ch))], writes=[xt[k]])
        yex = [ctx.tile((128, TJ), F32) for _ in range(FT)]
        for sub in range(TJ // 128):
            tt = ch * (TJ // 128) + sub
            yj = yj2[tt % 2]
            ytok = ytk2[tt % 2]
            tb = tb2[tt % 2]
            for j in range(KTOP):
                op_dve(ctx, lambda E, a=yj[j]: E.memset(a[:], 0.0), [], [yj[j]])
                ctx.pr.op("pool", lambda E, a=yj[j], tt=tt, j=j: E.indirect_dma_start(
                    out=a[:, 0:H], out_offset=None, in_=sb["outrep"].ap(),
                    in_offset=IndirectOffsetOnAxis(
                        ap=si[:, tt * KTOP + j:tt * KTOP + j + 1], axis=0),
                    bounds_check=ctx.breg(E, EXP * CAP - 1),
                    oob_is_err=False),
                    reads=[sb["outrep"], si, yj[j]], writes=[yj[j]], inc=16)
            op_dve(ctx, lambda E, a=ytok, b=yj[0], c=yj[1]: E.tensor_tensor(
                a[:, 0:H], b[:, 0:H], c[:, 0:H], ALU.add),
                [yj[0], yj[1]], [ytok])
            for j in (2, 4, 6):
                op_dve(ctx, lambda E, a=tb, b=yj[j], c=yj[j + 1]: E.tensor_tensor(
                    a[:, 0:H], b[:, 0:H], c[:, 0:H], ALU.add),
                    [yj[j], yj[j + 1]], [tb])
                op_dve(ctx, lambda E, a=ytok, b=tb: E.tensor_add(a[:, 0:H], a[:, 0:H], b[:, 0:H]),
                       [ytok, tb], [ytok])
            for k in range(FT):
                pf = ctx.psum[k % 4]
                op_pe(ctx, lambda E, k=k, pf=pf, a=ytok: E.transpose(
                    pf[0:128, 0:128], a[:, 128 * k:128 * (k + 1)],
                    ctx.consts["ident_f"][0:128, 0:128]),
                    [ytok, ctx.consts["ident_f"]], [pf])
                op_act(ctx, lambda E, k=k, sub=sub, pf=pf, a=yex[k]: E.activation(
                    a[:, 128 * sub:128 * (sub + 1)], pf[0:128, 0:128], AF.Copy),
                    [pf], [(yex[k], sub)])
        for m in range(FT):
            shb = ctx.tile((128, TJ), BF16)
            dma(ctx, "sp", shb[:, 0:TJ], sb["sh_scr"].ap()[128 * m:128 * (m + 1), c0:c0 + TJ],
                reads=[(sb["sh_scr"], (c0 // TC, m))], writes=[shb])
            op_dve(ctx, lambda E, a=yex[m], b=shb: E.tensor_add(
                a[:, 0:TJ], a[:, 0:TJ], b[:, 0:TJ]),
                [(yex[m], 0), (yex[m], 1), shb], [yex[m]])
            op_dve(ctx, lambda E, a=xt[m], b=yex[m]: E.scalar_tensor_tensor(
                a[:, 0:TJ], b[:, 0:TJ], scale, a[:, 0:TJ], ALU.mult, ALU.add),
                [yex[m], xt[m]], [xt[m]])
            dma(ctx, "pool", xout.ap()[128 * m:128 * (m + 1), c0:c0 + TJ], xt[m][:, 0:TJ],
                reads=[xt[m]], writes=[(xout, ("j", ch))], is_out=True)
            ctx.free(shb)
        ctx.free(*xt, *yex)
    ctx.pop()


IS = 2048
I_EXP = 512


def load_w_direct_panels_off(ctx, wb, r0, n_in, n_out):
    panels = []
    for k in range(n_in // 128):
        t = ctx.tile((128, n_out), BF16)
        dma(ctx, "sp", t[0:128, 0:n_out],
            wb.ap()[r0 + 128 * k:r0 + 128 * (k + 1), 0:n_out],
            reads=[wb], writes=[t])
        panels.append(t)
    return panels


def stage_moe_weights_dram(ctx, nc, dr, pfx):
    return {
        "gate_f32": dr["gate_w"],
        "gate": cast_to_bf16_dram(ctx, nc, f"{pfx}_gw", dr["gate_w"], (H, EXP)),
        "sg": cast_to_bf16_dram(ctx, nc, f"{pfx}_sgw", dr["shared_gate_w"], (H, 1)),
        "sgu": cast_to_bf16_dram(ctx, nc, f"{pfx}_sguw", dr["shared_gate_up_w"], (H, 2 * IS)),
        "sdw": cast_to_bf16_dram(ctx, nc, f"{pfx}_sdww", dr["shared_down_w"], (IS, H)),
        "egu": cast_to_bf16_dram(ctx, nc, f"{pfx}_eguw", dr["my_egu"], (8 * H, 2 * I_EXP)),
        "edw": cast_to_bf16_dram(ctx, nc, f"{pfx}_edww", dr["my_edw"], (8 * I_EXP, H)),
    }


def build_cm_program():
    ctx = new_ctx()
    nc = ctx.nc
    dr = decl_const_inputs(nc)
    dr["L1_W_r"] = nc.dram_tensor("W_r", [H, H], F32, kind="ExternalInput")
    dr["L1_W_k"] = nc.dram_tensor("W_k", [H, H], F32, kind="ExternalInput")
    dr["L1_W_v"] = nc.dram_tensor("W_v", [H, H], F32, kind="ExternalInput")
    dr["L1_mu_r"] = nc.dram_tensor("mu_r", [H], F32, kind="ExternalInput")
    dr["L1_mu_k"] = nc.dram_tensor("mu_k", [H], F32, kind="ExternalInput")
    x = nc.dram_tensor("x", [H, T], F32, kind="ExternalInput")
    y = nc.dram_tensor("y", [H, T], F32, kind="ExternalOutput")
    setup_consts(ctx, dr)
    w = stage_cm_weights(ctx, nc, dr, "L1")
    emit_cm(ctx, x, y, w["wTr"], w["wTk"], w["wTv"], w["mur"], w["muk"], 0.5)
    ctx.finish()
    return nc


def build_rwkv_program():
    ctx = new_ctx()
    nc = ctx.nc
    dr = decl_const_inputs(nc)
    for nm, shp in (("Wk", [H, H]), ("Wv", [H, H]), ("Wr", [H, H]), ("Wo", [H, H]),
                    ("time_decay", [H]), ("time_first", [H]), ("bo", [H]),
                    ("time_mix_key", [1, 1, H]), ("time_mix_value", [1, 1, H]),
                    ("time_mix_receptance", [1, 1, H])):
        dr[nm] = nc.dram_tensor(nm, shp, F32, kind="ExternalInput")
    x = nc.dram_tensor("x", [H, T], F32, kind="ExternalInput")
    y = nc.dram_tensor("y", [H, T], F32, kind="ExternalOutput")
    setup_consts(ctx, dr)
    w = stage_rwkv_weights(ctx, nc, dr)
    emit_rwkv(ctx, x, y, w)
    ctx.finish()
    return nc


def build_mamba_program():
    ctx = new_ctx()
    nc = ctx.nc
    dr = decl_const_inputs(nc)
    for nm, shp in (("in_proj_w", [H, 2 * DI]), ("conv_w", [DI, 4]), ("conv_b", [DI]),
                    ("x_proj_w", [DI, 96]), ("dt_proj_w", [DTR, DI]), ("dt_proj_b", [DI]),
                    ("A_log", [DI, NN]), ("D", [DI]), ("out_proj_w", [DI, H]),
                    ("ln_g", [H]), ("ln_b", [H]), ("ff_w1", [H, 4 * H]), ("ff_b1", [4 * H]),
                    ("ff_w2", [4 * H, H]), ("ff_b2", [H])):
        dr[nm] = nc.dram_tensor(nm, shp, F32, kind="ExternalInput")
    x = nc.dram_tensor("x", [H, T], F32, kind="ExternalInput")
    y = nc.dram_tensor("y", [H, T], F32, kind="ExternalOutput")
    setup_consts(ctx, dr)
    emit_mamba(ctx, nc, x, y, dr)
    ctx.finish()
    return nc


def decl_moe_meta(nc):
    return {
        "mask_lt": nc.dram_tensor("mask_lt", [1, NCORES], F32, kind="ExternalInput"),
        "tokid0": nc.dram_tensor("tokid0", [128, 1], F32, kind="ExternalInput"),
        "myslot0": nc.dram_tensor("myslot0", [128, 1], I32, kind="ExternalInput"),
    }


def host_moe_meta(me, Tloc):
    return {
        "mask_lt": np.asarray([[1.0 if c < me else 0.0 for c in range(NCORES)]], np.float32),
        "tokid0": (me * Tloc + np.arange(128, dtype=np.float32)).reshape(128, 1),
        "myslot0": (me * 8 * CAP + np.arange(128, dtype=np.int32)).reshape(128, 1),
    }


def build_moe_program():
    ctx = new_ctx()
    nc = ctx.nc
    dr = decl_const_inputs(nc)
    for nm, shp in (("gate_w", [H, EXP]), ("shared_gate_w", [H, 1]),
                    ("shared_gate_up_w", [H, 2 * IS]), ("shared_down_w", [IS, H]),
                    ("my_egu", [8, H, 2 * I_EXP]), ("my_edw", [8, I_EXP, H])):
        dr[nm] = nc.dram_tensor(nm, shp, F32, kind="ExternalInput")
    meta = decl_moe_meta(nc)
    x = nc.dram_tensor("x", [H, T], F32, kind="ExternalInput")
    y = nc.dram_tensor("y", [H, T], F32, kind="ExternalOutput")
    setup_consts(ctx, dr)
    wb = stage_moe_weights_dram(ctx, nc, dr, "m0")
    sbuffs = moe_shared_buffers(nc, "m0")
    emit_moe(ctx, nc, x, y, dr, wb, sbuffs, meta, 0.5)
    ctx.finish()
    return nc


FUSED = True


def build_fused_program():
    ctx = new_ctx()
    nc = ctx.nc
    dr = decl_const_inputs(nc)
    meta = decl_moe_meta(nc)
    names = [("L0_ffn_gate_w", [H, EXP]), ("L0_ffn_shared_gate_w", [H, 1]),
             ("L0_ffn_shared_gate_up_w", [H, 2 * IS]), ("L0_ffn_shared_down_w", [IS, H]),
             ("my_egu", [8, H, 2 * I_EXP]), ("my_edw", [8, I_EXP, H]),
             ("L0_attn_time_decay", [H]), ("L0_attn_time_first", [H]),
             ("L0_attn_time_mix_key", [1, 1, H]), ("L0_attn_time_mix_value", [1, 1, H]),
             ("L0_attn_time_mix_receptance", [1, 1, H]),
             ("L0_attn_Wk", [H, H]), ("L0_attn_Wv", [H, H]), ("L0_attn_Wr", [H, H]),
             ("L0_attn_Wo", [H, H]), ("L0_attn_bo", [H]),
             ("L1_ffn_W_r", [H, H]), ("L1_ffn_W_k", [H, H]), ("L1_ffn_W_v", [H, H]),
             ("L1_ffn_mu_r", [H]), ("L1_ffn_mu_k", [H]),
             ("L1_attn_in_proj_w", [H, 2 * DI]), ("L1_attn_conv_w", [DI, 4]),
             ("L1_attn_conv_b", [DI]), ("L1_attn_x_proj_w", [DI, 96]),
             ("L1_attn_dt_proj_w", [DTR, DI]), ("L1_attn_dt_proj_b", [DI]),
             ("L1_attn_A_log", [DI, NN]), ("L1_attn_D", [DI]),
             ("L1_attn_out_proj_w", [DI, H]), ("L1_attn_ln_g", [H]), ("L1_attn_ln_b", [H]),
             ("L1_attn_ff_w1", [H, 4 * H]), ("L1_attn_ff_b1", [4 * H]),
             ("L1_attn_ff_w2", [4 * H, H]), ("L1_attn_ff_b2", [H])]
    for nm, shp in names:
        dr[nm] = nc.dram_tensor(nm, shp, F32, kind="ExternalInput")
    x0 = nc.dram_tensor("x", [H, T], F32, kind="ExternalInput")
    y = nc.dram_tensor("y", [H, T], F32, kind="ExternalOutput")
    xr = [nc.dram_tensor(f"xr{i}", [H, T], F32) for i in range(5)]
    setup_consts(ctx, dr)

    moe_dr = {"gate_w": dr["L0_ffn_gate_w"], "shared_gate_w": dr["L0_ffn_shared_gate_w"],
              "shared_gate_up_w": dr["L0_ffn_shared_gate_up_w"],
              "shared_down_w": dr["L0_ffn_shared_down_w"],
              "my_egu": dr["my_egu"], "my_edw": dr["my_edw"]}
    wb = stage_moe_weights_dram(ctx, nc, moe_dr, "m0")
    sbuffs = moe_shared_buffers(nc, "m0")
    ctx.push()
    emit_moe(ctx, nc, x0, xr[0], moe_dr, wb, sbuffs, meta, 0.5)
    ctx.pop()
    ctx.push()
    rdr = {"Wk": dr["L0_attn_Wk"], "Wv": dr["L0_attn_Wv"], "Wr": dr["L0_attn_Wr"],
           "Wo": dr["L0_attn_Wo"], "time_decay": dr["L0_attn_time_decay"],
           "time_first": dr["L0_attn_time_first"], "bo": dr["L0_attn_bo"],
           "time_mix_key": dr["L0_attn_time_mix_key"],
           "time_mix_value": dr["L0_attn_time_mix_value"],
           "time_mix_receptance": dr["L0_attn_time_mix_receptance"]}
    w2 = stage_rwkv_weights(ctx, nc, rdr)
    emit_rwkv(ctx, xr[0], xr[1], w2)
    ctx.pop()
    ctx.push()
    emit_moe(ctx, nc, xr[1], xr[2], moe_dr, wb, sbuffs, meta, 0.5)
    ctx.pop()
    cmb = {}
    for nm in ("W_r", "W_k", "W_v"):
        cmb[nm] = cast_to_bf16_dram(ctx, nc, f"cm_{nm}", dr[f"L1_ffn_{nm}"], (H, H))
    ctx.push()
    w4 = {"wTr": load_wT_panels(ctx, cmb["W_r"]), "wTk": load_wT_panels(ctx, cmb["W_k"]),
          "wTv": load_wT_panels(ctx, cmb["W_v"]),
          "mur": load_percol(ctx, dr["L1_ffn_mu_r"]), "muk": load_percol(ctx, dr["L1_ffn_mu_k"])}
    emit_cm(ctx, xr[2], xr[3], w4["wTr"], w4["wTk"], w4["wTv"], w4["mur"], w4["muk"], 0.5)
    ctx.pop()
    ctx.push()
    mdr = {k[len("L1_attn_"):]: v for k, v in dr.items() if k.startswith("L1_attn_")}
    emit_mamba(ctx, nc, xr[3], xr[4], mdr)
    ctx.pop()
    ctx.push()
    w6 = {"wTr": load_wT_panels(ctx, cmb["W_r"]), "wTk": load_wT_panels(ctx, cmb["W_k"]),
          "wTv": load_wT_panels(ctx, cmb["W_v"]),
          "mur": load_percol(ctx, dr["L1_ffn_mu_r"]), "muk": load_percol(ctx, dr["L1_ffn_mu_k"])}
    emit_cm(ctx, xr[4], y, w6["wTr"], w6["wTk"], w6["wTv"], w6["mur"], w6["muk"], 0.5)
    ctx.pop()
    ctx.finish()
    return nc


def _common_in_map(c, inputs):
    m = dict(host_consts())
    m.update(host_moe_meta(c, T))
    m["my_egu"] = inputs["L0_ffn_expert_gate_up_w"][c * 8:(c + 1) * 8]
    m["my_edw"] = inputs["L0_ffn_expert_down_w"][c * 8:(c + 1) * 8]
    return m


def kernel(**inputs):
    x_full = inputs["L0_attn_x"]
    if FUSED:
        nc = build_fused_program()
        in_maps = []
        for c in range(NCORES):
            m = _common_in_map(c, inputs)
            for k, v in inputs.items():
                if k in ("L0_ffn_expert_gate_up_w", "L0_ffn_expert_down_w", "L0_attn_x"):
                    continue
                m[k] = v
            m["x"] = np.ascontiguousarray(x_full[c].T)
            in_maps.append(m)
        res = run_bass_kernel_spmd(nc, in_maps, core_ids=list(range(NCORES)))
        return np.stack([res.results[c]["y"].T for c in range(NCORES)]).astype(np.float32)

    xs = [np.ascontiguousarray(x_full[c].T) for c in range(NCORES)]
    consts = host_consts()

    def run(nc, extra):
        nonlocal xs
        in_maps = []
        for c in range(NCORES):
            m = dict(consts)
            m.update(extra(c))
            m["x"] = xs[c]
            in_maps.append(m)
        res = run_bass_kernel_spmd(nc, in_maps, core_ids=list(range(NCORES)))
        xs = [res.results[c]["y"] for c in range(NCORES)]

    moe_extra = lambda c: {
        "gate_w": inputs["L0_ffn_gate_w"], "shared_gate_w": inputs["L0_ffn_shared_gate_w"],
        "shared_gate_up_w": inputs["L0_ffn_shared_gate_up_w"],
        "shared_down_w": inputs["L0_ffn_shared_down_w"],
        "my_egu": inputs["L0_ffn_expert_gate_up_w"][c * 8:(c + 1) * 8],
        "my_edw": inputs["L0_ffn_expert_down_w"][c * 8:(c + 1) * 8],
        **host_moe_meta(c, T)}
    rwkv_extra = lambda c: {
        "Wk": inputs["L0_attn_Wk"], "Wv": inputs["L0_attn_Wv"], "Wr": inputs["L0_attn_Wr"],
        "Wo": inputs["L0_attn_Wo"], "time_decay": inputs["L0_attn_time_decay"],
        "time_first": inputs["L0_attn_time_first"], "bo": inputs["L0_attn_bo"],
        "time_mix_key": inputs["L0_attn_time_mix_key"],
        "time_mix_value": inputs["L0_attn_time_mix_value"],
        "time_mix_receptance": inputs["L0_attn_time_mix_receptance"]}
    cm_extra = lambda c: {
        "W_r": inputs["L1_ffn_W_r"], "W_k": inputs["L1_ffn_W_k"], "W_v": inputs["L1_ffn_W_v"],
        "mu_r": inputs["L1_ffn_mu_r"], "mu_k": inputs["L1_ffn_mu_k"]}
    mamba_extra = lambda c: {k[len("L1_attn_"):]: v for k, v in inputs.items()
                             if k.startswith("L1_attn_") and k != "L0_attn_x" and
                             not k.endswith("attn_x")}

    p_moe = build_moe_program()
    run(p_moe, moe_extra)
    run(build_rwkv_program(), rwkv_extra)
    run(p_moe, moe_extra)
    p_cm = build_cm_program()
    run(p_cm, cm_extra)
    run(build_mamba_program(), mamba_extra)
    run(p_cm, cm_extra)
    return np.stack([x.T for x in xs]).astype(np.float32)


# version 18 (final)
# speedup vs baseline: 1.1439x; 1.1439x over previous
import sys
sys.path.insert(0, "/opt/trn_rl_repo")
import numpy as np
import ml_dtypes
import concourse.bass as bass
import concourse.mybir as mybir
from concourse.bass_utils import run_bass_kernel_spmd
from concourse.bass_core_rust import IndirectOffsetOnAxis
from contextlib import ExitStack

F32 = mybir.dt.float32
BF16 = mybir.dt.bfloat16
I32 = mybir.dt.int32
U32 = mybir.dt.uint32
ALU = mybir.AluOpType
AF = mybir.ActivationFunctionType
AX = mybir.AxisListType

NCORES = 8
T = 4096
H = 1024
FT = H // 128
TC = 512
NCH = T // TC

ENGINES = ("pe", "act", "dve", "pool", "sp")


class Prog:
    R = 12

    def __init__(self, nc):
        self.nc = nc
        self.ops = {e: [] for e in ENGINES}
        self.res = {}
        self.out_toks = []
        self.dma_count = {}
        self.fence_toks = set()
        self.last_dma = {}
        self.cc_unfenced = []

    def fence(self):
        toks = set()
        for e in ENGINES:
            for i in range(len(self.ops[e]) - 1, -1, -1):
                if self.ops[e][i]["kind"] == "c":
                    toks.add((e, i))
                    break
        toks |= set(self.last_dma.values())
        toks |= set(self.cc_unfenced)
        self.cc_unfenced = []
        self.fence_toks = toks

    def _rec(self, handle, tag):
        d = self.res.setdefault(id(handle), {})
        return d.setdefault(tag, {"w": None, "r": []})

    def _tags(self, handle, tag):
        d = self.res.setdefault(id(handle), {})
        if tag is None:
            return list(d.keys())
        out = [tag]
        if None in d:
            out.append(None)
        return out

    def op(self, eng, emit, reads=(), writes=(), inc=1, is_out=False):
        def norm(x):
            return x if isinstance(x, tuple) else (x, None)
        waits = set()
        for r in reads:
            h, t = norm(r)
            for tt in self._tags(h, t):
                rec = self.res[id(h)].get(tt)
                if rec and rec["w"] is not None:
                    waits.add(rec["w"])
        for w in writes:
            h, t = norm(w)
            for tt in self._tags(h, t):
                rec = self.res[id(h)].get(tt)
                if rec:
                    if rec["w"] is not None:
                        waits.add(rec["w"])
                    waits.update(rec["r"])
        waits |= self.fence_toks
        waits = {tk for tk in waits if not (eng == "pe" and tk[0] == "pe")}
        idx = len(self.ops[eng])
        tok = (eng, idx)
        kind = "dma" if inc == 16 else ("cc" if inc == -1 else "c")
        d = None
        if kind == "dma":
            d = self.dma_count.get(eng, 0)
            self.dma_count[eng] = d + 1
            self.last_dma[(eng, d % self.R)] = tok
        elif kind == "cc":
            self.cc_unfenced.append(tok)
        self.ops[eng].append({"emit": emit, "waits": waits, "inc": inc,
                              "need": kind != "c", "kind": kind, "d": d})
        for r in reads:
            h, t = norm(r)
            self._rec(h, t)["r"].append(tok)
        for w in writes:
            h, t = norm(w)
            rec = self._rec(h, t)
            rec["w"] = tok
            rec["r"] = []
        if is_out:
            self.out_toks.append(tok)
        return tok

    def build(self, block, sems, dsems, ccsems):
        needed = set(self.out_toks)
        for e in ENGINES:
            for o in self.ops[e]:
                needed |= o["waits"]
        for (e, i) in needed:
            self.ops[e][i]["need"] = True
        cnt = {}
        for e in ENGINES:
            c = 0
            cl = []
            for o in self.ops[e]:
                if o["kind"] == "c" and o["need"]:
                    c += o["inc"]
                cl.append(c)
            cnt[e] = cl
        cc_of = {}
        ci = 0
        for e in ENGINES:
            for i, o in enumerate(self.ops[e]):
                if o["kind"] == "cc":
                    cc_of[(e, i)] = ccsems[ci]
                    ci += 1
        R = self.R

        def wait_target(tok):
            we, wi = tok
            o = self.ops[we][wi]
            if o["kind"] == "dma":
                d = o["d"]
                return (dsems[we][d % R], 16 * (d // R + 1))
            if o["kind"] == "cc":
                return (cc_of[tok], 1)
            return (sems[we], cnt[we][wi])

        def make_body(eng):
            def body(E):
                covered = {}
                for o in self.ops[eng]:
                    for tok in sorted(o["waits"]):
                        sem, val = wait_target(tok)
                        if covered.get(id(sem), -1) < val:
                            E.wait_ge(sem, val)
                            covered[id(sem)] = val
                    if o["kind"] == "dma" and o["d"] >= R:
                        sem = dsems[eng][o["d"] % R]
                        val = 16 * (o["d"] // R)
                        if covered.get(id(sem), -1) < val:
                            E.wait_ge(sem, val)
                            covered[id(sem)] = val
                    inst = o["emit"](E)
                    if o["kind"] == "dma":
                        inst.then_inc(dsems[eng][o["d"] % R], 16)
                    elif o["kind"] == "cc":
                        inst.then_inc(cc_of_by_inst[id(o)], 1)
                    elif o["need"]:
                        inst.then_inc(sems[eng], o["inc"])
                if eng == "pool":
                    for tok in sorted(set(self.out_toks)):
                        sem, val = wait_target(tok)
                        if covered.get(id(sem), -1) < val:
                            E.wait_ge(sem, val)
                            covered[id(sem)] = val
            return body

        cc_of_by_inst = {id(self.ops[e][i]): s for (e, i), s in cc_of.items()}
        block.tensor(make_body("pe"))
        block.scalar(make_body("act"))
        block.vector(make_body("dve"))
        block.gpsimd(make_body("pool"))
        block.sync(make_body("sp"))

    def n_cc(self):
        return sum(1 for e in ENGINES for o in self.ops[e] if o["kind"] == "cc")


class Ctx:
    def __init__(self, nc, es):
        self.nc = nc
        self.es = es
        self.pr = Prog(nc)
        self._n = 0
        self.psum = [es.enter_context(nc.psum_tensor(f"pb{i}", [128, 512], F32))
                     for i in range(6)]
        self.psum_bf2 = [es.enter_context(nc.psum_tensor(f"pbf{i}", [128, 1024], BF16))
                         for i in range(2)]
        self.psum_bf = self.psum_bf2[0]
        self._pool = {}
        self.consts = {}
        self._ccols = {}
        self._scopes = []
        self._bregs = {}

    def breg(self, E, val):
        key = (id(E), val)
        r = self._bregs.get(key)
        if r is None:
            r = E.to_reg(val)
            self._bregs[key] = r
        return r

    def constcol(self, val, dtype=F32):
        key = (float(val), dtype)
        t = self._ccols.get(key)
        if t is None:
            assert not self._scopes, f"constcol({val}) first used inside a phase scope; pre-create it"
            self._n += 1
            t = self.es.enter_context(self.nc.sbuf_tensor(f"t{self._n}", [128, 1], dtype))
            self.pr.op("dve", lambda E: E.memset(t[:], val), [], [t])
            self._ccols[key] = t
        return t

    def push(self):
        self._scopes.append((ExitStack(), self._pool))
        self._pool = {}

    def pop(self):
        es, pool = self._scopes.pop()
        es.close()
        self._pool = pool
        self.pr.fence()

    def sbuf(self, shape, dtype):
        self._n += 1
        es = self._scopes[-1][0] if self._scopes else self.es
        return es.enter_context(self.nc.sbuf_tensor(f"t{self._n}", list(shape), dtype))

    def tile(self, shape, dtype):
        key = (tuple(shape), dtype)
        lst = self._pool.get(key)
        if lst:
            return lst.pop()
        t = self.sbuf(shape, dtype)
        self.pr.op("dve", lambda E: E.memset(t[:], 0.0), [], [t])
        return t

    def free(self, *hs):
        for h in hs:
            self._pool.setdefault((tuple(h.shape), h.dtype), []).append(h)

    def finish(self):
        es2 = ExitStack()
        sems = {e: es2.enter_context(self.nc.semaphore(f"s_{e}")) for e in ENGINES}
        dsems = {e: [es2.enter_context(self.nc.semaphore(f"d_{e}{i}"))
                     for i in range(Prog.R)]
                 for e in ENGINES if self.pr.dma_count.get(e)}
        ccsems = [es2.enter_context(self.nc.semaphore(f"cc{i}"))
                  for i in range(self.pr.n_cc())]
        block = es2.enter_context(self.nc.Block())
        self.pr.build(block, sems, dsems, ccsems)
        es2.close()
        self.es.close()
        return self.nc


def new_ctx():
    nc = bass.Bass("TRN2", target_bir_lowering=False)
    es = ExitStack()
    return Ctx(nc, es)


def dma(ctx, eng, out, in_, reads, writes, is_out=False, **kw):
    return ctx.pr.op(eng, lambda E: E.dma_start(out=out, in_=in_, **kw),
                     reads=reads, writes=writes, inc=16, is_out=is_out)


def op_dve(ctx, emit, reads, writes):
    return ctx.pr.op("dve", emit, reads, writes)


def op_act(ctx, emit, reads, writes):
    return ctx.pr.op("act", emit, reads, writes)


def op_pe(ctx, emit, reads, writes):
    return ctx.pr.op("pe", emit, reads, writes)


def op_pool(ctx, emit, reads, writes, inc=1, is_out=False):
    return ctx.pr.op("pool", emit, reads, writes, inc=inc, is_out=is_out)


def decl_const_inputs(nc):
    return {
        "konst_f32": nc.dram_tensor("konst_f32", [128, 128], F32, kind="ExternalInput"),
        "konst_bf16": nc.dram_tensor("konst_bf16", [128, 2], BF16, kind="ExternalInput"),
        "ident_bf16": nc.dram_tensor("ident_bf16", [128, 128], BF16, kind="ExternalInput"),
        "ident_f32": nc.dram_tensor("ident_f32", [128, 128], F32, kind="ExternalInput"),
    }


def host_consts():
    kf = np.zeros((128, 128), np.float32)
    kf[:, 0] = 1.0
    kf[0, :] = 1.0
    kb = np.zeros((128, 2), ml_dtypes.bfloat16)
    kb[:, 0] = 1.0
    return {"konst_f32": kf, "konst_bf16": kb,
            "ident_bf16": np.eye(128, dtype=ml_dtypes.bfloat16),
            "ident_f32": np.eye(128, dtype=np.float32)}


def setup_consts(ctx, dr):
    c = {}
    c["ones_col_bf"] = ctx.sbuf((128, 1), BF16)
    c["ones_col_f"] = ctx.sbuf((128, 1), F32)
    c["ones_row_f"] = ctx.sbuf((1, 128), F32)
    dma(ctx, "sp", c["ones_col_bf"][:], dr["konst_bf16"].ap()[:, 0:1],
        reads=[dr["konst_bf16"]], writes=[c["ones_col_bf"]],
        allow_slow_non_contiguous=True)
    dma(ctx, "sp", c["ones_col_f"][:], dr["konst_f32"].ap()[:, 0:1],
        reads=[dr["konst_f32"]], writes=[c["ones_col_f"]],
        allow_slow_non_contiguous=True)
    dma(ctx, "sp", c["ones_row_f"][:], dr["konst_f32"].ap()[0:1, 0:128],
        reads=[dr["konst_f32"]], writes=[c["ones_row_f"]])
    c["ident_bf"] = ctx.sbuf((128, 128), BF16)
    c["ident_f"] = ctx.sbuf((128, 128), F32)
    dma(ctx, "sp", c["ident_bf"][:], dr["ident_bf16"].ap(),
        reads=[dr["ident_bf16"]], writes=[c["ident_bf"]])
    dma(ctx, "sp", c["ident_f"][:], dr["ident_f32"].ap(),
        reads=[dr["ident_f32"]], writes=[c["ident_f"]])
    ctx.consts = c
    for v in (-1.0, -0.5, 1e-5, 1.0, 0.0):
        ctx.constcol(v)
    return c


def cast_to_bf16_dram(ctx, nc, name, src_handle, shape):
    dst = nc.dram_tensor(name, list(shape), BF16)
    dma(ctx, "pool", dst.ap(), src_handle.ap(), reads=[src_handle], writes=[dst])
    return dst


def load_wT_panels(ctx, wb, n_in=H, n_out=H):
    panels = []
    for k in range(n_in // 128):
        t = ctx.sbuf((128, n_out), BF16)
        dma(ctx, "sp", t[0:128, 0:n_out], wb.ap()[0:n_out, 128 * k:128 * (k + 1)],
            reads=[wb], writes=[t], transpose=True)
        panels.append(t)
    return panels


def load_w_direct_panels(ctx, wb, n_in, n_out):
    panels = []
    for k in range(n_in // 128):
        t = ctx.sbuf((128, n_out), BF16)
        dma(ctx, "sp", t[0:128, 0:n_out], wb.ap()[128 * k:128 * (k + 1), 0:n_out],
            reads=[wb], writes=[t])
        panels.append(t)
    return panels


def load_percol(ctx, vec_handle, n=H):
    t = ctx.sbuf((128, n // 128), F32)
    src = bass.AP(vec_handle, 0, [[1, 128], [128, n // 128]])
    dma(ctx, "sp", t[:], src, reads=[vec_handle], writes=[t],
        allow_slow_non_contiguous=True)
    return t


def ln_chunk(ctx, x_tiles, Tc, bmu_bank=4, brs_bank=5):
    c = ctx.consts
    p_mu = ctx.psum[bmu_bank]
    p_sq = ctx.psum[brs_bank]
    sq = [ctx.tile((128, Tc), BF16) for _ in range(FT)]
    for k in range(FT):
        op_act(ctx, lambda E, k=k: E.activation(sq[k][:, 0:Tc], x_tiles[k][:, 0:Tc], AF.Square),
               [x_tiles[k]], [sq[k]])
    for k in range(FT):
        op_pe(ctx, lambda E, k=k: E.matmul(p_mu[0:1, 0:Tc], c["ones_col_f"][:],
                                           x_tiles[k][:, 0:Tc], start=(k == 0), stop=(k == FT - 1)),
              [x_tiles[k], c["ones_col_f"]], [p_mu])
    for k in range(FT):
        op_pe(ctx, lambda E, k=k: E.matmul(p_sq[0:1, 0:Tc], c["ones_col_bf"][:],
                                           sq[k][:, 0:Tc], start=(k == 0), stop=(k == FT - 1)),
              [sq[k], c["ones_col_bf"]], [p_sq])
    mu = ctx.tile((1, 512), F32)
    var = ctx.tile((1, 512), F32)
    op_dve(ctx, lambda E: E.tensor_scalar_mul(mu[0:1, 0:Tc], p_mu[0:1, 0:Tc], 1.0 / H),
           [p_mu], [mu])
    op_dve(ctx, lambda E: E.tensor_scalar_mul(var[0:1, 0:Tc], p_sq[0:1, 0:Tc], 1.0 / H),
           [p_sq], [var])
    negmusq = ctx.tile((1, 512), F32)
    op_dve(ctx, lambda E: E.scalar_tensor_tensor(negmusq[0:1, 0:Tc], mu[0:1, 0:Tc], -1.0,
                                                 mu[0:1, 0:Tc], ALU.mult, ALU.mult),
           [mu], [negmusq])
    op_dve(ctx, lambda E: E.tensor_add(var[0:1, 0:Tc], var[0:1, 0:Tc], negmusq[0:1, 0:Tc]),
           [var, negmusq], [var])
    rstd = ctx.tile((1, 512), F32)
    epsc = ctx.constcol(1e-5)
    mh = ctx.constcol(-0.5)
    op_act(ctx, lambda E: E.activation(rstd[0:1, 0:Tc], var[0:1, 0:Tc], AF.Ln,
                                       bias=epsc[0:1, 0:1]),
           [var, epsc], [rstd])
    op_act(ctx, lambda E: E.activation(rstd[0:1, 0:Tc], rstd[0:1, 0:Tc], AF.Exp,
                                       scale=mh[0:1, 0:1]),
           [rstd, mh], [rstd])
    op_pe(ctx, lambda E: E.matmul(p_mu[0:128, 0:Tc], c["ones_row_f"][0:1, 0:128],
                                  mu[0:1, 0:Tc], start=True, stop=True),
          [mu, c["ones_row_f"]], [p_mu])
    op_pe(ctx, lambda E: E.matmul(p_sq[0:128, 0:Tc], c["ones_row_f"][0:1, 0:128],
                                  rstd[0:1, 0:Tc], start=True, stop=True),
          [rstd, c["ones_row_f"]], [p_sq])
    xn = [ctx.tile((128, Tc), F32) for _ in range(FT)]
    for k in range(FT):
        op_dve(ctx, lambda E, k=k: E.tensor_sub(xn[k][:, 0:Tc], x_tiles[k][:, 0:Tc],
                                                p_mu[0:128, 0:Tc]),
               [x_tiles[k], p_mu], [xn[k]])
        op_dve(ctx, lambda E, k=k: E.tensor_mul(xn[k][:, 0:Tc], xn[k][:, 0:Tc],
                                                p_sq[0:128, 0:Tc]),
               [xn[k], p_sq], [xn[k]])
    ctx.free(mu, var, negmusq, rstd, *sq)
    return xn


def emit_cm(ctx, xin, xout, wTr, wTk, wTv, mur_t, muk_t, scale=0.5):
    carry = [ctx.sbuf((128, 1), F32) for _ in range(FT)]
    for k in range(FT):
        op_dve(ctx, lambda E, k=k: E.memset(carry[k][:], 0.0), [], [carry[k]])
    for ch in range(NCH):
        c0 = ch * TC
        xt = [ctx.tile((128, TC), F32) for _ in range(FT)]
        for k in range(FT):
            dma(ctx, "sp", xt[k][:], xin.ap()[128 * k:128 * (k + 1), c0:c0 + TC],
                reads=[(xin, ("c", ch))], writes=[xt[k]])
        xn = ln_chunk(ctx, xt, TC)
        xr = [ctx.tile((128, TC), BF16) for _ in range(FT)]
        xk = [ctx.tile((128, TC), BF16) for _ in range(FT)]
        xp = [ctx.tile((128, TC), F32) for _ in range(FT)]
        df = [ctx.tile((128, TC), F32) for _ in range(FT)]
        for k in range(FT):
            op_act(ctx, lambda E, a=xp[k], b=xn[k]: E.activation(a[:, 1:TC], b[:, 0:TC - 1], AF.Copy),
                   [xn[k]], [xp[k]])
            op_dve(ctx, lambda E, a=xp[k], b=carry[k]: E.tensor_copy(a[:, 0:1], b[:]),
                   [carry[k]], [xp[k]])
            op_dve(ctx, lambda E, a=carry[k], b=xn[k]: E.tensor_copy(a[:], b[:, TC - 1:TC]),
                   [xn[k]], [carry[k]])
            op_dve(ctx, lambda E, a=df[k], b=xn[k], c=xp[k]: E.tensor_sub(a[:], b[:], c[:]),
                   [xn[k], xp[k]], [df[k]])
            op_dve(ctx, lambda E, a=xr[k], b=df[k], m=mur_t, k=k, c=xp[k]: E.scalar_tensor_tensor(
                a[:], b[:], m[:, k:k + 1], c[:], ALU.mult, ALU.add),
                [df[k], mur_t, xp[k]], [xr[k]])
            op_dve(ctx, lambda E, a=xk[k], b=df[k], m=muk_t, k=k, c=xp[k]: E.scalar_tensor_tensor(
                a[:], b[:], m[:, k:k + 1], c[:], ALU.mult, ALU.add),
                [df[k], muk_t, xp[k]], [xk[k]])
        ctx.free(*xn, *xp, *df)
        r = [ctx.tile((128, TC), F32) for _ in range(FT)]
        for m in range(FT):
            pb = ctx.psum[m % 4]
            for k in range(FT):
                op_pe(ctx, lambda E, m=m, k=k, pb=pb, w=wTr[k], a=xr[k]: E.matmul(
                    pb[0:128, 0:TC], w[:, 128 * m:128 * (m + 1)], a[:, 0:TC],
                    start=(k == 0), stop=(k == FT - 1)),
                    [wTr[k], xr[k]], [pb])
            op_act(ctx, lambda E, pb=pb, a=r[m]: E.activation(a[:, 0:TC], pb[0:128, 0:TC], AF.Sigmoid),
                   [pb], [r[m]])
        ctx.free(*xr)
        kact = [ctx.tile((128, TC), BF16) for _ in range(FT)]
        ktmp = ctx.tile((128, TC), F32)
        for m in range(FT):
            pb = ctx.psum[m % 4]
            for k in range(FT):
                op_pe(ctx, lambda E, m=m, k=k, pb=pb, w=wTk[k], a=xk[k]: E.matmul(
                    pb[0:128, 0:TC], w[:, 128 * m:128 * (m + 1)], a[:, 0:TC],
                    start=(k == 0), stop=(k == FT - 1)),
                    [wTk[k], xk[k]], [pb])
            op_act(ctx, lambda E, pb=pb, a=ktmp: E.activation(a[:, 0:TC], pb[0:128, 0:TC], AF.Relu),
                   [pb], [ktmp])
            op_act(ctx, lambda E, a=kact[m], b=ktmp: E.activation(a[:, 0:TC], b[:, 0:TC], AF.Square),
                   [ktmp], [kact[m]])
        ctx.free(ktmp, *xk)
        for m in range(FT):
            pb = ctx.psum[m % 4]
            for k in range(FT):
                op_pe(ctx, lambda E, m=m, k=k, pb=pb, w=wTv[k], a=kact[k]: E.matmul(
                    pb[0:128, 0:TC], w[:, 128 * m:128 * (m + 1)], a[:, 0:TC],
                    start=(k == 0), stop=(k == FT - 1)),
                    [wTv[k], kact[k]], [pb])
            op_dve(ctx, lambda E, pb=pb, a=r[m]: E.tensor_mul(a[:, 0:TC], a[:, 0:TC], pb[0:128, 0:TC]),
                   [pb, r[m]], [r[m]])
            op_dve(ctx, lambda E, a=xt[m], b=r[m]: E.scalar_tensor_tensor(
                a[:, 0:TC], b[:, 0:TC], scale, a[:, 0:TC], ALU.mult, ALU.add),
                [r[m], xt[m]], [xt[m]])
            dma(ctx, "pool", xout.ap()[128 * m:128 * (m + 1), c0:c0 + TC], xt[m][:, 0:TC],
                reads=[xt[m]], writes=[(xout, ("c", ch))], is_out=True)
        ctx.free(*kact, *r, *xt)


def stage_cm_weights(ctx, nc, dr, pfx):
    wrb = cast_to_bf16_dram(ctx, nc, f"{pfx}_wrb", dr[f"{pfx}_W_r"], (H, H))
    wkb = cast_to_bf16_dram(ctx, nc, f"{pfx}_wkb", dr[f"{pfx}_W_k"], (H, H))
    wvb = cast_to_bf16_dram(ctx, nc, f"{pfx}_wvb", dr[f"{pfx}_W_v"], (H, H))
    return {
        "wTr": load_wT_panels(ctx, wrb),
        "wTk": load_wT_panels(ctx, wkb),
        "wTv": load_wT_panels(ctx, wvb),
        "mur": load_percol(ctx, dr[f"{pfx}_mu_r"]),
        "muk": load_percol(ctx, dr[f"{pfx}_mu_k"]),
    }


def emit_rwkv(ctx, xin, xout, w):
    ncar = [ctx.sbuf((128, 1), F32) for _ in range(FT)]
    dcar = [ctx.sbuf((128, 1), F32) for _ in range(FT)]
    carry = [ctx.sbuf((128, 1), F32) for _ in range(FT)]
    for k in range(FT):
        op_dve(ctx, lambda E, a=ncar[k]: E.memset(a[:], 0.0), [], [ncar[k]])
        op_dve(ctx, lambda E, a=dcar[k]: E.memset(a[:], 0.0), [], [dcar[k]])
        op_dve(ctx, lambda E, a=carry[k]: E.memset(a[:], 0.0), [], [carry[k]])
    for ch in range(NCH):
        c0 = ch * TC
        xt = [ctx.tile((128, TC), F32) for _ in range(FT)]
        for k in range(FT):
            dma(ctx, "sp", xt[k][:], xin.ap()[128 * k:128 * (k + 1), c0:c0 + TC],
                reads=[(xin, ("c", ch))], writes=[xt[k]])
        xn = ln_chunk(ctx, xt, TC)
        ki = [ctx.tile((128, TC), BF16) for _ in range(FT)]
        vi = [ctx.tile((128, TC), BF16) for _ in range(FT)]
        ri = [ctx.tile((128, TC), BF16) for _ in range(FT)]
        xp = [ctx.tile((128, TC), F32) for _ in range(FT)]
        df = [ctx.tile((128, TC), F32) for _ in range(FT)]
        for k in range(FT):
            op_act(ctx, lambda E, a=xp[k], b=xn[k]: E.activation(a[:, 1:TC], b[:, 0:TC - 1], AF.Copy),
                   [xn[k]], [xp[k]])
            op_dve(ctx, lambda E, a=xp[k], b=carry[k]: E.tensor_copy(a[:, 0:1], b[:]),
                   [carry[k]], [xp[k]])
            op_dve(ctx, lambda E, a=carry[k], b=xn[k]: E.tensor_copy(a[:], b[:, TC - 1:TC]),
                   [xn[k]], [carry[k]])
            op_dve(ctx, lambda E, a=df[k], b=xn[k], c=xp[k]: E.tensor_sub(a[:], b[:], c[:]),
                   [xn[k], xp[k]], [df[k]])
            for dst, mix in ((ki, w["tmk"]), (vi, w["tmv"]), (ri, w["tmr"])):
                op_dve(ctx, lambda E, a=dst[k], b=df[k], m=mix, k=k, c=xp[k]:
                       E.scalar_tensor_tensor(a[:], b[:], m[:, k:k + 1], c[:],
                                              ALU.mult, ALU.add),
                       [df[k], mix, xp[k]], [dst[k]])
        ctx.free(*xn, *xp, *df)
        kv = [ctx.tile((128, TC), F32) for _ in range(FT)]
        vv = [ctx.tile((128, TC), F32) for _ in range(FT)]
        rv = [ctx.tile((128, TC), F32) for _ in range(FT)]
        for (dst, src, pans, act) in ((kv, ki, w["wTk"], None), (vv, vi, w["wTv"], None),
                                      (rv, ri, w["wTr"], AF.Sigmoid)):
            for m in range(FT):
                pb = ctx.psum[m % 4]
                for k in range(FT):
                    op_pe(ctx, lambda E, m=m, k=k, pb=pb, wp=pans[k], a=src[k]: E.matmul(
                        pb[0:128, 0:TC], wp[:, 128 * m:128 * (m + 1)], a[:, 0:TC],
                        start=(k == 0), stop=(k == FT - 1)),
                        [pans[k], src[k]], [pb])
                if act is None:
                    op_act(ctx, lambda E, pb=pb, a=dst[m]: E.activation(a[:, 0:TC], pb[0:128, 0:TC], AF.Copy),
                           [pb], [dst[m]])
                else:
                    op_act(ctx, lambda E, pb=pb, a=dst[m]: E.activation(a[:, 0:TC], pb[0:128, 0:TC], act),
                           [pb], [dst[m]])
        ctx.free(*ki, *vi, *ri)
        rwkv = [ctx.tile((128, TC), BF16) for _ in range(FT)]
        ek2 = [ctx.tile((128, TC), F32) for _ in range(2)]
        ee2 = [ctx.tile((128, TC), F32)]
        ekv = ctx.tile((128, TC), F32)
        num = [ctx.tile((128, TC + 1), F32) for _ in range(FT)]
        den = [ctx.tile((128, TC + 1), F32) for _ in range(FT)]
        an = ctx.tile((128, TC), F32)
        ad = ctx.tile((128, TC), F32)
        for m in range(FT):
            ek, ee = ek2[m % 2], ee2[0]
            op_act(ctx, lambda E, a=ek, b=kv[m]: E.activation(a[:, 0:TC], b[:, 0:TC], AF.Exp),
                   [kv[m]], [ek])
            op_act(ctx, lambda E, a=ee, b=kv[m], tf=w["tf"], m=m:
                   E.activation(a[:, 0:TC], b[:, 0:TC], AF.Exp, bias=tf[:, m:m + 1]),
                   [kv[m], w["tf"]], [ee])
            op_dve(ctx, lambda E, a=ekv, b=ek, c=vv[m]: E.tensor_mul(a[:, 0:TC], b[:, 0:TC], c[:, 0:TC]),
                   [ek, vv[m]], [ekv])
            op_dve(ctx, lambda E, a=num[m], b=ncar[m]: E.tensor_copy(a[:, 0:1], b[:]),
                   [ncar[m]], [num[m]])
            op_dve(ctx, lambda E, a=den[m], b=dcar[m]: E.tensor_copy(a[:, 0:1], b[:]),
                   [dcar[m]], [den[m]])
            op_dve(ctx, lambda E, a=num[m], td=w["td"], m=m, b=ekv: E.tensor_tensor_scan(
                a[:, 1:TC + 1], td[:, m:m + 1].to_broadcast([128, TC]), b[:, 0:TC],
                a[:, 0:1], ALU.mult, ALU.add),
                [ekv, w["td"], num[m]], [num[m]])
            op_dve(ctx, lambda E, a=den[m], td=w["td"], m=m, b=ek: E.tensor_tensor_scan(
                a[:, 1:TC + 1], td[:, m:m + 1].to_broadcast([128, TC]), b[:, 0:TC],
                a[:, 0:1], ALU.mult, ALU.add),
                [ek, w["td"], den[m]], [den[m]])
            op_dve(ctx, lambda E, a=ncar[m], b=num[m]: E.tensor_copy(a[:], b[:, TC:TC + 1]),
                   [num[m]], [ncar[m]])
            op_dve(ctx, lambda E, a=dcar[m], b=den[m]: E.tensor_copy(a[:], b[:, TC:TC + 1]),
                   [den[m]], [dcar[m]])
            op_dve(ctx, lambda E, a=an, b=ee, c=vv[m]: E.tensor_mul(a[:, 0:TC], b[:, 0:TC], c[:, 0:TC]),
                   [ee, vv[m]], [an])
            op_dve(ctx, lambda E, a=an, b=num[m]: E.tensor_add(a[:, 0:TC], a[:, 0:TC], b[:, 0:TC]),
                   [an, num[m]], [an])
            op_dve(ctx, lambda E, a=ad, b=ee, c=den[m]: E.tensor_add(a[:, 0:TC], b[:, 0:TC], c[:, 0:TC]),
                   [ee, den[m]], [ad])
            op_dve(ctx, lambda E, a=ad: E.reciprocal(a[:, 0:TC], a[:, 0:TC]),
                   [ad], [ad])
            op_dve(ctx, lambda E, a=an, b=ad: E.tensor_mul(a[:, 0:TC], a[:, 0:TC], b[:, 0:TC]),
                   [an, ad], [an])
            op_dve(ctx, lambda E, a=rwkv[m], b=an, c=rv[m]: E.tensor_mul(a[:, 0:TC], b[:, 0:TC], c[:, 0:TC]),
                   [an, rv[m]], [rwkv[m]])
        ctx.free(*ek2, *ee2, ekv, an, ad, *kv, *vv, *rv, *num, *den)
        for m in range(FT):
            pb = ctx.psum[m % 4]
            for k in range(FT):
                op_pe(ctx, lambda E, m=m, k=k, pb=pb, wp=w["wTo"][k], a=rwkv[k]: E.matmul(
                    pb[0:128, 0:TC], wp[:, 128 * m:128 * (m + 1)], a[:, 0:TC],
                    start=(k == 0), stop=(k == FT - 1)),
                    [w["wTo"][k], rwkv[k]], [pb])
            op_dve(ctx, lambda E, pb=pb, a=xt[m], bo=w["bo"], m=m: E.scalar_tensor_tensor(
                a[:, 0:TC], pb[0:128, 0:TC], bo[:, m:m + 1], a[:, 0:TC], ALU.add, ALU.add),
                [pb, w["bo"], xt[m]], [xt[m]])
            dma(ctx, "pool", xout.ap()[128 * m:128 * (m + 1), c0:c0 + TC], xt[m][:, 0:TC],
                reads=[xt[m]], writes=[(xout, ("c", ch))], is_out=True)
        ctx.free(*rwkv, *xt)


def stage_rwkv_weights(ctx, nc, dr):
    wkb = cast_to_bf16_dram(ctx, nc, "rk_wkb", dr["Wk"], (H, H))
    wvb = cast_to_bf16_dram(ctx, nc, "rk_wvb", dr["Wv"], (H, H))
    wrb = cast_to_bf16_dram(ctx, nc, "rk_wrb", dr["Wr"], (H, H))
    wob = cast_to_bf16_dram(ctx, nc, "rk_wob", dr["Wo"], (H, H))
    w = {
        "wTk": load_wT_panels(ctx, wkb),
        "wTv": load_wT_panels(ctx, wvb),
        "wTr": load_wT_panels(ctx, wrb),
        "wTo": load_wT_panels(ctx, wob),
        "tmk": load_percol(ctx, dr["time_mix_key"]),
        "tmv": load_percol(ctx, dr["time_mix_value"]),
        "tmr": load_percol(ctx, dr["time_mix_receptance"]),
        "tf": load_percol(ctx, dr["time_first"]),
        "bo": load_percol(ctx, dr["bo"]),
    }
    tdraw = load_percol(ctx, dr["time_decay"])
    td = ctx.sbuf((128, FT), F32)
    m1 = ctx.constcol(-1.0)
    op_act(ctx, lambda E: E.activation(td[:], tdraw[:], AF.Exp), [tdraw], [td])
    op_act(ctx, lambda E: E.activation(td[:], td[:], AF.Exp, scale=m1[:, 0:1]),
           [td, m1], [td])
    w["td"] = td
    return w


DI = 2048
DT2 = DI // 128
NN = 16
DTR = 64
TCB = 256
NCHB = None


def load_percol2(ctx, vec_handle, d, k):
    nt = d // 128
    t = ctx.sbuf((128, nt * k), F32)
    src = bass.AP(vec_handle, 0, [[k, 128], [128 * k, nt], [1, k]])
    dst = bass.AP(t, 0, [[nt * k, 128], [k, nt], [1, k]])
    dma(ctx, "sp", dst, src, reads=[vec_handle], writes=[t])
    return t


def emit_mamba(ctx, nc, xin, xout, dr, pfx="mb"):
    nchb = T // TCB
    xcpre = nc.dram_tensor(f"{pfx}_xcpre", [DI, T], F32)
    z_scr = nc.dram_tensor(f"{pfx}_z", [DI, T], F32)
    yf_scr = nc.dram_tensor(f"{pfx}_yf", [DI, T], BF16)
    ys_scr = nc.dram_tensor(f"{pfx}_ys", [DI, T], BF16)
    bc_scr = nc.dram_tensor(f"{pfx}_bc", [64, TCB], BF16)
    mon_scr = nc.dram_tensor(f"{pfx}_mon", [H, T], BF16)
    one = ctx.constcol(1.0)
    m1 = ctx.constcol(-1.0)

    cw = load_percol2(ctx, dr["conv_w"], DI, 4)
    cb = load_percol(ctx, dr["conv_b"], DI)
    dpb = load_percol(ctx, dr["dt_proj_b"], DI)
    Dc = load_percol(ctx, dr["D"], DI)
    lng = load_percol(ctx, dr["ln_g"], H)
    lnb = load_percol(ctx, dr["ln_b"], H)
    fb1 = load_percol(ctx, dr["ff_b1"], 4 * H)
    fb2 = load_percol(ctx, dr["ff_b2"], H)
    a_neg = load_percol2(ctx, dr["A_log"], DI, NN)
    op_act(ctx, lambda E: E.activation(a_neg[:], a_neg[:], AF.Exp), [a_neg], [a_neg])
    op_dve(ctx, lambda E: E.tensor_scalar_mul(a_neg[:], a_neg[:], -1.0), [a_neg], [a_neg])
    carr = [ctx.sbuf((128, 2 * NN), F32) for _ in range(DT2)]
    for dt in range(DT2):
        op_dve(ctx, lambda E, a=carr[dt]: E.memset(a[:], 0.0), [], [carr[dt]])

    ctx.push()
    ipb = cast_to_bf16_dram(ctx, nc, f"{pfx}_ipb", dr["in_proj_w"], (H, 2 * DI))
    ipp = load_w_direct_panels(ctx, ipb, H, 2 * DI)
    for ch in range(NCH):
        c0 = ch * TC
        xt = [ctx.tile((128, TC), F32) for _ in range(FT)]
        for k in range(FT):
            dma(ctx, "sp", xt[k][:], xin.ap()[128 * k:128 * (k + 1), c0:c0 + TC],
                reads=[(xin, ("c", ch))], writes=[xt[k]])
        xn = ln_chunk(ctx, xt, TC)
        xb = [ctx.tile((128, TC), BF16) for _ in range(FT)]
        for k in range(FT):
            op_act(ctx, lambda E, a=xb[k], b=xn[k]: E.activation(a[:, 0:TC], b[:, 0:TC], AF.Copy),
                   [xn[k]], [xb[k]])
        ev = [ctx.tile((128, TC), F32) for _ in range(4)]
        for m in range(32):
            pb = ctx.psum[m % 4]
            for k in range(FT):
                op_pe(ctx, lambda E, m=m, k=k, pb=pb, w=ipp[k], a=xb[k]: E.matmul(
                    pb[0:128, 0:TC], w[:, 128 * m:128 * (m + 1)], a[:, 0:TC],
                    start=(k == 0), stop=(k == FT - 1)),
                    [ipp[k], xb[k]], [pb])
            e = ev[m % 4]
            op_act(ctx, lambda E, pb=pb, a=e: E.activation(a[:, 0:TC], pb[0:128, 0:TC], AF.Copy),
                   [pb], [e])
            tgt, row = (xcpre, m) if m < DT2 else (z_scr, m - DT2)
            dma(ctx, "pool", tgt.ap()[128 * row:128 * (row + 1), c0:c0 + TC], e[:, 0:TC],
                reads=[e], writes=[(tgt, ("c", ch, row))])
        ctx.free(*xt, *xn, *xb, *ev)
    ctx.pop()

    ctx.push()
    xpb = cast_to_bf16_dram(ctx, nc, f"{pfx}_xpb", dr["x_proj_w"], (DI, 96))
    xpp = load_w_direct_panels(ctx, xpb, DI, 96)
    dtb_w = cast_to_bf16_dram(ctx, nc, f"{pfx}_dtb", dr["dt_proj_w"], (DTR, DI))
    dtp = ctx.sbuf((DTR, DI), BF16)
    dma(ctx, "sp", dtp[0:DTR, 0:DI], dtb_w.ap()[0:DTR, 0:DI], reads=[dtb_w], writes=[dtp])

    HW = 16
    xcp4 = [ctx.sbuf((128, TCB + HW), F32) for _ in range(4)]
    raw4 = [ctx.sbuf((128, TCB + HW), F32) for _ in range(4)]
    ta8 = [[ctx.sbuf((128, TCB), F32) for _ in range(4)] for _ in range(2)]
    for b in xcp4 + raw4 + ta8[0] + ta8[1]:
        op_dve(ctx, lambda E, a=b: E.memset(a[:, 0:HW], 0.0), [], [a for a in ()] or [b])
    for dirn in range(2):
        for ch in range(nchb):
            ol = (T - (ch + 1) * TCB) if dirn else (ch * TCB)
            xc = [ctx.tile((128, TCB), F32) for _ in range(DT2)]
            xcb = [ctx.tile((128, TCB), BF16) for _ in range(DT2)]
            for dt in range(DT2):
                xcp = xcp4[dt % 4]
                if ch == 0:
                    op_dve(ctx, lambda E, a=xcp: E.memset(a[:, 0:HW], 0.0), [], [(xcp, "h")])
                    if dirn == 0:
                        dma(ctx, "sp", xcp[:, HW:HW + TCB],
                            xcpre.ap()[128 * dt:128 * (dt + 1), 0:TCB],
                            reads=[xcpre], writes=[(xcp, "b")])
                    else:
                        raw = raw4[dt % 4]
                        dma(ctx, "sp", raw[:, 0:TCB],
                            xcpre.ap()[128 * dt:128 * (dt + 1), T - TCB:T],
                            reads=[xcpre], writes=[raw])
                        op_dve(ctx, lambda E, a=xcp, b=raw: E.tensor_copy(
                            a[:, HW:HW + TCB],
                            bass.AP(b, TCB - 1, [[TCB + HW, 128], [-1, TCB]])),
                            [raw], [(xcp, "b")])
                else:
                    if dirn == 0:
                        dma(ctx, "sp", xcp[:, HW - 3:HW + TCB],
                            xcpre.ap()[128 * dt:128 * (dt + 1), ol - 3:ol + TCB],
                            reads=[xcpre], writes=[(xcp, "b")])
                    else:
                        raw = raw4[dt % 4]
                        dma(ctx, "sp", raw[:, 0:TCB + 3],
                            xcpre.ap()[128 * dt:128 * (dt + 1), ol:ol + TCB + 3],
                            reads=[xcpre], writes=[raw])
                        op_dve(ctx, lambda E, a=xcp, b=raw: E.tensor_copy(
                            a[:, HW - 3:HW + TCB],
                            bass.AP(b, TCB + 2, [[TCB + HW, 128], [-1, TCB + 3]])),
                            [raw], [(xcp, "b")])
                ta = ta8[dt % 2]
                for i in range(4):
                    op_act(ctx, lambda E, a=ta[i], b=xcp, c=cw, dt=dt, i=i: E.activation(
                        a[:, 0:TCB], b[:, HW - 3 + i:HW - 3 + i + TCB], AF.Copy,
                        scale=c[:, 4 * dt + i:4 * dt + i + 1]),
                        [(xcp, "h"), (xcp, "b"), cw], [ta[i]])
                op_pool(ctx, lambda E, a=ta[0], b=ta[1]: E.tensor_add(a[:, 0:TCB], a[:, 0:TCB], b[:, 0:TCB]),
                        [ta[0], ta[1]], [ta[0]])
                op_pool(ctx, lambda E, a=ta[2], b=ta[3]: E.tensor_add(a[:, 0:TCB], a[:, 0:TCB], b[:, 0:TCB]),
                        [ta[2], ta[3]], [ta[2]])
                op_pool(ctx, lambda E, a=ta[0], b=ta[2]: E.tensor_add(a[:, 0:TCB], a[:, 0:TCB], b[:, 0:TCB]),
                        [ta[0], ta[2]], [ta[0]])
                if dt >= 1:
                    pcv = dt - 1
                    op_act(ctx, lambda E, a=xc[pcv], b=ta8[pcv % 2][0], c=cb, pcv=pcv: E.activation(
                        a[:, 0:TCB], b[:, 0:TCB], AF.Silu, bias=c[:, pcv:pcv + 1]),
                        [ta8[pcv % 2][0], cb], [xc[pcv]])
                    op_act(ctx, lambda E, a=xcb[pcv], b=ta8[pcv % 2][0], c=cb, pcv=pcv: E.activation(
                        a[:, 0:TCB], b[:, 0:TCB], AF.Silu, bias=c[:, pcv:pcv + 1]),
                        [ta8[pcv % 2][0], cb], [xcb[pcv]])
            pcv = DT2 - 1
            op_act(ctx, lambda E, a=xc[pcv], b=ta8[pcv % 2][0], c=cb, pcv=pcv: E.activation(
                a[:, 0:TCB], b[:, 0:TCB], AF.Silu, bias=c[:, pcv:pcv + 1]),
                [ta8[pcv % 2][0], cb], [xc[pcv]])
            op_act(ctx, lambda E, a=xcb[pcv], b=ta8[pcv % 2][0], c=cb, pcv=pcv: E.activation(
                a[:, 0:TCB], b[:, 0:TCB], AF.Silu, bias=c[:, pcv:pcv + 1]),
                [ta8[pcv % 2][0], cb], [xcb[pcv]])
            par = (dirn * nchb + ch) % 2
            pb = ctx.psum[4]
            for dt in range(DT2):
                op_pe(ctx, lambda E, dt=dt, pb=pb, w=xpp[dt], a=xcb[dt]: E.matmul(
                    pb[0:96, 0:TCB], w[0:128, 0:96], a[:, 0:TCB],
                    start=(dt == 0), stop=(dt == DT2 - 1)),
                    [xpp[dt], xcb[dt]], [pb])
            dblb = ctx.tile((128, TCB), BF16)
            bcb = ctx.tile((128, TCB), BF16)
            op_act(ctx, lambda E, a=dblb, pb=pb: E.activation(a[0:DTR, 0:TCB], pb[0:DTR, 0:TCB], AF.Copy),
                   [pb], [dblb])
            op_act(ctx, lambda E, a=bcb, pb=pb: E.activation(a[64:96, 0:TCB], pb[64:96, 0:TCB], AF.Copy),
                   [pb], [bcb])
            dma(ctx, "pool", bc_scr.ap()[par * 32:par * 32 + 32, 0:TCB], bcb[64:96, 0:TCB],
                reads=[bcb], writes=[(bc_scr, par)])
            brep = ctx.tile((128, NN * TCB), BF16)
            crep = ctx.tile((128, NN * TCB), BF16)
            for rep, r0 in ((brep, 0), (crep, NN)):
                src = bass.AP(bc_scr, (par * 32 + r0) * TCB, [[0, 1], [TCB, NN], [1, TCB]])
                dst = bass.AP(rep, 0, [[NN * TCB, 128], [TCB, NN], [1, TCB]])
                dma(ctx, "sp", dst, src.partition_broadcast(128),
                    reads=[(bc_scr, par)], writes=[rep])
            delta = [ctx.tile((128, TCB), F32) for _ in range(DT2)]
            ut2 = [ctx.tile((128, TCB), F32) for _ in range(2)]
            a12 = [ctx.tile((128, TCB), F32) for _ in range(2)]
            for dt in range(DT2):
                pb2 = ctx.psum[3 - dt % 2]
                u_t, a1 = ut2[dt % 2], a12[dt % 2]
                op_pe(ctx, lambda E, dt=dt, pb2=pb2, a=dblb, w=dtp: E.matmul(
                    pb2[0:128, 0:TCB], w[0:DTR, 128 * dt:128 * (dt + 1)],
                    a[0:DTR, 0:TCB], start=True, stop=True),
                    [dtp, dblb], [pb2])
                op_dve(ctx, lambda E, a=u_t, pb=pb2, c=dpb, dt=dt: E.tensor_scalar(
                    a[:, 0:TCB], pb[0:128, 0:TCB], c[:, dt:dt + 1], None, ALU.add),
                    [pb2, dpb], [u_t])
                op_act(ctx, lambda E, a=a1, b=u_t: E.activation(a[:, 0:TCB], b[:, 0:TCB], AF.Abs),
                       [u_t], [a1])
                op_act(ctx, lambda E, a=a1, s=m1: E.activation(a[:, 0:TCB], a[:, 0:TCB],
                                                               AF.Exp, scale=s[:, 0:1]),
                       [a1, m1], [a1])
                op_act(ctx, lambda E, a=a1, s=one: E.activation(a[:, 0:TCB], a[:, 0:TCB],
                                                                AF.Ln, bias=s[:, 0:1]),
                       [a1, one], [a1])
                op_dve(ctx, lambda E, a=delta[dt], b=u_t, c=a1: E.scalar_tensor_tensor(
                    a[:, 0:TCB], b[:, 0:TCB], 0.0, c[:, 0:TCB], ALU.max, ALU.add),
                    [u_t, a1], [delta[dt]])
            ctx.free(*ut2, *a12, dblb, bcb)
            aa2 = [ctx.tile((128, NN * TCB), BF16) for _ in range(2)]
            db2 = [ctx.tile((128, NN * TCB), BF16) for _ in range(2)]
            hh2 = [ctx.tile((128, NN * TCB), BF16) for _ in range(2)]
            t2a2 = [ctx.tile((128, 8 * TCB), F32) for _ in range(2)]
            du2 = [ctx.tile((128, TCB), BF16) for _ in range(2)]
            yy = [ctx.tile((128, TCB), F32) for _ in range(DT2)]
            for dt in range(DT2):
                aa, db, hh = aa2[dt % 2], db2[dt % 2], hh2[dt % 2]
                t2a, du = t2a2[dt % 2], du2[dt % 2]
                op_dve(ctx, lambda E, a=du, b=delta[dt], c=xc[dt]: E.tensor_mul(
                    a[:, 0:TCB], b[:, 0:TCB], c[:, 0:TCB]),
                    [delta[dt], xc[dt]], [du])
                for n in range(NN):
                    op_act(ctx, lambda E, a=aa, n=n, b=delta[dt], s=a_neg, dt=dt: E.activation(
                        a[:, n * TCB:(n + 1) * TCB], b[:, 0:TCB], AF.Exp,
                        scale=s[:, NN * dt + n:NN * dt + n + 1]),
                        [delta[dt], a_neg], [(aa, n)])
                duv = bass.AP(du, 0, [[TCB, 128], [0, NN], [1, TCB]])
                op_dve(ctx, lambda E, a=db, b=duv, c=brep: E.tensor_tensor(
                    a[:, 0:NN * TCB], b, c[:, 0:NN * TCB], ALU.mult),
                    [du, brep], [db])
                aaS = bass.AP(aa, 0, [[NN * TCB, 128], [TCB, NN]])
                dbS = bass.AP(db, 0, [[NN * TCB, 128], [TCB, NN]])
                t16 = ctx.tile((128, NN), F32)
                op_dve(ctx, lambda E, a=t16, b=aaS, cr=carr[dt], c0=dirn * NN: E.tensor_tensor(
                    a[:, 0:NN], b, cr[:, c0:c0 + NN], ALU.mult),
                    [(aa, n) for n in range(NN)] + [carr[dt]], [t16])
                op_dve(ctx, lambda E, a=dbS, b=t16: E.tensor_tensor(
                    a, b[:, 0:NN], a, ALU.add),
                    [t16, db], [db])
                op_dve(ctx, lambda E, a=aaS: E.tensor_scalar_mul(a, a, 0.0),
                       [(aa, n) for n in range(NN)], [(aa, "z")])
                zc = ctx.constcol(0.0)
                op_dve(ctx, lambda E, a=hh, b=aa, c=db, z=zc: E.tensor_tensor_scan(
                    a[:, 0:NN * TCB], b[:, 0:NN * TCB], c[:, 0:NN * TCB],
                    z[:, 0:1], ALU.mult, ALU.add),
                    [(aa, "z"), db, zc] + [(aa, n) for n in range(NN)],
                    [hh] + [(hh, n) for n in range(NN)])
                ctx.free(t16)
                op_dve(ctx, lambda E, a=carr[dt], b=hh, col=dirn * NN: E.tensor_copy(
                    a[:, col:col + NN],
                    bass.AP(b, TCB - 1, [[NN * TCB, 128], [TCB, NN]])),
                    [hh], [carr[dt]])
                op_dve(ctx, lambda E, a=hh, b=crep: E.tensor_mul(a[:, 0:NN * TCB], a[:, 0:NN * TCB],
                                                                 b[:, 0:NN * TCB]),
                       [hh, crep], [hh])
                op_pool(ctx, lambda E, a=t2a, b=hh: E.tensor_add(
                    a[:, 0:8 * TCB], b[:, 0:8 * TCB], b[:, 8 * TCB:16 * TCB]),
                    [hh], [t2a])
                op_pool(ctx, lambda E, a=t2a: E.tensor_add(
                    a[:, 0:4 * TCB], a[:, 0:4 * TCB], a[:, 4 * TCB:8 * TCB]),
                    [t2a], [t2a])
                if dt >= 1:
                    pdt = dt - 1
                    t2p = t2a2[pdt % 2]
                    op_dve(ctx, lambda E, a=t2p: E.tensor_add(
                        a[:, 0:2 * TCB], a[:, 0:2 * TCB], a[:, 2 * TCB:4 * TCB]),
                        [t2p], [t2p])
                    op_dve(ctx, lambda E, a=t2p: E.tensor_add(
                        a[:, 0:TCB], a[:, 0:TCB], a[:, TCB:2 * TCB]),
                        [t2p], [t2p])
                    op_dve(ctx, lambda E, a=yy[pdt], b=t2p, c=xc[pdt], d=Dc, pdt=pdt:
                           E.scalar_tensor_tensor(a[:, 0:TCB], c[:, 0:TCB], d[:, pdt:pdt + 1],
                                                  b[:, 0:TCB], ALU.mult, ALU.add),
                           [t2p, xc[pdt], Dc], [yy[pdt]])
            pdt = DT2 - 1
            t2p = t2a2[pdt % 2]
            op_dve(ctx, lambda E, a=t2p: E.tensor_add(
                a[:, 0:2 * TCB], a[:, 0:2 * TCB], a[:, 2 * TCB:4 * TCB]),
                [t2p], [t2p])
            op_dve(ctx, lambda E, a=t2p: E.tensor_add(
                a[:, 0:TCB], a[:, 0:TCB], a[:, TCB:2 * TCB]),
                [t2p], [t2p])
            op_dve(ctx, lambda E, a=yy[pdt], b=t2p, c=xc[pdt], d=Dc, pdt=pdt:
                   E.scalar_tensor_tensor(a[:, 0:TCB], c[:, 0:TCB], d[:, pdt:pdt + 1],
                                          b[:, 0:TCB], ALU.mult, ALU.add),
                   [t2p, xc[pdt], Dc], [yy[pdt]])
            ctx.free(*aa2, *db2, *hh2, *t2a2, *du2, *delta, *xc, *xcb, brep, crep)
            if dirn == 0:
                for dt in range(DT2):
                    dma(ctx, "pool", yf_scr.ap()[128 * dt:128 * (dt + 1), ol:ol + TCB],
                        yy[dt][:, 0:TCB], reads=[yy[dt]], writes=[(yf_scr, ("c", ch, dt))])
            else:
                zt2 = [ctx.tile((128, TCB), F32) for _ in range(2)]
                yfb2 = [ctx.tile((128, TCB), BF16) for _ in range(2)]
                yf2 = [ctx.tile((128, TCB), F32) for _ in range(2)]
                zs2 = [ctx.tile((128, TCB), F32) for _ in range(2)]
                for dt in range(DT2):
                    zt, yfb, yf, zs = zt2[dt % 2], yfb2[dt % 2], yf2[dt % 2], zs2[dt % 2]
                    dma(ctx, "sp", yfb[:, 0:TCB], yf_scr.ap()[128 * dt:128 * (dt + 1), ol:ol + TCB],
                        reads=[yf_scr], writes=[yfb])
                    op_dve(ctx, lambda E, a=yf, c=yfb, b=yy[dt]: E.tensor_tensor(
                        a[:, 0:TCB], c[:, 0:TCB],
                        bass.AP(b, TCB - 1, [[TCB, 128], [-1, TCB]]), ALU.add),
                        [yy[dt], yfb], [yf])
                    dma(ctx, "sp", zt[:, 0:TCB], z_scr.ap()[128 * dt:128 * (dt + 1), ol:ol + TCB],
                        reads=[z_scr], writes=[zt])
                    op_act(ctx, lambda E, a=zs, b=zt: E.activation(a[:, 0:TCB], b[:, 0:TCB], AF.Silu),
                           [zt], [zs])
                    op_dve(ctx, lambda E, a=yf, b=zs: E.tensor_mul(a[:, 0:TCB], a[:, 0:TCB], b[:, 0:TCB]),
                           [zs, yf], [yf])
                    dma(ctx, "pool", ys_scr.ap()[128 * dt:128 * (dt + 1), ol:ol + TCB],
                        yf[:, 0:TCB], reads=[yf], writes=[(ys_scr, ("c", ch, dt))])
                ctx.free(*zt2, *yfb2, *yf2, *zs2)
            ctx.free(*yy)
    ctx.pop()

    ctx.push()
    opb = cast_to_bf16_dram(ctx, nc, f"{pfx}_opb", dr["out_proj_w"], (DI, H))
    opp = load_w_direct_panels(ctx, opb, DI, H)
    for ch in range(NCH):
        c0 = ch * TC
        ysb = [ctx.tile((128, TC), BF16) for _ in range(DT2)]
        for k in range(DT2):
            dma(ctx, "pool", ysb[k][:, 0:TC], ys_scr.ap()[128 * k:128 * (k + 1), c0:c0 + TC],
                reads=[ys_scr], writes=[ysb[k]])
        mo = [ctx.tile((128, TC), F32) for _ in range(FT)]
        for m in range(FT):
            pb = ctx.psum[m % 4]
            for k in range(DT2):
                op_pe(ctx, lambda E, m=m, k=k, pb=pb, w=opp[k], a=ysb[k]: E.matmul(
                    pb[0:128, 0:TC], w[:, 128 * m:128 * (m + 1)], a[:, 0:TC],
                    start=(k == 0), stop=(k == DT2 - 1)),
                    [opp[k], ysb[k]], [pb])
            op_act(ctx, lambda E, pb=pb, a=mo[m]: E.activation(a[:, 0:TC], pb[0:128, 0:TC], AF.Copy),
                   [pb], [mo[m]])
        xn = ln_chunk(ctx, mo, TC)
        for m in range(FT):
            ob = ctx.tile((128, TC), BF16)
            op_dve(ctx, lambda E, a=ob, b=xn[m], g=lng, bb=lnb, m=m: E.tensor_scalar(
                a[:, 0:TC], b[:, 0:TC], g[:, m:m + 1], bb[:, m:m + 1], ALU.mult, ALU.add),
                [xn[m], lng, lnb], [ob])
            dma(ctx, "pool", mon_scr.ap()[128 * m:128 * (m + 1), c0:c0 + TC], ob[:, 0:TC],
                reads=[ob], writes=[(mon_scr, ("c", ch, m))])
            ctx.free(ob)
        ctx.free(*ysb, *mo, *xn)
    ctx.pop()

    ctx.push()
    f1b = cast_to_bf16_dram(ctx, nc, f"{pfx}_f1b", dr["ff_w1"], (H, 4 * H))
    f2b = cast_to_bf16_dram(ctx, nc, f"{pfx}_f2b", dr["ff_w2"], (4 * H, H))
    f1p = load_w_direct_panels(ctx, f1b, H, 4 * H)
    f2p = load_w_direct_panels(ctx, f2b, 4 * H, H)
    TM = 256
    for ch in range(T // TM):
        c0 = ch * TM
        h0 = [ctx.tile((128, TM), BF16) for _ in range(FT)]
        for k in range(FT):
            dma(ctx, "sp", h0[k][:, 0:TM], mon_scr.ap()[128 * k:128 * (k + 1), c0:c0 + TM],
                reads=[mon_scr], writes=[h0[k]])
        h1 = [ctx.tile((128, TM), BF16) for _ in range(32)]
        for m in range(32):
            pb = ctx.psum[m % 4]
            for k in range(FT):
                op_pe(ctx, lambda E, m=m, k=k, pb=pb, w=f1p[k], a=h0[k]: E.matmul(
                    pb[0:128, 0:TM], w[:, 128 * m:128 * (m + 1)], a[:, 0:TM],
                    start=(k == 0), stop=(k == FT - 1)),
                    [f1p[k], h0[k]], [pb])
            op_act(ctx, lambda E, pb=pb, a=h1[m], b=fb1, m=m: E.activation(
                a[:, 0:TM], pb[0:128, 0:TM], AF.Gelu, bias=b[:, m:m + 1]),
                [pb, fb1], [h1[m]])
        xt = [ctx.tile((128, TM), F32) for _ in range(FT)]
        for m in range(FT):
            dma(ctx, "sp", xt[m][:, 0:TM], xin.ap()[128 * m:128 * (m + 1), c0:c0 + TM],
                reads=[(xin, ("d", ch))], writes=[xt[m]])
        for m in range(FT):
            pb = ctx.psum[4 + m % 2]
            for k in range(32):
                op_pe(ctx, lambda E, m=m, k=k, pb=pb, w=f2p[k], a=h1[k]: E.matmul(
                    pb[0:128, 0:TM], w[:, 128 * m:128 * (m + 1)], a[:, 0:TM],
                    start=(k == 0), stop=(k == 31)),
                    [f2p[k], h1[k]], [pb])
            op_dve(ctx, lambda E, pb=pb, a=xt[m], b=fb2, m=m: E.scalar_tensor_tensor(
                a[:, 0:TM], pb[0:128, 0:TM], b[:, m:m + 1], a[:, 0:TM], ALU.add, ALU.add),
                [pb, fb2, xt[m]], [xt[m]])
            dma(ctx, "pool", xout.ap()[128 * m:128 * (m + 1), c0:c0 + TM], xt[m][:, 0:TM],
                reads=[xt[m]], writes=[(xout, ("d", ch))], is_out=True)
        ctx.free(*h0, *h1, *xt)
    ctx.pop()


EXP = 64
KTOP = 8
CAP = 512
NLOC = 8 * T


def moe_shared_buffers(nc, pfx):
    nloc = KTOP * T
    return {
        "xtm_loc": nc.dram_tensor(f"{pfx}_xtm", [T, H], BF16),
        "xrep": nc.dram_tensor(f"{pfx}_xrep", [NCORES * T, H], BF16),
        "idx_scr": nc.dram_tensor(f"{pfx}_idxs", [nloc], F32),
        "rank_scr": nc.dram_tensor(f"{pfx}_rank", [nloc], F32),
        "cnt_loc": nc.dram_tensor(f"{pfx}_cntl", [EXP, 1], F32),
        "cnt_all": nc.dram_tensor(f"{pfx}_cnta", [NCORES * EXP, 1], F32),
        "canvas_loc": nc.dram_tensor(f"{pfx}_cvl", [EXP * CAP, 2], F32),
        "canvas": nc.dram_tensor(f"{pfx}_cv", [EXP * CAP, 2], F32),
        "pub_loc": nc.dram_tensor(f"{pfx}_pub", [8 * CAP, H], BF16),
        "outrep": nc.dram_tensor(f"{pfx}_orep", [EXP * CAP, H], BF16),
        "sh_scr": nc.dram_tensor(f"{pfx}_sh", [H, T], BF16),
    }


def emit_moe(ctx, nc, xin, xout, dr, wb, sb, meta, scale=0.5):
    NTT = T // 128
    idxf = ctx.sbuf((128, NTT * KTOP), F32)
    w_all = ctx.sbuf((128, NTT * KTOP), F32)
    si = ctx.sbuf((128, NTT * KTOP), I32)
    tokf = ctx.sbuf((128, NTT), F32)
    tok0 = ctx.sbuf((128, 1), F32)
    dma(ctx, "sp", tok0[:], meta["tokid0"].ap(), reads=[meta["tokid0"]], writes=[tok0])
    for tt in range(NTT):
        op_dve(ctx, lambda E, tt=tt: E.tensor_scalar(
            tokf[:, tt:tt + 1], tok0[:].to_broadcast([128, 1]), float(128 * tt), None, ALU.add),
            [tok0], [(tokf, tt)])

    ctx.push()
    gwp = []
    for k in range(FT):
        t = ctx.sbuf((128, EXP), F32)
        dma(ctx, "sp", t[0:128, 0:EXP], wb["gate_f32"].ap()[128 * k:128 * (k + 1), 0:EXP],
            reads=[wb["gate_f32"]], writes=[t])
        gwp.append(t)
    sgup = load_w_direct_panels(ctx, wb["sgu"], H, 2 * IS)
    sdwp = load_w_direct_panels(ctx, wb["sdw"], IS, H)
    sgwp = load_w_direct_panels(ctx, wb["sg"], H, 1)
    for ch in range(NCH):
        c0 = ch * TC
        xt = [ctx.tile((128, TC), F32) for _ in range(FT)]
        for k in range(FT):
            dma(ctx, "sp", xt[k][:], xin.ap()[128 * k:128 * (k + 1), c0:c0 + TC],
                reads=[(xin, ("c", ch))], writes=[xt[k]])
        xn = ln_chunk(ctx, xt, TC)
        xnb = [ctx.tile((128, TC), BF16) for _ in range(FT)]
        for k in range(FT):
            op_act(ctx, lambda E, a=xnb[k], b=xn[k]: E.activation(a[:, 0:TC], b[:, 0:TC], AF.Copy),
                   [xn[k]], [xnb[k]])
        ctx.free(*xt)
        xtm = ctx.tile((128, H), BF16)
        for sub in range(TC // 128):
            tt = ch * (TC // 128) + sub
            pbf = ctx.psum_bf2[sub % 2]
            for k in range(FT):
                op_pe(ctx, lambda E, k=k, sub=sub, a=xnb[k], pbf=pbf: E.transpose(
                    pbf[0:128, 128 * k:128 * (k + 1)], a[:, 128 * sub:128 * (sub + 1)],
                    ctx.consts["ident_bf"][0:128, 0:128]),
                    [xnb[k], ctx.consts["ident_bf"]], [(pbf, k), (pbf, "bank")])
            for k in range(FT):
                op_act(ctx, lambda E, k=k, a=xtm, pbf=pbf: E.activation(
                    a[:, 128 * k:128 * (k + 1)], pbf[0:128, 128 * k:128 * (k + 1)], AF.Copy),
                    [(pbf, k), (pbf, "bank")], [(xtm, k)])
            dma(ctx, "pool", sb["xtm_loc"].ap()[128 * tt:128 * (tt + 1), 0:H], xtm[:, 0:H],
                reads=[(xtm, k) for k in range(FT)], writes=[(sb["xtm_loc"], tt)])
            pl = ctx.psum[sub % 2]
            for k in range(FT):
                op_pe(ctx, lambda E, k=k, sub=sub, pl=pl, a=xn[k], g=gwp[k]: E.matmul(
                    pl[0:128, 0:EXP], a[:, 128 * sub:128 * (sub + 1)], g[:, 0:EXP],
                    start=(k == 0), stop=(k == FT - 1)),
                    [xn[k], gwp[k]], [pl])
            lg = ctx.tile((128, EXP), F32)
            mx = ctx.tile((128, KTOP), F32)
            mi = ctx.tile((128, KTOP), U32)
            op_act(ctx, lambda E, pl=pl, a=lg: E.activation(a[:, 0:EXP], pl[0:128, 0:EXP], AF.Copy),
                   [pl], [lg])
            op_dve(ctx, lambda E, a=mx, b=lg: E.max(out=a[:], in_=b[:, 0:EXP]), [lg], [mx])
            op_dve(ctx, lambda E, a=mi, b=mx, c=lg: E.max_index(a[:], b[:], c[:, 0:EXP]),
                   [mx, lg], [mi])
            we = ctx.tile((128, KTOP), F32)
            op_dve(ctx, lambda E, a=we, b=mx: E.tensor_scalar(
                a[:], b[:], b[:, 0:1], None, ALU.subtract), [mx], [we])
            op_act(ctx, lambda E, a=we: E.activation(a[:], a[:], AF.Exp), [we], [we])
            sm = ctx.tile((128, 1), F32)
            op_dve(ctx, lambda E, a=sm, b=we: E.tensor_reduce(a[:], b[:], AX.X, ALU.add),
                   [we], [sm])
            op_dve(ctx, lambda E, a=sm: E.reciprocal(a[:], a[:]), [sm], [sm])
            op_dve(ctx, lambda E, a=w_all, b=we, c=sm, tt=tt: E.tensor_scalar(
                a[:, KTOP * tt:KTOP * (tt + 1)], b[:], c[:, 0:1], None, ALU.mult),
                [we, sm], [(w_all, tt)])
            op_dve(ctx, lambda E, a=idxf, b=mi, tt=tt: E.tensor_copy(
                a[:, KTOP * tt:KTOP * (tt + 1)], b[:]),
                [mi], [(idxf, tt)])
            ctx.free(lg, mx, mi, we, sm)
        pgt = ctx.psum[4]
        for k in range(FT):
            op_pe(ctx, lambda E, k=k, pgt=pgt, a=xnb[k], w=sgwp[k]: E.matmul(
                pgt[0:1, 0:TC], w[:, 0:1], a[:, 0:TC], start=(k == 0), stop=(k == FT - 1)),
                [xnb[k], sgwp[k]], [pgt])
        grow = ctx.tile((1, 512), F32)
        op_act(ctx, lambda E, a=grow, pgt=pgt: E.activation(a[0:1, 0:TC], pgt[0:1, 0:TC], AF.Sigmoid),
               [pgt], [grow])
        op_pe(ctx, lambda E, pgt=pgt, a=grow: E.matmul(
            pgt[0:128, 0:TC], ctx.consts["ones_row_f"][0:1, 0:128], a[0:1, 0:TC],
            start=True, stop=True),
            [grow, ctx.consts["ones_row_f"], pgt], [(pgt, "bc")])
        gbc = ctx.tile((128, TC), F32)
        op_act(ctx, lambda E, a=gbc, pgt=pgt: E.activation(a[:, 0:TC], pgt[0:128, 0:TC], AF.Copy),
               [(pgt, "bc")], [gbc])
        actb = [ctx.tile((128, TC), BF16) for _ in range(IS // 128)]
        sg_t = ctx.tile((128, TC), F32)
        for mi2 in range(IS // 128):
            pg = ctx.psum[2]
            pu = ctx.psum[3]
            for k in range(FT):
                op_pe(ctx, lambda E, k=k, m=mi2, pg=pg, a=xnb[k], w=sgup[k]: E.matmul(
                    pg[0:128, 0:TC], w[:, 128 * m:128 * (m + 1)], a[:, 0:TC],
                    start=(k == 0), stop=(k == FT - 1)),
                    [sgup[k], xnb[k]], [pg])
            for k in range(FT):
                op_pe(ctx, lambda E, k=k, m=mi2, pu=pu, a=xnb[k], w=sgup[k]: E.matmul(
                    pu[0:128, 0:TC], w[:, IS + 128 * m:IS + 128 * (m + 1)], a[:, 0:TC],
                    start=(k == 0), stop=(k == FT - 1)),
                    [sgup[k], xnb[k]], [pu])
            op_act(ctx, lambda E, pg=pg, a=sg_t: E.activation(a[:, 0:TC], pg[0:128, 0:TC], AF.Silu),
                   [pg], [sg_t])
            op_dve(ctx, lambda E, pu=pu, a=actb[mi2], b=sg_t: E.tensor_tensor(
                a[:, 0:TC], b[:, 0:TC], pu[0:128, 0:TC], ALU.mult),
                [pu, sg_t], [actb[mi2]])
        for m in range(FT):
            pd = ctx.psum[4 + (m % 2)]
            for k in range(IS // 128):
                op_pe(ctx, lambda E, m=m, k=k, pd=pd, a=actb[k], w=sdwp[k]: E.matmul(
                    pd[0:128, 0:TC], w[:, 128 * m:128 * (m + 1)], a[:, 0:TC],
                    start=(k == 0), stop=(k == IS // 128 - 1)),
                    [actb[k], sdwp[k]], [pd])
            shb = ctx.tile((128, TC), BF16)
            op_dve(ctx, lambda E, pd=pd, a=shb, g=gbc: E.tensor_tensor(
                a[:, 0:TC], pd[0:128, 0:TC], g[:, 0:TC], ALU.mult),
                [pd, gbc], [shb])
            dma(ctx, "pool", sb["sh_scr"].ap()[128 * m:128 * (m + 1), c0:c0 + TC], shb[:, 0:TC],
                reads=[shb], writes=[(sb["sh_scr"], (ch, m))])
            ctx.free(shb)
        ctx.free(*xnb, *xn, xtm, grow, gbc, sg_t, *actb)
        ntc = TC // 128
        dst = bass.AP(sb["idx_scr"], ntc * ch * 128 * KTOP,
                      [[KTOP, 128], [128 * KTOP, ntc], [1, KTOP]])
        dma(ctx, "pool", dst, idxf[:, KTOP * ntc * ch:KTOP * ntc * (ch + 1)],
            reads=[(idxf, tt) for tt in range(ntc * ch, ntc * (ch + 1))],
            writes=[(sb["idx_scr"], ch)])
    op_pool(ctx, lambda E: E.collective_compute(
        "AllGather", ALU.bypass, replica_groups=[list(range(NCORES))],
        ins=[sb["xtm_loc"].ap().opt()], outs=[sb["xrep"].ap().opt()]),
        reads=[sb["xtm_loc"]], writes=[sb["xrep"]], inc=-1)
    ctx.pop()

    ctx.push()
    nloc = KTOP * T
    ACH = min(8192, nloc)
    nach = nloc // ACH
    ecol = ctx.sbuf((EXP, 1), F32)
    op_pool(ctx, lambda E: E.iota(ecol[:], [[1, 1]], base=0, channel_multiplier=1,
                                  allow_small_or_imprecise_dtypes=True), [], [ecol])
    cnt = ctx.sbuf((EXP, 1), F32)
    op_dve(ctx, lambda E: E.memset(cnt[:], 0.0), [], [cnt])
    zcol = ctx.constcol(0.0)
    irep = ctx.tile((EXP, ACH), F32)
    oneh = ctx.tile((EXP, ACH), F32)
    csum = ctx.tile((EXP, ACH), F32)
    for a0 in range(nach):
        src = bass.AP(sb["idx_scr"], a0 * ACH, [[0, 1], [1, ACH]])
        dma(ctx, "sp", irep[:], src.partition_broadcast(EXP), reads=[sb["idx_scr"]],
            writes=[irep])
        op_dve(ctx, lambda E: E.tensor_scalar(oneh[:], irep[:], ecol[:, 0:1], None, ALU.is_equal),
               [irep, ecol], [oneh])
        op_dve(ctx, lambda E: E.tensor_reduce(csum[:, 0:1], oneh[:], AX.X, ALU.add),
               [oneh], [(csum, "r")])
        op_dve(ctx, lambda E: E.tensor_add(cnt[:], cnt[:], csum[:, 0:1]),
               [(csum, "r"), cnt], [cnt])
    dma(ctx, "pool", sb["cnt_loc"].ap(), cnt[:], reads=[cnt], writes=[sb["cnt_loc"]])
    op_pool(ctx, lambda E: E.collective_compute(
        "AllGather", ALU.bypass, replica_groups=[list(range(NCORES))],
        ins=[sb["cnt_loc"].ap().opt()], outs=[sb["cnt_all"].ap().opt()]),
        reads=[sb["cnt_loc"]], writes=[sb["cnt_all"]], inc=-1)
    cnt8 = ctx.tile((EXP, NCORES), F32)
    src = bass.AP(sb["cnt_all"], 0, [[1, EXP], [EXP, NCORES]])
    dma(ctx, "sp", cnt8[:], src, reads=[sb["cnt_all"]], writes=[cnt8],
        allow_slow_non_contiguous=True)
    mlt = ctx.tile((EXP, NCORES), F32)
    msrc = bass.AP(meta["mask_lt"], 0, [[0, 1], [1, NCORES]])
    dma(ctx, "sp", mlt[:], msrc.partition_broadcast(EXP), reads=[meta["mask_lt"]],
        writes=[mlt])
    base = ctx.tile((EXP, 1), F32)
    op_dve(ctx, lambda E: E.tensor_mul(cnt8[:], cnt8[:], mlt[:]), [cnt8, mlt], [cnt8])
    op_dve(ctx, lambda E: E.tensor_reduce(base[:], cnt8[:], AX.X, ALU.add),
           [cnt8], [base])
    carry = ctx.tile((EXP, 1), F32)
    op_dve(ctx, lambda E: E.memset(carry[:], 0.0), [], [carry])
    rk = ctx.tile((EXP, ACH), F32)
    ev1 = ctx.tile((1, 512), F32)
    for a0 in range(nach):
        src = bass.AP(sb["idx_scr"], a0 * ACH, [[0, 1], [1, ACH]])
        dma(ctx, "sp", irep[:], src.partition_broadcast(EXP), reads=[sb["idx_scr"]],
            writes=[irep])
        op_dve(ctx, lambda E: E.tensor_scalar(oneh[:], irep[:], ecol[:, 0:1], None, ALU.is_equal),
               [irep, ecol], [oneh])
        op_dve(ctx, lambda E: E.tensor_tensor_scan(
            csum[:], oneh[:], zcol[0:EXP, 0:1].to_broadcast([EXP, ACH]), carry[:, 0:1],
            ALU.add, ALU.add),
            [oneh, zcol, carry], [csum])
        op_dve(ctx, lambda E: E.tensor_copy(carry[:], csum[:, ACH - 1:ACH]),
               [csum], [carry])
        op_dve(ctx, lambda E: E.tensor_scalar(csum[:], csum[:], base[:, 0:1], None, ALU.add),
               [csum, base], [csum])
        op_dve(ctx, lambda E: E.tensor_mul(csum[:], csum[:], oneh[:]), [oneh, csum], [csum])
        for q in range(ACH // 512):
            pq = ctx.psum[q % 2]
            op_pe(ctx, lambda E, q=q, pq=pq: E.matmul(
                pq[0:1, 0:512], ctx.consts["ones_col_f"][0:EXP, 0:1],
                csum[:, 512 * q:512 * (q + 1)], start=True, stop=True),
                [csum, ctx.consts["ones_col_f"]], [pq])
            op_dve(ctx, lambda E, pq=pq: E.tensor_copy(ev1[0:1, 0:512], pq[0:1, 0:512]),
                   [pq], [ev1])
            dma(ctx, "pool", bass.AP(sb["rank_scr"], a0 * ACH + 512 * q, [[0, 1], [1, 512]]),
                ev1[0:1, 0:512], reads=[ev1], writes=[(sb["rank_scr"], (a0, q))])
    ctx.free(irep, oneh, csum, rk, ev1, cnt8, mlt)
    grk = ctx.tile((128, NTT * KTOP), F32)
    rsrc = bass.AP(sb["rank_scr"], 0, [[KTOP, 128], [128 * KTOP, NTT], [1, KTOP]])
    dma(ctx, "sp", grk[:], rsrc, reads=[(sb["rank_scr"], (a, q)) for a in range(nach)
                                        for q in range(ACH // 512)], writes=[grk])
    op_dve(ctx, lambda E: E.tensor_scalar(grk[:], grk[:], -1.0, None, ALU.add),
           [grk], [grk])
    srow = ctx.tile((128, NTT * KTOP), F32)
    op_dve(ctx, lambda E: E.scalar_tensor_tensor(srow[:], idxf[:], float(CAP), grk[:],
                                                 ALU.mult, ALU.add),
           [(idxf, tt) for tt in range(NTT)] + [grk], [srow])
    surv = ctx.tile((128, NTT * KTOP), F32)
    op_dve(ctx, lambda E: E.tensor_scalar(surv[:], grk[:], float(CAP), None, ALU.is_lt),
           [grk], [surv])
    op_dve(ctx, lambda E: E.tensor_mul(srow[:], srow[:], surv[:]), [surv, srow], [srow])
    op_dve(ctx, lambda E: E.tensor_scalar(surv[:], surv[:], -1e7, 1e7, ALU.mult, ALU.add),
           [surv], [surv])
    op_dve(ctx, lambda E: E.tensor_add(srow[:], srow[:], surv[:]), [surv, srow], [srow])
    op_dve(ctx, lambda E: E.tensor_copy(si[:], srow[:]), [srow], [si])
    ctx.free(srow, surv, grk)
    zt = ctx.tile((128, 512), F32)
    op_dve(ctx, lambda E: E.memset(zt[:], 0.0), [], [zt])
    dma(ctx, "pool", bass.AP(sb["canvas_loc"], 0, [[512, 128], [1, 512]]), zt[:],
        reads=[zt], writes=[sb["canvas_loc"]])
    val = ctx.tile((128, 2 * KTOP), F32)
    for tt in range(NTT):
        for j in range(KTOP):
            col = tt * KTOP + j
            op_dve(ctx, lambda E, tt=tt, j=j, col=col: E.tensor_copy(
                bass.AP(val, 2 * j, [[2 * KTOP, 128], [1, 1]]), tokf[:, tt:tt + 1]),
                [(tokf, tt)], [(val, j)]) if False else None
        op_dve(ctx, lambda E, tt=tt: E.tensor_copy(
            bass.AP(val, 0, [[2 * KTOP, 128], [2, KTOP]]),
            tokf[:, tt:tt + 1].to_broadcast([128, KTOP])),
            [(tokf, tt)], [(val, "t")])
        op_dve(ctx, lambda E, tt=tt: E.tensor_copy(
            bass.AP(val, 1, [[2 * KTOP, 128], [2, KTOP]]),
            w_all[:, KTOP * tt:KTOP * (tt + 1)]),
            [(w_all, tt)], [(val, "w")])
        for j in range(KTOP):
            ctx.pr.op("pool", lambda E, tt=tt, j=j: E.indirect_dma_start(
                out=sb["canvas_loc"].ap(),
                out_offset=IndirectOffsetOnAxis(ap=si[:, tt * KTOP + j:tt * KTOP + j + 1], axis=0),
                in_=val[:, 2 * j:2 * j + 2], in_offset=None,
                bounds_check=ctx.breg(E, EXP * CAP - 1), oob_is_err=False),
                reads=[si, (val, "t"), (val, "w")],
                writes=[(sb["canvas_loc"], (tt, j))], inc=16)
    op_pool(ctx, lambda E: E.collective_compute(
        "AllReduce", ALU.add, replica_groups=[list(range(NCORES))],
        ins=[sb["canvas_loc"].ap().opt()], outs=[sb["canvas"].ap().opt()]),
        reads=[sb["canvas_loc"]], writes=[sb["canvas"]], inc=-1)
    ctx.pop()

    ctx.push()
    slot0 = ctx.sbuf((128, 1), I32)
    dma(ctx, "sp", slot0[:], meta["myslot0"].ap(), reads=[meta["myslot0"]], writes=[slot0])
    pbf = ctx.psum_bf
    panq = [(load_w_direct_panels_off(ctx, wb["egu"], 0, H, 2 * I_EXP),
             load_w_direct_panels_off(ctx, wb["edw"], 0, I_EXP, H))]
    xg2 = [ctx.sbuf((128, H), BF16) for _ in range(2)]
    ti2 = [ctx.sbuf((128, 1), I32) for _ in range(2)]
    for el in range(8):
        egup, edwp = panq.pop(0)
        if el < 7:
            panq.append((load_w_direct_panels_off(ctx, wb["egu"], (el + 1) * H, H, 2 * I_EXP),
                         load_w_direct_panels_off(ctx, wb["edw"], (el + 1) * I_EXP, I_EXP, H)))
        tw = [ctx.tile((128, 2), F32) for _ in range(4)]
        wcol = [ctx.tile((128, 1), F32) for _ in range(4)]
        xfm = [ctx.tile((128, 512), BF16) for _ in range(FT)]
        for sg in range(4):
            ti = ti2[sg % 2]
            sidx = ctx.tile((128, 1), I32)
            op_dve(ctx, lambda E, a=sidx, b=slot0, el=el, sg=sg: E.tensor_scalar(
                a[:], b[:], el * CAP + sg * 128, None, ALU.add), [slot0], [sidx])
            ctx.pr.op("pool", lambda E, a=tw[sg], b=sidx: E.indirect_dma_start(
                out=a[:], out_offset=None, in_=sb["canvas"].ap(),
                in_offset=IndirectOffsetOnAxis(ap=b[:], axis=0)),
                reads=[sb["canvas"], sidx], writes=[tw[sg]], inc=16)
            op_dve(ctx, lambda E, a=wcol[sg], b=tw[sg]: E.tensor_copy(a[:], b[:, 1:2]),
                   [tw[sg]], [wcol[sg]])
            op_dve(ctx, lambda E, a=ti, b=tw[sg]: E.tensor_copy(a[:], b[:, 0:1]),
                   [tw[sg]], [ti])
            xg = xg2[sg % 2]
            ctx.pr.op("pool", lambda E, a=xg, b=ti: E.indirect_dma_start(
                out=a[:, 0:H], out_offset=None, in_=sb["xrep"].ap(),
                in_offset=IndirectOffsetOnAxis(ap=b[:], axis=0)),
                reads=[sb["xrep"], ti], writes=[xg], inc=16)
            pbf = ctx.psum_bf2[sg % 2]
            for k in range(FT):
                op_pe(ctx, lambda E, k=k, a=xg, pbf=pbf: E.transpose(
                    pbf[0:128, 128 * k:128 * (k + 1)], a[:, 128 * k:128 * (k + 1)],
                    ctx.consts["ident_bf"][0:128, 0:128]),
                    [xg, ctx.consts["ident_bf"]], [(pbf, k), (pbf, "bank")])
            for k in range(FT):
                op_act(ctx, lambda E, k=k, sg=sg, a=xfm[k], pbf=pbf: E.activation(
                    a[:, 128 * sg:128 * (sg + 1)], pbf[0:128, 128 * k:128 * (k + 1)], AF.Copy),
                    [(pbf, k), (pbf, "bank")], [(xfm[k], sg)])
            ctx.free(sidx)
        actb = [ctx.tile((128, 512), BF16) for _ in range(4)]
        sg_t = ctx.tile((128, 512), F32)
        for mi2 in range(4):
            pg = ctx.psum[mi2 % 2]
            pu = ctx.psum[2 + mi2 % 2]
            for k in range(FT):
                op_pe(ctx, lambda E, k=k, m=mi2, pg=pg, w=egup[k], a=xfm[k]: E.matmul(
                    pg[0:128, 0:512], w[:, 128 * m:128 * (m + 1)], a[:, 0:512],
                    start=(k == 0), stop=(k == FT - 1)),
                    [egup[k], xfm[k]], [pg])
            for k in range(FT):
                op_pe(ctx, lambda E, k=k, m=mi2, pu=pu, w=egup[k], a=xfm[k]: E.matmul(
                    pu[0:128, 0:512], w[:, 512 + 128 * m:512 + 128 * (m + 1)], a[:, 0:512],
                    start=(k == 0), stop=(k == FT - 1)),
                    [egup[k], xfm[k]], [pu])
            op_act(ctx, lambda E, pg=pg, a=sg_t: E.activation(a[:, 0:512], pg[0:128, 0:512], AF.Silu),
                   [pg], [sg_t])
            op_dve(ctx, lambda E, pu=pu, a=actb[mi2], b=sg_t: E.tensor_tensor(
                a[:, 0:512], b[:, 0:512], pu[0:128, 0:512], ALU.mult),
                [pu, sg_t], [actb[mi2]])
        ctx.free(sg_t, *xfm)
        pub = ctx.tile((128, 512), BF16)
        for sl in range(4):
            for nh in range(2):
                pd = ctx.psum[4 + (2 * sl + nh) % 2]
                for k in range(4):
                    op_pe(ctx, lambda E, sl=sl, nh=nh, k=k, pd=pd, a=actb[k], w=edwp[k]: E.matmul(
                        pd[0:128, 0:512], a[:, 128 * sl:128 * (sl + 1)],
                        w[:, 512 * nh:512 * (nh + 1)], start=(k == 0), stop=(k == 3)),
                        [actb[k], edwp[k]], [pd])
                op_dve(ctx, lambda E, pd=pd, a=pub, wc=wcol[sl]: E.tensor_scalar(
                    a[:, 0:512], pd[0:128, 0:512], wc[:, 0:1], None, ALU.mult),
                    [pd, wcol[sl]], [pub])
                dma(ctx, "pool",
                    sb["pub_loc"].ap()[el * CAP + 128 * sl:el * CAP + 128 * (sl + 1),
                                       512 * nh:512 * (nh + 1)],
                    pub[:, 0:512], reads=[pub], writes=[(sb["pub_loc"], (el, sl, nh))])
        ctx.free(pub, *actb, *tw, *wcol, *egup, *edwp)
    op_pool(ctx, lambda E: E.collective_compute(
        "AllGather", ALU.bypass, replica_groups=[list(range(NCORES))],
        ins=[sb["pub_loc"].ap().opt()], outs=[sb["outrep"].ap().opt()]),
        reads=[sb["pub_loc"]], writes=[sb["outrep"]], inc=-1)
    ctx.pop()

    ctx.push()
    TJ = 256
    yj2 = [[ctx.sbuf((128, H), BF16) for _ in range(KTOP)] for _ in range(2)]
    ytk2 = [ctx.sbuf((128, H), F32) for _ in range(2)]
    tb2 = [ctx.sbuf((128, H), F32) for _ in range(2)]
    for ch in range(T // TJ):
        c0 = ch * TJ
        xt = [ctx.tile((128, TJ), F32) for _ in range(FT)]
        for k in range(FT):
            dma(ctx, "sp", xt[k][:, 0:TJ], xin.ap()[128 * k:128 * (k + 1), c0:c0 + TJ],
                reads=[(xin, ("j", ch))], writes=[xt[k]])
        yex = [ctx.tile((128, TJ), F32) for _ in range(FT)]
        for sub in range(TJ // 128):
            tt = ch * (TJ // 128) + sub
            yj = yj2[tt % 2]
            ytok = ytk2[tt % 2]
            tb = tb2[tt % 2]
            for j in range(KTOP):
                op_dve(ctx, lambda E, a=yj[j]: E.memset(a[:], 0.0), [], [yj[j]])
                ctx.pr.op("pool", lambda E, a=yj[j], tt=tt, j=j: E.indirect_dma_start(
                    out=a[:, 0:H], out_offset=None, in_=sb["outrep"].ap(),
                    in_offset=IndirectOffsetOnAxis(
                        ap=si[:, tt * KTOP + j:tt * KTOP + j + 1], axis=0),
                    bounds_check=ctx.breg(E, EXP * CAP - 1),
                    oob_is_err=False),
                    reads=[sb["outrep"], si, yj[j]], writes=[yj[j]], inc=16)
            op_dve(ctx, lambda E, a=ytok, b=yj[0], c=yj[1]: E.tensor_tensor(
                a[:, 0:H], b[:, 0:H], c[:, 0:H], ALU.add),
                [yj[0], yj[1]], [ytok])
            for j in (2, 4, 6):
                op_dve(ctx, lambda E, a=tb, b=yj[j], c=yj[j + 1]: E.tensor_tensor(
                    a[:, 0:H], b[:, 0:H], c[:, 0:H], ALU.add),
                    [yj[j], yj[j + 1]], [tb])
                op_dve(ctx, lambda E, a=ytok, b=tb: E.tensor_add(a[:, 0:H], a[:, 0:H], b[:, 0:H]),
                       [ytok, tb], [ytok])
            for k in range(FT):
                pf = ctx.psum[k % 4]
                op_pe(ctx, lambda E, k=k, pf=pf, a=ytok: E.transpose(
                    pf[0:128, 0:128], a[:, 128 * k:128 * (k + 1)],
                    ctx.consts["ident_f"][0:128, 0:128]),
                    [ytok, ctx.consts["ident_f"]], [pf])
                op_act(ctx, lambda E, k=k, sub=sub, pf=pf, a=yex[k]: E.activation(
                    a[:, 128 * sub:128 * (sub + 1)], pf[0:128, 0:128], AF.Copy),
                    [pf], [(yex[k], sub)])
        for m in range(FT):
            shb = ctx.tile((128, TJ), BF16)
            dma(ctx, "sp", shb[:, 0:TJ], sb["sh_scr"].ap()[128 * m:128 * (m + 1), c0:c0 + TJ],
                reads=[(sb["sh_scr"], (c0 // TC, m))], writes=[shb])
            op_dve(ctx, lambda E, a=yex[m], b=shb: E.tensor_add(
                a[:, 0:TJ], a[:, 0:TJ], b[:, 0:TJ]),
                [(yex[m], 0), (yex[m], 1), shb], [yex[m]])
            op_dve(ctx, lambda E, a=xt[m], b=yex[m]: E.scalar_tensor_tensor(
                a[:, 0:TJ], b[:, 0:TJ], scale, a[:, 0:TJ], ALU.mult, ALU.add),
                [yex[m], xt[m]], [xt[m]])
            dma(ctx, "pool", xout.ap()[128 * m:128 * (m + 1), c0:c0 + TJ], xt[m][:, 0:TJ],
                reads=[xt[m]], writes=[(xout, ("j", ch))], is_out=True)
            ctx.free(shb)
        ctx.free(*xt, *yex)
    ctx.pop()


IS = 2048
I_EXP = 512


def load_w_direct_panels_off(ctx, wb, r0, n_in, n_out):
    panels = []
    for k in range(n_in // 128):
        t = ctx.tile((128, n_out), BF16)
        dma(ctx, "sp", t[0:128, 0:n_out],
            wb.ap()[r0 + 128 * k:r0 + 128 * (k + 1), 0:n_out],
            reads=[wb], writes=[t])
        panels.append(t)
    return panels


def stage_moe_weights_dram(ctx, nc, dr, pfx):
    return {
        "gate_f32": dr["gate_w"],
        "gate": cast_to_bf16_dram(ctx, nc, f"{pfx}_gw", dr["gate_w"], (H, EXP)),
        "sg": cast_to_bf16_dram(ctx, nc, f"{pfx}_sgw", dr["shared_gate_w"], (H, 1)),
        "sgu": cast_to_bf16_dram(ctx, nc, f"{pfx}_sguw", dr["shared_gate_up_w"], (H, 2 * IS)),
        "sdw": cast_to_bf16_dram(ctx, nc, f"{pfx}_sdww", dr["shared_down_w"], (IS, H)),
        "egu": cast_to_bf16_dram(ctx, nc, f"{pfx}_eguw", dr["my_egu"], (8 * H, 2 * I_EXP)),
        "edw": cast_to_bf16_dram(ctx, nc, f"{pfx}_edww", dr["my_edw"], (8 * I_EXP, H)),
    }


def build_cm_program():
    ctx = new_ctx()
    nc = ctx.nc
    dr = decl_const_inputs(nc)
    dr["L1_W_r"] = nc.dram_tensor("W_r", [H, H], F32, kind="ExternalInput")
    dr["L1_W_k"] = nc.dram_tensor("W_k", [H, H], F32, kind="ExternalInput")
    dr["L1_W_v"] = nc.dram_tensor("W_v", [H, H], F32, kind="ExternalInput")
    dr["L1_mu_r"] = nc.dram_tensor("mu_r", [H], F32, kind="ExternalInput")
    dr["L1_mu_k"] = nc.dram_tensor("mu_k", [H], F32, kind="ExternalInput")
    x = nc.dram_tensor("x", [H, T], F32, kind="ExternalInput")
    y = nc.dram_tensor("y", [H, T], F32, kind="ExternalOutput")
    setup_consts(ctx, dr)
    w = stage_cm_weights(ctx, nc, dr, "L1")
    emit_cm(ctx, x, y, w["wTr"], w["wTk"], w["wTv"], w["mur"], w["muk"], 0.5)
    ctx.finish()
    return nc


def build_rwkv_program():
    ctx = new_ctx()
    nc = ctx.nc
    dr = decl_const_inputs(nc)
    for nm, shp in (("Wk", [H, H]), ("Wv", [H, H]), ("Wr", [H, H]), ("Wo", [H, H]),
                    ("time_decay", [H]), ("time_first", [H]), ("bo", [H]),
                    ("time_mix_key", [1, 1, H]), ("time_mix_value", [1, 1, H]),
                    ("time_mix_receptance", [1, 1, H])):
        dr[nm] = nc.dram_tensor(nm, shp, F32, kind="ExternalInput")
    x = nc.dram_tensor("x", [H, T], F32, kind="ExternalInput")
    y = nc.dram_tensor("y", [H, T], F32, kind="ExternalOutput")
    setup_consts(ctx, dr)
    w = stage_rwkv_weights(ctx, nc, dr)
    emit_rwkv(ctx, x, y, w)
    ctx.finish()
    return nc


def build_mamba_program():
    ctx = new_ctx()
    nc = ctx.nc
    dr = decl_const_inputs(nc)
    for nm, shp in (("in_proj_w", [H, 2 * DI]), ("conv_w", [DI, 4]), ("conv_b", [DI]),
                    ("x_proj_w", [DI, 96]), ("dt_proj_w", [DTR, DI]), ("dt_proj_b", [DI]),
                    ("A_log", [DI, NN]), ("D", [DI]), ("out_proj_w", [DI, H]),
                    ("ln_g", [H]), ("ln_b", [H]), ("ff_w1", [H, 4 * H]), ("ff_b1", [4 * H]),
                    ("ff_w2", [4 * H, H]), ("ff_b2", [H])):
        dr[nm] = nc.dram_tensor(nm, shp, F32, kind="ExternalInput")
    x = nc.dram_tensor("x", [H, T], F32, kind="ExternalInput")
    y = nc.dram_tensor("y", [H, T], F32, kind="ExternalOutput")
    setup_consts(ctx, dr)
    emit_mamba(ctx, nc, x, y, dr)
    ctx.finish()
    return nc


def decl_moe_meta(nc):
    return {
        "mask_lt": nc.dram_tensor("mask_lt", [1, NCORES], F32, kind="ExternalInput"),
        "tokid0": nc.dram_tensor("tokid0", [128, 1], F32, kind="ExternalInput"),
        "myslot0": nc.dram_tensor("myslot0", [128, 1], I32, kind="ExternalInput"),
    }


def host_moe_meta(me, Tloc):
    return {
        "mask_lt": np.asarray([[1.0 if c < me else 0.0 for c in range(NCORES)]], np.float32),
        "tokid0": (me * Tloc + np.arange(128, dtype=np.float32)).reshape(128, 1),
        "myslot0": (me * 8 * CAP + np.arange(128, dtype=np.int32)).reshape(128, 1),
    }


def build_moe_program():
    ctx = new_ctx()
    nc = ctx.nc
    dr = decl_const_inputs(nc)
    for nm, shp in (("gate_w", [H, EXP]), ("shared_gate_w", [H, 1]),
                    ("shared_gate_up_w", [H, 2 * IS]), ("shared_down_w", [IS, H]),
                    ("my_egu", [8, H, 2 * I_EXP]), ("my_edw", [8, I_EXP, H])):
        dr[nm] = nc.dram_tensor(nm, shp, F32, kind="ExternalInput")
    meta = decl_moe_meta(nc)
    x = nc.dram_tensor("x", [H, T], F32, kind="ExternalInput")
    y = nc.dram_tensor("y", [H, T], F32, kind="ExternalOutput")
    setup_consts(ctx, dr)
    wb = stage_moe_weights_dram(ctx, nc, dr, "m0")
    sbuffs = moe_shared_buffers(nc, "m0")
    emit_moe(ctx, nc, x, y, dr, wb, sbuffs, meta, 0.5)
    ctx.finish()
    return nc


FUSED = True


def build_fused_program():
    ctx = new_ctx()
    nc = ctx.nc
    dr = decl_const_inputs(nc)
    meta = decl_moe_meta(nc)
    names = [("L0_ffn_gate_w", [H, EXP]), ("L0_ffn_shared_gate_w", [H, 1]),
             ("L0_ffn_shared_gate_up_w", [H, 2 * IS]), ("L0_ffn_shared_down_w", [IS, H]),
             ("my_egu", [8, H, 2 * I_EXP]), ("my_edw", [8, I_EXP, H]),
             ("L0_attn_time_decay", [H]), ("L0_attn_time_first", [H]),
             ("L0_attn_time_mix_key", [1, 1, H]), ("L0_attn_time_mix_value", [1, 1, H]),
             ("L0_attn_time_mix_receptance", [1, 1, H]),
             ("L0_attn_Wk", [H, H]), ("L0_attn_Wv", [H, H]), ("L0_attn_Wr", [H, H]),
             ("L0_attn_Wo", [H, H]), ("L0_attn_bo", [H]),
             ("L1_ffn_W_r", [H, H]), ("L1_ffn_W_k", [H, H]), ("L1_ffn_W_v", [H, H]),
             ("L1_ffn_mu_r", [H]), ("L1_ffn_mu_k", [H]),
             ("L1_attn_in_proj_w", [H, 2 * DI]), ("L1_attn_conv_w", [DI, 4]),
             ("L1_attn_conv_b", [DI]), ("L1_attn_x_proj_w", [DI, 96]),
             ("L1_attn_dt_proj_w", [DTR, DI]), ("L1_attn_dt_proj_b", [DI]),
             ("L1_attn_A_log", [DI, NN]), ("L1_attn_D", [DI]),
             ("L1_attn_out_proj_w", [DI, H]), ("L1_attn_ln_g", [H]), ("L1_attn_ln_b", [H]),
             ("L1_attn_ff_w1", [H, 4 * H]), ("L1_attn_ff_b1", [4 * H]),
             ("L1_attn_ff_w2", [4 * H, H]), ("L1_attn_ff_b2", [H])]
    for nm, shp in names:
        dr[nm] = nc.dram_tensor(nm, shp, F32, kind="ExternalInput")
    x0 = nc.dram_tensor("x", [H, T], F32, kind="ExternalInput")
    y = nc.dram_tensor("y", [H, T], F32, kind="ExternalOutput")
    xr = [nc.dram_tensor(f"xr{i}", [H, T], F32) for i in range(5)]
    setup_consts(ctx, dr)

    moe_dr = {"gate_w": dr["L0_ffn_gate_w"], "shared_gate_w": dr["L0_ffn_shared_gate_w"],
              "shared_gate_up_w": dr["L0_ffn_shared_gate_up_w"],
              "shared_down_w": dr["L0_ffn_shared_down_w"],
              "my_egu": dr["my_egu"], "my_edw": dr["my_edw"]}
    wb = stage_moe_weights_dram(ctx, nc, moe_dr, "m0")
    sbuffs = moe_shared_buffers(nc, "m0")
    ctx.push()
    emit_moe(ctx, nc, x0, xr[0], moe_dr, wb, sbuffs, meta, 0.5)
    ctx.pop()
    ctx.push()
    rdr = {"Wk": dr["L0_attn_Wk"], "Wv": dr["L0_attn_Wv"], "Wr": dr["L0_attn_Wr"],
           "Wo": dr["L0_attn_Wo"], "time_decay": dr["L0_attn_time_decay"],
           "time_first": dr["L0_attn_time_first"], "bo": dr["L0_attn_bo"],
           "time_mix_key": dr["L0_attn_time_mix_key"],
           "time_mix_value": dr["L0_attn_time_mix_value"],
           "time_mix_receptance": dr["L0_attn_time_mix_receptance"]}
    w2 = stage_rwkv_weights(ctx, nc, rdr)
    emit_rwkv(ctx, xr[0], xr[1], w2)
    ctx.pop()
    ctx.push()
    emit_moe(ctx, nc, xr[1], xr[2], moe_dr, wb, sbuffs, meta, 0.5)
    ctx.pop()
    cmb = {}
    for nm in ("W_r", "W_k", "W_v"):
        cmb[nm] = cast_to_bf16_dram(ctx, nc, f"cm_{nm}", dr[f"L1_ffn_{nm}"], (H, H))
    ctx.push()
    w4 = {"wTr": load_wT_panels(ctx, cmb["W_r"]), "wTk": load_wT_panels(ctx, cmb["W_k"]),
          "wTv": load_wT_panels(ctx, cmb["W_v"]),
          "mur": load_percol(ctx, dr["L1_ffn_mu_r"]), "muk": load_percol(ctx, dr["L1_ffn_mu_k"])}
    emit_cm(ctx, xr[2], xr[3], w4["wTr"], w4["wTk"], w4["wTv"], w4["mur"], w4["muk"], 0.5)
    ctx.pop()
    ctx.push()
    mdr = {k[len("L1_attn_"):]: v for k, v in dr.items() if k.startswith("L1_attn_")}
    emit_mamba(ctx, nc, xr[3], xr[4], mdr)
    ctx.pop()
    ctx.push()
    w6 = {"wTr": load_wT_panels(ctx, cmb["W_r"]), "wTk": load_wT_panels(ctx, cmb["W_k"]),
          "wTv": load_wT_panels(ctx, cmb["W_v"]),
          "mur": load_percol(ctx, dr["L1_ffn_mu_r"]), "muk": load_percol(ctx, dr["L1_ffn_mu_k"])}
    emit_cm(ctx, xr[4], y, w6["wTr"], w6["wTk"], w6["wTv"], w6["mur"], w6["muk"], 0.5)
    ctx.pop()
    ctx.finish()
    return nc


def _common_in_map(c, inputs):
    m = dict(host_consts())
    m.update(host_moe_meta(c, T))
    m["my_egu"] = inputs["L0_ffn_expert_gate_up_w"][c * 8:(c + 1) * 8]
    m["my_edw"] = inputs["L0_ffn_expert_down_w"][c * 8:(c + 1) * 8]
    return m


def kernel(**inputs):
    x_full = inputs["L0_attn_x"]
    if FUSED:
        nc = build_fused_program()
        in_maps = []
        for c in range(NCORES):
            m = _common_in_map(c, inputs)
            for k, v in inputs.items():
                if k in ("L0_ffn_expert_gate_up_w", "L0_ffn_expert_down_w", "L0_attn_x"):
                    continue
                m[k] = v
            m["x"] = np.ascontiguousarray(x_full[c].T)
            in_maps.append(m)
        res = run_bass_kernel_spmd(nc, in_maps, core_ids=list(range(NCORES)))
        return np.stack([res.results[c]["y"].T for c in range(NCORES)]).astype(np.float32)

    xs = [np.ascontiguousarray(x_full[c].T) for c in range(NCORES)]
    consts = host_consts()

    def run(nc, extra):
        nonlocal xs
        in_maps = []
        for c in range(NCORES):
            m = dict(consts)
            m.update(extra(c))
            m["x"] = xs[c]
            in_maps.append(m)
        res = run_bass_kernel_spmd(nc, in_maps, core_ids=list(range(NCORES)))
        xs = [res.results[c]["y"] for c in range(NCORES)]

    moe_extra = lambda c: {
        "gate_w": inputs["L0_ffn_gate_w"], "shared_gate_w": inputs["L0_ffn_shared_gate_w"],
        "shared_gate_up_w": inputs["L0_ffn_shared_gate_up_w"],
        "shared_down_w": inputs["L0_ffn_shared_down_w"],
        "my_egu": inputs["L0_ffn_expert_gate_up_w"][c * 8:(c + 1) * 8],
        "my_edw": inputs["L0_ffn_expert_down_w"][c * 8:(c + 1) * 8],
        **host_moe_meta(c, T)}
    rwkv_extra = lambda c: {
        "Wk": inputs["L0_attn_Wk"], "Wv": inputs["L0_attn_Wv"], "Wr": inputs["L0_attn_Wr"],
        "Wo": inputs["L0_attn_Wo"], "time_decay": inputs["L0_attn_time_decay"],
        "time_first": inputs["L0_attn_time_first"], "bo": inputs["L0_attn_bo"],
        "time_mix_key": inputs["L0_attn_time_mix_key"],
        "time_mix_value": inputs["L0_attn_time_mix_value"],
        "time_mix_receptance": inputs["L0_attn_time_mix_receptance"]}
    cm_extra = lambda c: {
        "W_r": inputs["L1_ffn_W_r"], "W_k": inputs["L1_ffn_W_k"], "W_v": inputs["L1_ffn_W_v"],
        "mu_r": inputs["L1_ffn_mu_r"], "mu_k": inputs["L1_ffn_mu_k"]}
    mamba_extra = lambda c: {k[len("L1_attn_"):]: v for k, v in inputs.items()
                             if k.startswith("L1_attn_") and k != "L0_attn_x" and
                             not k.endswith("attn_x")}

    p_moe = build_moe_program()
    run(p_moe, moe_extra)
    run(build_rwkv_program(), rwkv_extra)
    run(p_moe, moe_extra)
    p_cm = build_cm_program()
    run(p_cm, cm_extra)
    run(build_mamba_program(), mamba_extra)
    run(p_cm, cm_extra)
    return np.stack([x.T for x in xs]).astype(np.float32)
